# version 05
# baseline (speedup 1.0000x reference)
import numpy as np
from contextlib import ExitStack

import concourse.bass as bass
import concourse.mybir as mybir
from concourse.bass_utils import run_bass_kernel_spmd

F32 = mybir.dt.float32
BF16 = mybir.dt.bfloat16
AF = mybir.ActivationFunctionType
ALU = mybir.AluOpType
AX = mybir.AxisListType

B, T, CD, H, D, CS = 8, 2048, 1024, 16, 64, 128
NCH = T // CS
PSL = T // 8
TOK = B * PSL
E = 8
QH = 4
NQ = H // QH
EPS = 1e-5

ENGINES = ("tensor", "vector", "scalar", "gpsimd", "sync")
DMA_SEMS = ("gs", "gw", "gw0", "gw1", "gm0", "gm1", "gg", "gq0", "gq1", "sy0", "sy1", "syt", "syg")


class Prog:
    def __init__(self):
        self.st = {e: [] for e in ENGINES}
        self.semnames = []
        self.cnt = {}

    def sem(self, name):
        self.semnames.append(name)
        self.cnt[name] = 0

    def op(self, eng, fn):
        self.st[eng].append(fn)

    def wait(self, eng, tok):
        if tok is None:
            return
        sem, val = tok
        if sem in DMA_SEMS:
            val = self.cnt[sem]
        if val <= 0:
            return
        self.op(eng, lambda e, S, sem=sem, val=val: e.wait_ge(S[sem], val))

    def waits(self, eng, toks):
        best = {}
        for t in toks:
            if t is None:
                continue
            s, v = t
            if v > best.get(s, -1):
                best[s] = v
        for s, v in best.items():
            self.wait(eng, (s, v))

    def do(self, eng, fn, sem=None, k=1):
        if sem is None:
            self.op(eng, fn)
            return None
        self.cnt[sem] += k
        v = self.cnt[sem]

        def wrapped(e, S, fn=fn, sem=sem, k=k):
            fn(e, S).then_inc(S[sem], k)

        self.op(eng, wrapped)
        return (sem, v)

    def bar(self, eng, sem):
        t = self.mark(eng, sem)
        self.wait(eng, t)
        return t

    def mark(self, eng, sem, k=1):
        self.cnt[sem] += k
        v = self.cnt[sem]
        fn = self.st[eng].pop()

        def wrapped(e, S, fn=fn, sem=sem, k=k):
            fn(e, S).then_inc(S[sem], k)

        self.st[eng].append(wrapped)
        return (sem, v)


def emit(nc, p):
    with ExitStack() as ctx:
        S = {}
        for name in p.semnames:
            S[name] = ctx.enter_context(nc.semaphore(name))
        block = ctx.enter_context(nc.Block())
        for eng in ENGINES:
            stream = p.st[eng]

            def body(handle, stream=stream, S=S):
                for fn in stream:
                    fn(handle, S)

            getattr(block, eng)(body)


def build_nc(taps=(), ffn=True, no_cc=False, stop_after=None):
    taps = set(taps)
    nc = bass.Bass("TRN2", target_bir_lowering=False)
    p = Prog()
    for s in ("td", "vd", "sd", "gs", "gw", "gw0", "gw1", "gm0", "gm1", "gg", "gq0", "gq1", "sy0", "sy1", "syt", "syg", "cc"):
        p.sem(s)

    dti = {}

    def din(name, shape, dtype=F32):
        dti[name] = nc.dram_tensor(name, list(shape), dtype, kind="ExternalInput")
        return dti[name]

    def dout(name, shape, dtype=F32):
        dti[name] = nc.dram_tensor(name, list(shape), dtype, kind="ExternalOutput")
        return dti[name]

    din("xb", (T, CD))
    din("xs", (TOK, CD))
    for w in ("wq", "wk", "wv", "wproj", "wout"):
        din(w, (CD, CD))
    din("win", (CD, 3 * CD))
    din("wr", (CD, E))
    din("w1", (E, CD, CD))
    din("w2", (E, CD, CD))
    din("binrow", (1, 3 * CD))
    din("bout", (1, CD))
    din("ones1", (1, 512))
    din("brout", (1, E))
    din("b1c", (128, E * 8))
    din("b2r", (E, CD))
    din("g1", (1, CD)); din("h1", (1, CD)); din("g2", (1, CD)); din("h2", (1, CD))
    din("araw", (1, H)); din("braw", (1, H))
    din("ident", (128, 128))
    din("stril", (128, 128))
    din("striu", (128, 128))
    din("blk8", (128, 128))
    din("iota8", (1, E))

    dout("out", (TOK, CD))
    tap_set = {}

    def tapout(name, shape, dtype=BF16):
        if name in taps:
            tap_set[name] = dout("tap_" + name, shape, dtype)
            return tap_set[name]
        return None

    ab_dram = nc.dram_tensor("ab_dram", [1, 2 * H], F32)
    oh_dram = nc.dram_tensor("oh_dram", [E, TOK], BF16)
    cc_in = nc.dram_tensor("cc_in", [T, CD], BF16)
    cc_out = nc.dram_tensor("cc_out", [B * T, CD], BF16)

    sb = {}

    def salloc(name, shape, dtype):
        sb[name] = nc.alloc_sbuf_tensor("sb_" + name, list(shape), dtype)
        return sb[name]

    salloc("ident", (128, 128), BF16)
    salloc("stril", (128, 128), F32)
    salloc("striu", (128, 128), F32)
    salloc("blk8", (128, 128), BF16)
    salloc("iota", (128, E), F32)
    salloc("ab_raw", (1, 2 * H), F32)
    salloc("ab_sig", (1, 2 * H), F32)
    salloc("abc", (128, 2 * H), F32)
    salloc("ones", (1, 512), BF16)
    salloc("WRB", (128, 64), BF16)
    salloc("OHF", (8, 2048), BF16)
    salloc("b1c", (128, E * 8), F32)
    salloc("b2r", (E, CD), BF16)
    salloc("brows", (1, 4 * CD + E), BF16)
    salloc("gbc", (128, CD), BF16)
    salloc("hbc", (128, CD), BF16)

    salloc("RA", (128, 16384), BF16)
    salloc("RB", (128, 16384), BF16)
    salloc("RC", (128, 8192), BF16)
    salloc("KFQ", (128, 8192), BF16)
    salloc("QFQ", (128, 8192), BF16)
    salloc("RE", (128, 15360), BF16)
    salloc("RF", (128, 8192), BF16)
    salloc("RG", (128, 4096), BF16)
    salloc("MS", (128, 1024), F32)
    salloc("GAB", (64, 512), F32)
    salloc("STK", (128, 384), F32)
    salloc("GBF", (64, 256), BF16)

    ps = [nc.alloc_psum_tensor(f"PP{i}", [128, 1024], F32) for i in range(4)]

    def psA(i, n=512):
        return ps[i].ap()[:, 0:n]

    def psbf(i, off, n):
        return ps[i].ap().bitcast(BF16)[:, off:off + n]

    def bfv(region, off, n, inner=None):
        ap = sb[region][:, off:off + n]
        if inner is not None:
            ap = ap.rearrange("p (a b) -> p a b", b=inner)
        return ap

    MSL = sb["MS"].ap()[:, 0:512].rearrange("p (a b) -> p a b", b=128)
    MSU = sb["MS"].ap()[:, 512:1024].rearrange("p (a b) -> p a b", b=128)
    GA = sb["GAB"].ap()[0:64, 0:256]
    GB = sb["GAB"].ap()[0:64, 256:512]
    KNS = sb["RG"].ap().bitcast(F32)[:, 0:1024]
    STT = [sb["STK"].ap()[:, 0:16], sb["STK"].ap()[:, 16:32]]
    KST = sb["STK"].ap()[:, 32:40]
    QTS = bfv("RG", 2048, 512)

    def PWT(lvl):
        return bfv("RE", lvl * 512, 512, 128)

    def PWU(par):
        return bfv("RE", (7 + par) * 512, 512, 128)

    def RBUF(par):
        return bfv("RE", (9 + par) * 512, 512, 128)

    ATB = bfv("RE", 11 * 512, 512, 128)
    NBUF = bfv("RE", 12 * 512, 512, 128)
    UFB = sb["RE"].ap()[0:64, 13 * 512:13 * 512 + 512].rearrange("p (a b) -> p a b", b=128)
    WBUF = bfv("RE", 14 * 512, 256, 64)
    GBF = sb["RE"].ap()[0:64, 14 * 512 + 256:14 * 512 + 512].rearrange("p (a b) -> p a b", b=64)

    XSTG = sb["RF"].ap().bitcast(F32)[:, 2048:4096].rearrange("p (a b) -> p a b", b=1024)
    XNT = bfv("RF", 0, 2048, 1024)
    WKB = bfv("RF", 2048, 2048, 256)
    WQB = bfv("RF", 4096, 2048, 256)
    WVBc = bfv("RF", 6144, 2048, 256)

    def setup():
        def g(dst, src):
            return p.do("gpsimd", lambda e, S, d=dst, s=src: e.dma_start(out=d, in_=s), "gs", 16)

        def bc(src):
            return src.partition_broadcast(128).rearrange("p o e -> p (o e)")

        g(sb["ident"][:, :], dti["ident"].ap())
        g(sb["stril"][:, :], dti["stril"].ap())
        g(sb["striu"][:, :], dti["striu"].ap())
        g(sb["blk8"][:, :], dti["blk8"].ap())
        g(sb["iota"][:, :], bc(dti["iota8"].ap()))
        g(sb["ones"][0:1, :], dti["ones1"].ap())
        g(sb["b1c"][:, :], dti["b1c"].ap())
        g(sb["b2r"][:, :], dti["b2r"].ap())
        g(sb["brows"][0:1, 0:3 * CD], dti["binrow"].ap())
        g(sb["brows"][0:1, 3 * CD:4 * CD], dti["bout"].ap())
        g(sb["brows"][0:1, 4 * CD:4 * CD + E], dti["brout"].ap())
        g(sb["WRB"].ap().rearrange("p (a b) -> p a b", b=8),
          dti["wr"].ap().rearrange("(a p) o -> p a o", p=128))
        g(sb["gbc"][:, :], bc(dti["g1"].ap()))
        g(sb["hbc"][:, :], bc(dti["h1"].ap()))
        g(sb["ab_raw"][0:1, 0:H], dti["araw"].ap())
        t_last = g(sb["ab_raw"][0:1, H:2 * H], dti["braw"].ap())
        p.wait("scalar", t_last)
        t_exp = p.do("scalar", lambda e, S: e.activation(sb["ab_sig"][0:1, :], sb["ab_raw"][0:1, :], AF.Exp, scale=-1.0), "sd")
        p.wait("vector", t_exp)
        p.do("vector", lambda e, S: e.tensor_scalar_add(sb["ab_sig"][0:1, :], sb["ab_sig"][0:1, :], 1.0))
        p.bar("vector", "vd")
        p.do("vector", lambda e, S: e.reciprocal(sb["ab_sig"][0:1, :], sb["ab_sig"][0:1, :]))
        p.bar("vector", "vd")
        p.do("vector", lambda e, S: e.tensor_scalar_mul(sb["ab_sig"][0:1, :], sb["ab_sig"][0:1, :], 0.5))
        t_sig = p.mark("vector", "vd")
        p.wait("gpsimd", t_sig)
        t_ab1 = p.do("gpsimd", lambda e, S: e.dma_start(out=ab_dram.ap(), in_=sb["ab_sig"][0:1, :]), "gs", 16)
        p.wait("gpsimd", t_ab1)
        t_abc = p.do("gpsimd", lambda e, S: e.dma_start(out=sb["abc"][:, :], in_=bc(ab_dram.ap())), "gs", 16)
        return t_abc

    ln_state = {"prev_apply": None}

    def ln_tile(xin, out, wait_toks, par=0, gain=None, bias=None, out2=None):
        st = STT[par]
        p.waits("vector", wait_toks + [ln_state["prev_apply"]])
        p.do("vector", lambda e, S, st=st, xin=xin: e.bn_stats(st[:, 4:10], xin[:, 0:512]))
        p.bar("vector", "vd")
        p.do("vector", lambda e, S, st=st, xin=xin: e.bn_stats(st[:, 10:16], xin[:, 512:1024]))
        p.bar("vector", "vd")
        p.do("vector", lambda e, S, st=st: e.bn_aggr(st[:, 0:2], st[:, 4:16]))
        p.bar("vector", "vd")
        p.do("vector", lambda e, S, st=st: e.tensor_scalar_add(st[:, 2:3], st[:, 1:2], float(EPS)))
        t_ag = p.mark("vector", "vd")
        p.wait("scalar", t_ag)
        p.do("scalar", lambda e, S, st=st: e.activation(st[:, 2:3], st[:, 2:3], AF.Ln))
        p.bar("scalar", "sd")
        p.do("scalar", lambda e, S, st=st: e.activation(st[:, 2:3], st[:, 2:3], AF.Exp, scale=-0.5))
        t_r = p.mark("scalar", "sd")
        p.wait("vector", t_r)
        p.do("vector", lambda e, S, st=st: e.tensor_scalar(st[:, 3:4], st[:, 0:1], st[:, 2:3], -1.0, ALU.mult, ALU.mult))
        t_b = p.mark("vector", "vd")
        p.wait("scalar", t_b)
        p.do("scalar", lambda e, S, st=st, out=out, xin=xin: e.activation(out, xin, AF.Identity, bias=st[:, 3:4], scale=st[:, 2:3]))
        tok = p.mark("scalar", "sd")
        ln_state["prev_apply"] = tok
        if gain is not None:
            p.wait("vector", tok)
            p.do("vector", lambda e, S, out=out, gain=gain: e.tensor_mul(out, out, gain))
            p.bar("vector", "vd")
            p.do("vector", lambda e, S, out=out, bias=bias, out2=out2: e.tensor_add(out2 if out2 is not None else out, out, bias))
            tok = p.mark("vector", "vd")
        return tok

    def titans(t_abc):
        xnf = bfv("RA", 0, 16384, 2048)
        yT = bfv("RB", 0, 16384, 1024)
        kv = bfv("RC", 0, 8192, 512).rearrange("p a (h d) -> p a h d", d=128)
        kfq = sb["KFQ"].ap()[0:64, :].rearrange("p (a b) -> p a b", b=2048)
        qfq = sb["QFQ"].ap()[0:64, :].rearrange("p (a b) -> p a b", b=2048)

        prev_slot = [None, None]
        t_ev = None
        for i in range(16):
            par = i % 2
            p.waits("sync", [prev_slot[par]])
            t_load = p.do("sync", lambda e, S, i=i, par=par: e.dma_start(
                out=XSTG[:, par, :], in_=dti["xb"].ap()[128 * i:128 * (i + 1), :]),
                "sy0" if par == 0 else "sy1", 16)
            tln = ln_tile(XSTG[:, par, :], XNT[:, par, :], [t_load, prev_slot[par]], par=par)
            p.waits("tensor", [tln, t_ev])
            for c8 in range(8):
                p.do("tensor", lambda e, S, par=par, c8=c8: e.transpose(
                    psbf(3, 128 * c8, 128), XNT[:, par, 128 * c8:128 * (c8 + 1)], sb["ident"][:, :]))
            t_tr = p.mark("tensor", "td")
            p.wait("vector", t_tr)
            p.do("vector", lambda e, S, i=i: e.tensor_copy(
                xnf[:, :, 128 * i:128 * (i + 1)], psbf(3, 0, 1024).rearrange("p (a b) -> p a b", b=128)))
            t_ev = p.mark("vector", "vd")
            prev_slot[par] = t_ev
        t_xnf = t_ev
        tp = tapout("xnf", (128, 16384), BF16)
        if tp is not None:
            p.wait("sync", t_xnf)
            p.do("sync", lambda e, S, tp=tp: e.dma_start(out=tp.ap(), in_=sb["RA"][:, :]), "syt", 16)
        if stop_after == "xn":
            return t_xnf

        t_pass_done = None
        t_pass_sd = None
        t_yev = None
        for q in range(NQ):
            h0 = q * QH
            p.waits("vector", [t_abc, t_pass_done])
            for hh in range(QH):
                h = h0 + hh
                p.do("vector", lambda e, S, hh=hh, h=h: e.tensor_scalar(
                    MSL[:, hh, :], sb["stril"][:, :], sb["abc"][:, h:h + 1], None, ALU.mult))
                p.do("vector", lambda e, S, hh=hh, h=h: e.tensor_scalar(
                    MSU[:, hh, :], sb["striu"][:, :], sb["abc"][:, h:h + 1], None, ALU.mult))
            t_masks = p.mark("vector", "vd")

            p.waits("gpsimd", [t_xnf, t_pass_done])
            t_wk = p.do("gpsimd", lambda e, S, h0=h0: e.dma_start(
                out=WKB, in_=dti["wk"].ap()[:, 64 * h0:64 * h0 + 256].rearrange("(a p) o -> p a o", p=128)), "gw", 16)
            t_wv = p.do("gpsimd", lambda e, S, h0=h0: e.dma_start(
                out=WVBc, in_=dti["wv"].ap()[:, 64 * h0:64 * h0 + 256].rearrange("(a p) o -> p a o", p=128)), "gw", 16)
            t_wq = p.do("gpsimd", lambda e, S, h0=h0: e.dma_start(
                out=WQB, in_=dti["wq"].ap()[:, 64 * h0:64 * h0 + 256].rearrange("(a p) o -> p a o", p=128)), "gw", 16)

            kv_ready = []
            t_ktr = None
            t_qtr = None
            prev_q = None
            for i in range(16):
                p.waits("tensor", [t_wk, t_xnf] + ([kv_ready[i - 1][0]] if i else [t_pass_done, t_pass_sd]))
                for c8 in range(8):
                    p.do("tensor", lambda e, S, i=i, c8=c8: e.matmul(
                        psA(0, 256), xnf[:, c8, 128 * i:128 * (i + 1)], WKB[:, c8, :],
                        start=(c8 == 0), stop=(c8 == 7)))
                t_kt = p.mark("tensor", "td")
                p.waits("tensor", [t_wv] + ([kv_ready[i - 1][1]] if i else [t_pass_done, t_pass_sd]))
                for c8 in range(8):
                    p.do("tensor", lambda e, S, i=i, c8=c8: e.matmul(
                        psA(1, 256), xnf[:, c8, 128 * i:128 * (i + 1)], WVBc[:, c8, :],
                        start=(c8 == 0), stop=(c8 == 7)))
                t_vt = p.mark("tensor", "td")
                p.waits("tensor", [t_wq, prev_q] + ([] if i else [t_pass_done, t_pass_sd]))
                for c8 in range(8):
                    p.do("tensor", lambda e, S, i=i, c8=c8: e.matmul(
                        psA(2, 256), xnf[:, c8, 128 * i:128 * (i + 1)], WQB[:, c8, :],
                        start=(c8 == 0), stop=(c8 == 7)))
                t_qt = p.mark("tensor", "td")
                p.wait("vector", t_kt)
                p.do("vector", lambda e, S: e.tensor_copy(KNS[:, 0:256], psA(0, 256)))
                t_cp = p.mark("vector", "vd")
                p.wait("scalar", t_cp)
                p.do("scalar", lambda e, S: e.activation(KNS[:, 512:768], KNS[:, 0:256], AF.Square))
                t_sq = p.mark("scalar", "sd")
                p.wait("vector", t_sq)
                p.do("vector", lambda e, S: e.reduce_sum(
                    KST[:, 0:4], KNS[:, 512:768].rearrange("p (h d) -> p h d", d=64), axis=AX.X))
                p.bar("vector", "vd")
                p.do("vector", lambda e, S: e.tensor_scalar_max(KST[:, 0:4], KST[:, 0:4], 1e-24))
                t_ss = p.mark("vector", "vd")
                p.wait("scalar", t_ss)
                p.do("scalar", lambda e, S: e.activation(KST[:, 0:4], KST[:, 0:4], AF.Ln))
                p.bar("scalar", "sd")
                p.do("scalar", lambda e, S: e.activation(KST[:, 0:4], KST[:, 0:4], AF.Exp, scale=-0.5))
                t_inv = p.mark("scalar", "sd")
                p.wait("vector", t_inv)
                p.do("vector", lambda e, S, i=i: e.tensor_tensor(
                    kv[:, i, :, 0:64], KNS[:, 0:256].rearrange("p (h d) -> p h d", d=64),
                    KST[:, 0:4].unsqueeze(2).broadcast_to([128, 4, 64]), ALU.mult))
                t_k = p.mark("vector", "vd")
                p.waits("vector", [t_vt, t_abc])
                p.do("vector", lambda e, S, i=i, h0=h0: e.tensor_tensor(
                    kv[:, i, :, 64:128], psA(1, 256).rearrange("p (h d) -> p h d", d=64),
                    sb["abc"].ap()[:, H + h0:H + h0 + 4].unsqueeze(2).broadcast_to([128, 4, 64]), ALU.mult))
                t_v = p.mark("vector", "vd")
                kv_ready.append((t_k, t_v))
                p.waits("vector", [t_qt, t_qtr])
                p.do("vector", lambda e, S: e.tensor_copy(QTS[:, 0:256], psA(2, 256)))
                t_qs = p.mark("vector", "vd")
                prev_q = t_qs
                p.waits("tensor", [t_qs, t_k, t_ktr])
                for hh in range(QH):
                    p.do("tensor", lambda e, S, hh=hh: e.transpose(
                        ps[3].ap().bitcast(BF16)[0:64, 128 * hh:128 * (hh + 1)],
                        QTS[:, 64 * hh:64 * (hh + 1)], sb["ident"][:, :]))
                for hh in range(QH):
                    p.do("tensor", lambda e, S, i=i, hh=hh: e.transpose(
                        ps[3].ap().bitcast(BF16)[0:64, 512 + 128 * hh:512 + 128 * (hh + 1)],
                        kv[:, i, hh, 0:64], sb["ident"][:, :]))
                t_tr = p.mark("tensor", "td")
                p.wait("vector", t_tr)
                p.do("vector", lambda e, S, i=i: e.tensor_copy(
                    qfq[:, :, 128 * i:128 * (i + 1)],
                    ps[3].ap().bitcast(BF16)[0:64, 0:512].rearrange("p (a b) -> p a b", b=128)))
                p.bar("vector", "vd")
                p.do("vector", lambda e, S, i=i: e.tensor_copy(
                    kfq[:, :, 128 * i:128 * (i + 1)],
                    ps[3].ap().bitcast(BF16)[0:64, 512:1024].rearrange("p (a b) -> p a b", b=128)))
                t_qtr = p.mark("vector", "vd")
                t_ktr = t_qtr
            t_proj = t_qtr
            if stop_after == "proj" and q == 0:
                return t_proj

            SETB = 7680

            def sPWT(s, lvl):
                return bfv("RE", SETB * s + lvl * 512, 512, 128)

            def sPWU(s, pr2):
                return bfv("RE", SETB * s + (7 + pr2) * 512, 512, 128)

            def sRBUF(s, pr2):
                return bfv("RE", SETB * s + (9 + pr2) * 512, 512, 128)

            def sATB(s):
                return bfv("RE", SETB * s + 11 * 512, 512, 128)

            def sNBUF(s):
                return bfv("RE", SETB * s + 12 * 512, 512, 128)

            def sUFB(s):
                return sb["RE"].ap()[0:64, SETB * s + 13 * 512:SETB * s + 13 * 512 + 512].rearrange("p (a b) -> p a b", b=128)

            def sWBUF(s):
                return bfv("RE", SETB * s + 13 * 512 + 512, 256, 64)

            def psH(i, s, off=0, n=512):
                return ps[i].ap()[:, 512 * s + off:512 * s + off + n]

            GBFs = sb["GBF"].ap().rearrange("p (a b) -> p a b", b=64)

            free_tok = {}
            for i4 in range(4):
                for s2 in range(2):
                    free_tok[(i4, s2)] = t_proj
            t_gbf = None
            t_gf = None
            t_uf0 = None
            for cp in range(NCH // 2):
                cs = (2 * cp, 2 * cp + 1)
                t_s = [None, None]; t_kq = [None, None]; t_n = [None, None]
                t_at = [None, None]; t_rfin = [None, None]; t_uf = [None, None]
                rfin = [None, None]
                for s in range(2):
                    c = cs[s]
                    p.waits("tensor", [t_proj, free_tok[(0, s)]])
                    for hh in range(QH):
                        p.do("tensor", lambda e, S, hh=hh, c=c, s=s: e.matmul(
                            psH(0, s, 128 * hh, 128),
                            kfq[0:64, hh, 128 * c:128 * (c + 1)],
                            kfq[0:64, hh, 128 * c:128 * (c + 1)], start=True, stop=True))
                    t_s[s] = p.mark("tensor", "td")
                    p.waits("tensor", [free_tok[(1, s)]])
                    for hh in range(QH):
                        p.do("tensor", lambda e, S, hh=hh, c=c, s=s: e.matmul(
                            psH(1, s, 128 * hh, 128),
                            kfq[0:64, hh, 128 * c:128 * (c + 1)],
                            qfq[0:64, hh, 128 * c:128 * (c + 1)], start=True, stop=True))
                    t_kq[s] = p.mark("tensor", "td")
                for s in range(2):
                    p.waits("vector", [t_s[s], t_masks])
                    p.do("vector", lambda e, S, s=s: e.tensor_tensor(
                        sNBUF(s).rearrange("p a b -> p (a b)"), psH(0, s), MSL.rearrange("p a b -> p (a b)"), ALU.mult))
                    p.bar("vector", "vd")
                    p.do("vector", lambda e, S, s=s: e.tensor_tensor(
                        sPWT(s, 0).rearrange("p a b -> p (a b)"), psH(0, s), MSU.rearrange("p a b -> p (a b)"), ALU.mult))
                    t_n[s] = p.mark("vector", "vd")
                    free_tok[(0, s)] = t_n[s]
                    p.wait("vector", t_kq[s])
                    p.do("vector", lambda e, S, s=s: e.tensor_tensor(
                        sATB(s), psH(1, s).rearrange("p (a b) -> p a b", b=128),
                        sb["striu"].ap().unsqueeze(1).broadcast_to([128, 4, 128]), ALU.mult))
                    t_at[s] = p.mark("vector", "vd")
                    free_tok[(1, s)] = t_at[s]

                cur_u = [sNBUF(0), sNBUF(1)]
                cur_t = [sPWT(0, 0), sPWT(1, 0)]
                t_cu = [t_n[0], t_n[1]]
                t_ctl = [t_n[0], t_n[1]]
                for lvl in range(1, 7):
                    for s in range(2):
                        t_mu_s = None
                        if lvl < 6:
                            p.waits("tensor", [t_cu[s], t_ctl[s], free_tok[(2, s)]])
                            for hh in range(QH):
                                p.do("tensor", lambda e, S, hh=hh, s=s, ct=cur_t[s], cu=cur_u[s]: e.matmul(
                                    psH(2, s, 128 * hh, 128), ct[:, hh, :], cu[:, hh, :],
                                    start=True, stop=True))
                            t_mu_s = p.mark("tensor", "td")
                        p.waits("tensor", [t_cu[s], t_ctl[s], free_tok[(3, s)]])
                        for hh in range(QH):
                            p.do("tensor", lambda e, S, hh=hh, s=s, ct=cur_t[s], cu=cur_u[s]: e.matmul(
                                psH(3, s, 128 * hh, 128), cu[:, hh, :], ct[:, hh, :],
                                start=True, stop=True))
                        t_mt_s = p.mark("tensor", "td")
                        if lvl < 6:
                            nxt_u = sPWU(s, lvl % 2)
                            p.wait("scalar", t_mu_s)
                            p.do("scalar", lambda e, S, s=s, nxt_u=nxt_u: e.copy(
                                nxt_u.rearrange("p a b -> p (a b)"), psH(2, s)))
                            t_cu[s] = p.mark("scalar", "sd")
                            free_tok[(2, s)] = t_cu[s]
                            cur_u[s] = nxt_u
                        p.wait("scalar", t_mt_s)
                        p.do("scalar", lambda e, S, s=s, lvl=lvl: e.copy(
                            sPWT(s, lvl).rearrange("p a b -> p (a b)"), psH(3, s)))
                        t_ctl[s] = p.mark("scalar", "sd")
                        free_tok[(3, s)] = t_ctl[s]
                        cur_t[s] = sPWT(s, lvl)

                r_prev = [kv[:, cs[0], :, :], kv[:, cs[1], :, :]]
                r_tok = [kv_ready[cs[0]][1], kv_ready[cs[1]][1]]
                for j, lvl in enumerate([6, 5, 4, 3, 2, 1, 0]):
                    pr = j % 2
                    for s in range(2):
                        tgt = sRBUF(s, j % 2)
                        p.waits("tensor", [r_tok[s], t_ctl[s], free_tok[(pr, s)]])
                        for hh in range(QH):
                            p.do("tensor", lambda e, S, hh=hh, s=s, lvl=lvl, rp=r_prev[s], pr=pr: e.matmul(
                                psH(pr, s, 128 * hh, 128), sPWT(s, lvl)[:, hh, :], rp[:, hh, :],
                                start=True, stop=True))
                        t_mm = p.mark("tensor", "td")
                        p.wait("vector", t_mm)
                        if lvl > 0:
                            p.do("vector", lambda e, S, s=s, tgt=tgt, rp=r_prev[s], pr=pr: e.tensor_tensor(
                                tgt.rearrange("p a b -> p (a b)"), psH(pr, s),
                                rp.rearrange("p a b -> p (a b)"), ALU.add))
                        else:
                            p.do("vector", lambda e, S, s=s, tgt=tgt, rp=r_prev[s], pr=pr: e.tensor_tensor(
                                tgt.rearrange("p a b -> p (a b)"),
                                rp.rearrange("p a b -> p (a b)"), psH(pr, s), ALU.subtract))
                        r_tok[s] = p.mark("vector", "vd")
                        free_tok[(pr, s)] = r_tok[s]
                        r_prev[s] = tgt
                for s in range(2):
                    rfin[s], t_rfin[s] = r_prev[s], r_tok[s]
                    p.waits("tensor", [t_rfin[s], free_tok[(2, s)]])
                    for hh in range(QH):
                        p.do("tensor", lambda e, S, hh=hh, s=s, rf=rfin[s]: e.transpose(
                            ps[2].ap().bitcast(BF16)[0:64, 1024 * s + 128 * hh:1024 * s + 128 * (hh + 1)],
                            rf[:, hh, 0:64], sb["ident"][:, :]))
                    t_ut = p.mark("tensor", "td")
                    p.waits("scalar", [t_ut, t_abc])
                    for hh in range(QH):
                        h = h0 + hh
                        p.do("scalar", lambda e, S, hh=hh, h=h, s=s: e.activation(
                            sUFB(s)[:, hh, :], ps[2].ap().bitcast(BF16)[0:64, 1024 * s + 128 * hh:1024 * s + 128 * (hh + 1)],
                            AF.Identity, scale=sb["abc"].ap()[0:64, h:h + 1]))
                    t_uf[s] = p.mark("scalar", "sd")
                    free_tok[(2, s)] = t_uf[s]
                t_uf0 = t_uf[1]

                for s in range(2):
                    c = cs[s]
                    if c == 0:
                        w_view, t_w = rfin[s][:, :, 64:128], t_rfin[s]
                    else:
                        p.waits("tensor", [t_uf[s], t_gbf, free_tok[(3, s)]])
                        for hh in range(QH):
                            p.do("tensor", lambda e, S, hh=hh, s=s: e.matmul(
                                psH(3, s, 64 * hh, 64),
                                sUFB(s)[0:64, hh, :], GBFs[0:64, hh, :], start=True, stop=True))
                        t_wmm = p.mark("tensor", "td")
                        p.waits("vector", [t_wmm, t_rfin[s]])
                        p.do("vector", lambda e, S, s=s, rf=rfin[s]: e.tensor_tensor(
                            sWBUF(s), rf[:, :, 64:128],
                            psH(3, s, 0, 256).rearrange("p (a b) -> p a b", b=64), ALU.subtract))
                        t_w = p.mark("vector", "vd")
                        free_tok[(3, s)] = t_w
                        w_view = sWBUF(s)

                    p.waits("tensor", [t_w, t_at[s], free_tok[(0, s)]] + ([t_gbf] if c > 0 else []))
                    for hh in range(QH):
                        if c > 0:
                            p.do("tensor", lambda e, S, hh=hh, c=c, s=s: e.matmul(
                                psH(0, s, 64 * hh, 64),
                                qfq[0:64, hh, 128 * c:128 * (c + 1)], GBFs[0:64, hh, :],
                                start=True, stop=False))
                        p.do("tensor", lambda e, S, hh=hh, c=c, s=s, w_view=w_view: e.matmul(
                            psH(0, s, 64 * hh, 64),
                            sATB(s)[:, hh, :], w_view[:, hh, :], start=(c == 0), stop=True))
                    t_y = p.mark("tensor", "td")
                    p.wait("vector", t_y)
                    p.do("vector", lambda e, S, c=c, q=q, s=s: e.tensor_copy(
                        yT[:, c, 256 * q:256 * (q + 1)], psH(0, s, 0, 256)))
                    t_yev = p.mark("vector", "vd")
                    free_tok[(0, s)] = t_yev

                    gsrc = GA if c % 2 == 0 else GB
                    gdst = GB if c % 2 == 0 else GA
                    p.waits("tensor", [t_w, free_tok[(1, s)]])
                    for hh in range(QH):
                        p.do("tensor", lambda e, S, hh=hh, c=c, s=s, w_view=w_view: e.matmul(
                            ps[1].ap()[0:64, 512 * s + 64 * hh:512 * s + 64 * (hh + 1)],
                            kv[:, c, hh, 0:64], w_view[:, hh, :], start=True, stop=True))
                    t_dg = p.mark("tensor", "td")
                    p.wait("vector", t_dg)
                    if c == 0:
                        p.do("vector", lambda e, S, gdst=gdst, s=s: e.tensor_copy(
                            gdst, ps[1].ap()[0:64, 512 * s:512 * s + 256]))
                    else:
                        p.do("vector", lambda e, S, gsrc=gsrc, gdst=gdst, s=s: e.tensor_tensor(
                            gdst, ps[1].ap()[0:64, 512 * s:512 * s + 256], gsrc, ALU.add))
                    t_gf = p.mark("vector", "vd")
                    free_tok[(1, s)] = t_gf
                    if c < NCH - 1:
                        p.wait("vector", t_gf)
                        p.do("vector", lambda e, S, gdst=gdst: e.tensor_copy(
                            sb["GBF"][:, :], gdst))
                        t_gbf = p.mark("vector", "vd")
            t_pass_done = t_gf
            t_pass_sd = t_uf0
            if stop_after == "chunks" and q == 0:
                return t_pass_done

        yF = bfv("RA", 0, 16384, 2048)
        t_yf = None
        for i in range(16):
            p.waits("tensor", [t_pass_done, t_pass_sd, t_yf])
            for c8 in range(8):
                p.do("tensor", lambda e, S, i=i, c8=c8: e.transpose(
                    psbf(3, 128 * c8, 128), yT[:, i, 128 * c8:128 * (c8 + 1)], sb["ident"][:, :]))
            t_tr = p.mark("tensor", "td")
            p.wait("vector", t_tr)
            p.do("vector", lambda e, S, i=i: e.tensor_copy(
                yF[:, :, 128 * i:128 * (i + 1)], psbf(3, 0, 1024).rearrange("p (a b) -> p a b", b=128)))
            t_yf = p.mark("vector", "vd")

        p.waits("gpsimd", [t_pass_done])
        t_wp = p.do("gpsimd", lambda e, S: e.dma_start(
            out=bfv("RF", 0, 4096, 512),
            in_=dti["wproj"].ap()[:, 0:512].rearrange("(a p) o -> p a o", p=128)), "gw", 16)
        t_wp2 = p.do("gpsimd", lambda e, S: e.dma_start(
            out=bfv("RF", 4096, 4096, 512),
            in_=dti["wproj"].ap()[:, 512:1024].rearrange("(a p) o -> p a o", p=128)), "gw", 16)
        sT = bfv("RB", 0, 16384, 1024)
        t_sev = None
        for i in range(16):
            for oh in range(2):
                wb = bfv("RF", 4096 * oh, 4096, 512)
                twb = t_wp if oh == 0 else t_wp2
                p.waits("tensor", [twb, t_yf, t_sev])
                for c8 in range(8):
                    p.do("tensor", lambda e, S, i=i, c8=c8, wb=wb: e.matmul(
                        psA(0), yF[:, c8, 128 * i:128 * (i + 1)], wb[:, c8, :],
                        start=(c8 == 0), stop=(c8 == 7)))
                t_mm = p.mark("tensor", "td")
                p.wait("vector", t_mm)
                p.do("vector", lambda e, S, i=i, oh=oh: e.tensor_copy(
                    sT[:, i, 512 * oh:512 * (oh + 1)], psA(0)))
                t_sev = p.mark("vector", "vd")
        p.wait("gpsimd", t_sev)
        t_ccin = p.do("gpsimd", lambda e, S: e.dma_start(
            out=cc_in.ap().rearrange("(a p) c -> p a c", p=128), in_=sT), "gw", 16)
        if no_cc:
            t_cc = t_ccin
        else:
            p.wait("gpsimd", t_ccin)
            t_cc = p.do("gpsimd", lambda e, S: e.collective_compute(
                "AllGather", ALU.bypass, replica_groups=[list(range(8))],
                ins=[cc_in.ap()], outs=[cc_out.ap()]), "cc", 1)
        tp = tapout("sT", (128, 16384), BF16)
        if tp is not None:
            p.wait("sync", t_sev)
            p.do("sync", lambda e, S, tp=tp: e.dma_start(out=tp.ap(), in_=sb["RB"][:, :]), "syt", 16)
        return t_cc, t_sev


    def ffn(t_cc, t_sev0):
        xnf2 = bfv("RA", 0, 16384, 2048)
        ctxT = bfv("RB", 0, 16384, 1024)
        vS = bfv("RC", 0, 4096, 256)
        kfq = sb["KFQ"].ap()[0:64, :].rearrange("p (a b) -> p a b", b=2048)
        qfq = sb["QFQ"].ap()[0:64, :].rearrange("p (a b) -> p a b", b=2048)
        XS2 = sb["RF"].ap().bitcast(F32)[:, 2048:4096].rearrange("p (a b) -> p a b", b=1024)
        XN2 = bfv("RF", 0, 2048, 1024)
        WKB2 = bfv("RF", 2048, 2048, 256)
        WQB2 = bfv("RF", 4096, 2048, 256)
        WVB2 = bfv("RF", 6144, 2048, 256)
        QT2 = bfv("RG", 2048, 1024)

        prev_slot = [None, None]
        t_ev = None
        for i in range(16):
            par = i % 2
            p.waits("sync", [prev_slot[par], t_sev0])
            t_load = p.do("sync", lambda e, S, i=i, par=par: e.dma_start(
                out=XS2[:, par, :], in_=dti["xs"].ap()[128 * i:128 * (i + 1), :]),
                "sy0" if par == 0 else "sy1", 16)
            tln = ln_tile(XS2[:, par, :], XN2[:, par, :], [t_load, prev_slot[par]], par=par)
            p.waits("tensor", [tln, t_ev])
            for c8 in range(8):
                p.do("tensor", lambda e, S, par=par, c8=c8: e.transpose(
                    psbf(3, 128 * c8, 128), XN2[:, par, 128 * c8:128 * (c8 + 1)], sb["ident"][:, :]))
            t_tr = p.mark("tensor", "td")
            p.wait("vector", t_tr)
            p.do("vector", lambda e, S, i=i: e.tensor_copy(
                xnf2[:, :, 128 * i:128 * (i + 1)], psbf(3, 0, 1024).rearrange("p (a b) -> p a b", b=128)))
            t_ev = p.mark("vector", "vd")
            prev_slot[par] = t_ev
        t_xnf2 = t_ev

        rb_gate = ("gw", p.cnt["gw"])
        t_grp = None
        t_ctxe = None
        for g4 in range(4):
            p.waits("gpsimd", [t_xnf2, t_grp])
            t_wq = p.do("gpsimd", lambda e, S, g4=g4: e.dma_start(
                out=WQB2, in_=dti["win"].ap()[:, 256 * g4:256 * g4 + 256].rearrange("(a p) o -> p a o", p=128)), "gw", 16)
            t_wk = p.do("gpsimd", lambda e, S, g4=g4: e.dma_start(
                out=WKB2, in_=dti["win"].ap()[:, CD + 256 * g4:CD + 256 * g4 + 256].rearrange("(a p) o -> p a o", p=128)), "gw", 16)
            t_wv = p.do("gpsimd", lambda e, S, g4=g4: e.dma_start(
                out=WVB2, in_=dti["win"].ap()[:, 2 * CD + 256 * g4:2 * CD + 256 * g4 + 256].rearrange("(a p) o -> p a o", p=128)), "gw", 16)
            free_q = [t_grp, t_grp]
            free_k = [t_grp, t_grp]
            free_v = [t_grp, t_grp]
            free_qt = [t_grp, t_grp]
            free_tr = [t_grp, t_grp]
            for i in range(16):
                par = i % 2
                qsl = ps[0].ap()[:, 512 * par:512 * par + 256]
                ksl = ps[1].ap()[:, 512 * par:512 * par + 256]
                vsl = ps[2].ap()[:, 512 * par:512 * par + 256]
                p.waits("tensor", [t_wq, t_xnf2, free_q[par]])
                for c8 in range(8):
                    p.do("tensor", lambda e, S, i=i, c8=c8, qsl=qsl: e.matmul(
                        qsl, xnf2[:, c8, 128 * i:128 * (i + 1)], WQB2[:, c8, :],
                        start=(c8 == 0), stop=False))
                p.do("tensor", lambda e, S, g4=g4, qsl=qsl: e.matmul(
                    qsl, sb["ones"][0:1, 0:128], sb["brows"].ap()[0:1, 256 * g4:256 * g4 + 256],
                    start=False, stop=True))
                t_qmm = p.mark("tensor", "td")
                p.waits("tensor", [t_wk, free_k[par]])
                for c8 in range(8):
                    p.do("tensor", lambda e, S, i=i, c8=c8, ksl=ksl: e.matmul(
                        ksl, xnf2[:, c8, 128 * i:128 * (i + 1)], WKB2[:, c8, :],
                        start=(c8 == 0), stop=False))
                p.do("tensor", lambda e, S, g4=g4, ksl=ksl: e.matmul(
                    ksl, sb["ones"][0:1, 0:128], sb["brows"].ap()[0:1, CD + 256 * g4:CD + 256 * g4 + 256],
                    start=False, stop=True))
                t_kmm = p.mark("tensor", "td")
                p.waits("tensor", [t_wv, free_v[par]])
                for c8 in range(8):
                    p.do("tensor", lambda e, S, i=i, c8=c8, vsl=vsl: e.matmul(
                        vsl, xnf2[:, c8, 128 * i:128 * (i + 1)], WVB2[:, c8, :],
                        start=(c8 == 0), stop=False))
                p.do("tensor", lambda e, S, g4=g4, vsl=vsl: e.matmul(
                    vsl, sb["ones"][0:1, 0:128], sb["brows"].ap()[0:1, 2 * CD + 256 * g4:2 * CD + 256 * g4 + 256],
                    start=False, stop=True))
                t_vmm = p.mark("tensor", "td")
                p.waits("vector", [t_vmm])
                p.do("vector", lambda e, S, i=i, vsl=vsl: e.tensor_copy(vS[:, i, :], vsl))
                t_vev = p.mark("vector", "vd")
                free_v[par] = t_vev
                p.waits("vector", [t_qmm, free_qt[par]])
                p.do("vector", lambda e, S, par=par, qsl=qsl: e.tensor_copy(
                    QT2[:, 512 * par:512 * par + 256], qsl))
                t_qs = p.mark("vector", "vd")
                free_q[par] = t_qs
                p.waits("vector", [t_kmm])
                p.do("vector", lambda e, S, par=par, ksl=ksl: e.tensor_copy(
                    QT2[:, 512 * par + 256:512 * par + 512], ksl))
                t_ks = p.mark("vector", "vd")
                free_k[par] = t_ks
                p.waits("tensor", [t_qs, t_ks, free_tr[par]])
                for hh in range(QH):
                    p.do("tensor", lambda e, S, hh=hh, par=par: e.transpose(
                        ps[3].ap().bitcast(BF16)[0:64, 1024 * par + 128 * hh:1024 * par + 128 * (hh + 1)],
                        QT2[:, 512 * par + 64 * hh:512 * par + 64 * (hh + 1)], sb["ident"][:, :]))
                for hh in range(QH):
                    p.do("tensor", lambda e, S, hh=hh, par=par: e.transpose(
                        ps[3].ap().bitcast(BF16)[0:64, 1024 * par + 512 + 128 * hh:1024 * par + 512 + 128 * (hh + 1)],
                        QT2[:, 512 * par + 256 + 64 * hh:512 * par + 256 + 64 * (hh + 1)], sb["ident"][:, :]))
                t_tr = p.mark("tensor", "td")
                free_qt[par] = t_tr
                p.waits("vector", [t_tr])
                p.do("vector", lambda e, S, i=i, par=par: e.tensor_copy(
                    qfq[:, :, 128 * i:128 * (i + 1)],
                    ps[3].ap().bitcast(BF16)[0:64, 1024 * par:1024 * par + 512].rearrange("p (a b) -> p a b", b=128)))
                p.bar("vector", "vd")
                p.do("vector", lambda e, S, i=i, par=par: e.tensor_copy(
                    kfq[:, :, 128 * i:128 * (i + 1)],
                    ps[3].ap().bitcast(BF16)[0:64, 1024 * par + 512:1024 * par + 1024].rearrange("p (a b) -> p a b", b=128)))
                t_tre = p.mark("vector", "vd")
                free_tr[par] = t_tre
            t_inproj = t_tre

            free_sc = [t_inproj, t_inproj]
            free1 = [t_inproj, t_inproj]
            free3 = [t_inproj, t_inproj]
            for hh in range(QH):
                h = 4 * g4 + hh
                for pg in range(4):
                    par = (4 * hh + pg) % 2
                    sbase = 40 + 16 * par
                    ebase = 1536 * par
                    pss = ps[par].ap()
                    p.waits("tensor", [t_inproj, free_sc[par]])
                    for pt4 in range(4):
                        pt = 4 * pg + pt4
                        p.do("tensor", lambda e, S, hh=hh, pt=pt, pt4=pt4, pss=pss: e.matmul(
                            pss[:, 128 * pt4:128 * (pt4 + 1)],
                            qfq[0:64, hh, 128 * pt:128 * (pt + 1)],
                            kfq[0:64, hh, 128 * pt:128 * (pt + 1)], start=True, stop=True))
                    t_sc = p.mark("tensor", "td")
                    p.waits("vector", [t_sc])
                    p.do("vector", lambda e, S, pss=pss, sbase=sbase: e.reduce_max(
                        sb["STK"].ap()[:, sbase:sbase + 4],
                        pss[:, 0:512].rearrange("p (a b) -> p a b", b=128), axis=AX.X))
                    p.bar("vector", "vd")
                    p.do("vector", lambda e, S, sbase=sbase: e.tensor_scalar_mul(
                        sb["STK"].ap()[:, sbase + 4:sbase + 8], sb["STK"].ap()[:, sbase:sbase + 4], -0.125))
                    t_mx = p.mark("vector", "vd")
                    p.waits("scalar", [t_mx])
                    for pt4 in range(4):
                        p.do("scalar", lambda e, S, pt4=pt4, pss=pss, ebase=ebase, sbase=sbase: e.activation(
                            bfv("RE", ebase + 128 * pt4, 128),
                            pss[:, 128 * pt4:128 * (pt4 + 1)], AF.Exp,
                            scale=0.125, bias=sb["STK"].ap()[:, sbase + 4 + pt4:sbase + 5 + pt4]))
                    t_ex = p.mark("scalar", "sd")
                    free_sc[par] = t_ex
                    p.waits("vector", [t_ex])
                    p.do("vector", lambda e, S, ebase=ebase: e.tensor_tensor(
                        bfv("RE", ebase + 512, 512, 128), bfv("RE", ebase, 512, 128),
                        sb["blk8"].ap().unsqueeze(1).broadcast_to([128, 4, 128]), ALU.mult))
                    p.bar("vector", "vd")
                    p.do("vector", lambda e, S, ebase=ebase, sbase=sbase: e.reduce_sum(
                        sb["STK"].ap()[:, sbase + 8:sbase + 12], bfv("RE", ebase + 512, 512, 128), axis=AX.X))
                    p.bar("vector", "vd")
                    p.do("vector", lambda e, S, sbase=sbase: e.reciprocal(
                        sb["STK"].ap()[:, sbase + 12:sbase + 16], sb["STK"].ap()[:, sbase + 8:sbase + 12]))
                    p.bar("vector", "vd")
                    p.do("vector", lambda e, S, ebase=ebase, sbase=sbase: e.tensor_tensor(
                        bfv("RE", ebase, 512, 128), bfv("RE", ebase + 512, 512, 128),
                        sb["STK"].ap()[:, sbase + 12:sbase + 16].unsqueeze(2).broadcast_to([128, 4, 128]), ALU.mult))
                    t_pn = p.mark("vector", "vd")
                    p.waits("tensor", [t_pn, free3[par]])
                    for pt4 in range(4):
                        p.do("tensor", lambda e, S, pt4=pt4, par=par, ebase=ebase: e.transpose(
                            ps[3].ap().bitcast(BF16)[:, 1024 * par + 128 * pt4:1024 * par + 128 * (pt4 + 1)],
                            bfv("RE", ebase + 128 * pt4, 128), sb["ident"][:, :]))
                    t_ptr = p.mark("tensor", "td")
                    p.waits("vector", [t_ptr])
                    p.do("vector", lambda e, S, par=par, ebase=ebase: e.tensor_copy(
                        bfv("RE", ebase + 1024, 512, 128),
                        ps[3].ap().bitcast(BF16)[:, 1024 * par:1024 * par + 512].rearrange("p (a b) -> p a b", b=128)))
                    t_pte = p.mark("vector", "vd")
                    free3[par] = t_pte
                    p.waits("tensor", [t_pte, free1[par]])
                    for pt4 in range(4):
                        pt = 4 * pg + pt4
                        p.do("tensor", lambda e, S, hh=hh, pt=pt, pt4=pt4, par=par, ebase=ebase: e.matmul(
                            ps[2].ap()[:, 512 * par + 64 * pt4:512 * par + 64 * (pt4 + 1)],
                            bfv("RE", ebase + 1024 + 128 * pt4, 128),
                            vS[:, pt, 64 * hh:64 * (hh + 1)], start=True, stop=True))
                    t_cx = p.mark("tensor", "td")
                    p.waits("vector", [t_cx, rb_gate])
                    p.do("vector", lambda e, S, h=h, pg=pg, par=par: e.tensor_copy(
                        ctxT[:, 4 * pg:4 * (pg + 1), 64 * h:64 * (h + 1)],
                        ps[2].ap()[:, 512 * par:512 * par + 256].rearrange("p (a b) -> p a b", b=64)))
                    t_ctxe = p.mark("vector", "vd")
                    free1[par] = t_ctxe
            t_grp = t_ctxe

        tpc = tapout("ctx", (128, 16384), BF16)
        t_tapc = None
        if tpc is not None:
            p.wait("sync", t_grp)
            t_tapc = p.do("sync", lambda e, S, tpc=tpc: e.dma_start(out=tpc.ap(), in_=sb["RB"][:, :]), "syt", 16)
        ctxF = bfv("RA", 0, 16384, 2048)
        t_cf = None
        for i in range(16):
            p.waits("tensor", [t_grp, t_cf])
            for c8 in range(8):
                p.do("tensor", lambda e, S, i=i, c8=c8: e.transpose(
                    psbf(3, 128 * c8, 128), ctxT[:, i, 128 * c8:128 * (c8 + 1)], sb["ident"][:, :]))
            t_tr = p.mark("tensor", "td")
            p.wait("vector", t_tr)
            p.do("vector", lambda e, S, i=i: e.tensor_copy(
                ctxF[:, :, 128 * i:128 * (i + 1)], psbf(3, 0, 1024).rearrange("p (a b) -> p a b", b=128)))
            t_cf = p.mark("vector", "vd")

        p.waits("gpsimd", [t_cf])
        t_wo1 = p.do("gpsimd", lambda e, S: e.dma_start(
            out=bfv("RF", 0, 4096, 512),
            in_=dti["wout"].ap()[:, 0:512].rearrange("(a p) o -> p a o", p=128)), "gw", 16)
        t_wo2 = p.do("gpsimd", lambda e, S: e.dma_start(
            out=bfv("RE", 0, 4096, 512),
            in_=dti["wout"].ap()[:, 512:1024].rearrange("(a p) o -> p a o", p=128)), "gw", 16)
        hT = ctxT
        t_hev = t_tapc
        t_xld = None
        xslotA = sb["RF"].ap().bitcast(F32)[:, 3072:4096]
        xslotB = KNS
        zbf = bfv("RG", 2048, 1024)
        for i in range(16):
            xs_slot = xslotA if i % 2 == 0 else xslotB
            p.waits("sync", [t_hev, t_cf])
            t_x = p.do("sync", lambda e, S, i=i, xs_slot=xs_slot: e.dma_start(
                out=xs_slot, in_=dti["xs"].ap()[128 * i:128 * (i + 1), :]),
                "sy0" if i % 2 == 0 else "sy1", 16)
            for oh in range(2):
                wb = bfv("RF", 0, 4096, 512) if oh == 0 else bfv("RE", 0, 4096, 512)
                twb = t_wo1 if oh == 0 else t_wo2
                p.waits("tensor", [twb, t_cf] + ([t_hev] if i or oh else []))
                for c8 in range(8):
                    p.do("tensor", lambda e, S, i=i, c8=c8, wb=wb, oh=oh: e.matmul(
                        ps[2].ap()[:, 512 * oh:512 * (oh + 1)], ctxF[:, c8, 128 * i:128 * (i + 1)], wb[:, c8, :],
                        start=(c8 == 0), stop=False))
                p.do("tensor", lambda e, S, oh=oh: e.matmul(
                    ps[2].ap()[:, 512 * oh:512 * (oh + 1)], sb["ones"][0:1, 0:128],
                    sb["brows"].ap()[0:1, 3 * CD + 512 * oh:3 * CD + 512 * (oh + 1)],
                    start=False, stop=True))
            t_om = p.mark("tensor", "td")
            p.waits("vector", [t_om, t_x])
            p.do("vector", lambda e, S, xs_slot=xs_slot: e.tensor_tensor(
                xs_slot, ps[2].ap()[:, 0:1024], xs_slot, ALU.add))
            t_hp = p.mark("vector", "vd")
            tln = ln_tile(xs_slot, zbf, [t_hp], par=i % 2,
                          gain=sb["gbc"][:, :], bias=sb["hbc"][:, :], out2=hT[:, i, :])
            t_hev = tln
        t_h = t_hev

        tph = tapout("hT", (128, 16384), BF16)
        t_taph = None
        if tph is not None:
            p.wait("sync", t_h)
            t_taph = p.do("sync", lambda e, S, tph=tph: e.dma_start(out=tph.ap(), in_=sb["RB"][:, :]), "syt", 16)
        hF = bfv("RA", 0, 16384, 2048)
        t_hf = None
        for i in range(16):
            p.waits("tensor", [t_h, t_hf])
            for c8 in range(8):
                p.do("tensor", lambda e, S, i=i, c8=c8: e.transpose(
                    psbf(3, 128 * c8, 128), hT[:, i, 128 * c8:128 * (c8 + 1)], sb["ident"][:, :]))
            t_tr = p.mark("tensor", "td")
            p.wait("vector", t_tr)
            p.do("vector", lambda e, S, i=i: e.tensor_copy(
                hF[:, :, 128 * i:128 * (i + 1)], psbf(3, 0, 1024).rearrange("p (a b) -> p a b", b=128)))
            t_hf = p.mark("vector", "vd")

        p.waits("tensor", [t_hf])
        for i in range(16):
            for c8 in range(8):
                p.do("tensor", lambda e, S, i=i, c8=c8: e.matmul(
                    ps[2].ap()[:, 8 * i:8 * (i + 1)], hF[:, c8, 128 * i:128 * (i + 1)],
                    sb["WRB"].ap().rearrange("p (a b) -> p a b", b=8)[:, c8, :],
                    start=(c8 == 0), stop=False))
            p.do("tensor", lambda e, S, i=i: e.matmul(
                ps[2].ap()[:, 8 * i:8 * (i + 1)], sb["ones"][0:1, 0:128],
                sb["brows"].ap()[0:1, 4 * CD:4 * CD + E], start=False, stop=True))
        t_rt = p.mark("tensor", "td")
        p.wait("vector", t_rt)
        LG = sb["STK"].ap()[:, 96:224]
        p.do("vector", lambda e, S: e.tensor_copy(LG, ps[2].ap()[:, 0:128]))
        p.bar("vector", "vd")
        p.do("vector", lambda e, S: e.reduce_max(
            sb["STK"].ap()[:, 40:56], LG.rearrange("p (a b) -> p a b", b=8), axis=AX.X))
        t_m1 = p.mark("vector", "vd")
        p.wait("vector", t_m1)
        CAND = sb["STK"].ap()[:, 224:352]
        for i in range(16):
            p.do("vector", lambda e, S, i=i: e.tensor_scalar(
                CAND[:, 8 * i:8 * (i + 1)], LG[:, 8 * i:8 * (i + 1)],
                sb["STK"].ap()[:, 40 + i:41 + i], None, ALU.is_ge))
        p.bar("vector", "vd")
        p.do("vector", lambda e, S: e.scalar_tensor_tensor(
            CAND, CAND, -1e30, LG, ALU.mult, ALU.add))
        p.bar("vector", "vd")
        p.do("vector", lambda e, S: e.reduce_max(
            sb["STK"].ap()[:, 56:72], CAND.rearrange("p (a b) -> p a b", b=8), axis=AX.X))
        t_m2 = p.mark("vector", "vd")
        p.wait("vector", t_m2)
        for i in range(16):
            p.do("vector", lambda e, S, i=i: e.tensor_scalar(
                CAND[:, 8 * i:8 * (i + 1)], LG[:, 8 * i:8 * (i + 1)],
                sb["STK"].ap()[:, 56 + i:57 + i], None, ALU.is_ge))
        p.bar("vector", "vd")
        p.do("vector", lambda e, S: e.tensor_tensor(
            CAND.rearrange("p (a b) -> p a b", b=8), CAND.rearrange("p (a b) -> p a b", b=8),
            sb["iota"].ap().unsqueeze(1).broadcast_to([128, 16, 8]), ALU.mult))
        p.bar("vector", "vd")
        p.do("vector", lambda e, S: e.reduce_max(
            sb["STK"].ap()[:, 72:88], CAND.rearrange("p (a b) -> p a b", b=8), axis=AX.X))
        t_asn = p.mark("vector", "vd")
        p.wait("vector", t_asn)
        OHT = bfv("RG", 3072, 128)
        for i in range(16):
            p.do("vector", lambda e, S, i=i: e.tensor_scalar(
                OHT[:, 8 * i:8 * (i + 1)], sb["iota"][:, :],
                sb["STK"].ap()[:, 72 + i:73 + i], None, ALU.is_equal))
        t_oht = p.mark("vector", "vd")
        t_ohf = None
        for ig in range(4):
            p.waits("tensor", [t_oht, t_ohf])
            for i4 in range(4):
                i = 4 * ig + i4
                p.do("tensor", lambda e, S, i=i, i4=i4: e.transpose(
                    ps[3].ap().bitcast(BF16)[0:8, 128 * i4:128 * (i4 + 1)],
                    OHT[:, 8 * i:8 * (i + 1)], sb["ident"][:, :]))
            t_tr = p.mark("tensor", "td")
            p.wait("vector", t_tr)
            p.do("vector", lambda e, S, ig=ig: e.tensor_copy(
                sb["OHF"].ap()[0:8, 512 * ig:512 * (ig + 1)],
                ps[3].ap().bitcast(BF16)[0:8, 0:512]))
            t_ohf = p.mark("vector", "vd")
        p.wait("gpsimd", t_ohf)
        t_ohd = p.do("gpsimd", lambda e, S: e.dma_start(out=oh_dram.ap(), in_=sb["OHF"][:, :]), "gs", 16)
        tpo = tapout("ohf", (E, TOK), BF16)
        if tpo is not None:
            p.wait("sync", t_ohf)
            p.do("sync", lambda e, S, tpo=tpo: e.dma_start(out=tpo.ap(), in_=sb["OHF"][:, :]), "syt", 16)

        wbufs = [(bfv("RC", 0, 8192, 1024), bfv("RE", 0, 8192, 1024)),
                 (sb["KFQ"].ap().rearrange("p (a b) -> p a b", b=1024),
                  sb["QFQ"].ap().rearrange("p (a b) -> p a b", b=1024))]
        heb = bfv("RF", 0, 2048, 256)
        hemb = bfv("RF", 2048, 2048, 256)
        mke = bfv("RG", 0, 2048)
        t_wload = [None, None]
        t_mload = [None, None]
        p.waits("gpsimd", [t_hf, t_ohd])
        t_w1 = p.do("gpsimd", lambda e, S: e.dma_start(
            out=wbufs[0][0], in_=dti["w1"].ap()[0].rearrange("(a p) o -> p a o", p=128)), "gw0", 16)
        t_w2 = p.do("gpsimd", lambda e, S: e.dma_start(
            out=wbufs[0][1], in_=dti["w2"].ap()[0].rearrange("(a p) o -> p a o", p=128)), "gw0", 16)
        t_wload[0] = t_w2
        t_mk = p.do("gpsimd", lambda e, S: e.dma_start(
            out=mke, in_=oh_dram.ap()[0:1, :].partition_broadcast(128).rearrange("p o e -> p (o e)")), "gm0", 16)
        t_mload[0] = t_mk
        mkev = [mke, bfv("RG", 2048, 2048)]
        t_heprev = None
        t_msk = None
        t_ev2 = {}
        t_mm2done = None
        for ex in range(E):
            par = ex % 2
            w1b, w2b = wbufs[par]
            if ex + 1 < E:
                npar = 1 - par
                p.waits("gpsimd", [t_wload[npar], t_mm2done, t_hf])
                t_nw1 = p.do("gpsimd", lambda e, S, ex=ex, npar=npar: e.dma_start(
                    out=wbufs[npar][0], in_=dti["w1"].ap()[ex + 1].rearrange("(a p) o -> p a o", p=128)), "gw0" if npar == 0 else "gw1", 16)
                t_nw2 = p.do("gpsimd", lambda e, S, ex=ex, npar=npar: e.dma_start(
                    out=wbufs[npar][1], in_=dti["w2"].ap()[ex + 1].rearrange("(a p) o -> p a o", p=128)), "gw0" if npar == 0 else "gw1", 16)
                t_wload[npar] = t_nw2
                p.waits("gpsimd", [t_mload[npar], t_msk])
                t_nmk = p.do("gpsimd", lambda e, S, ex=ex, npar=npar: e.dma_start(
                    out=mkev[npar], in_=oh_dram.ap()[ex + 1:ex + 2, :].partition_broadcast(128).rearrange("p o e -> p (o e)")), "gm0" if npar == 0 else "gm1", 16)
                t_mload[npar] = t_nmk
            t_gelh = {0: None, 1: None}
            for tc in range(8):
                for ohalf in range(2):
                    p.waits("tensor", [t_wload[par], t_hf, t_gelh[ohalf]])
                    for ot4 in range(4):
                        ot = 4 * ohalf + ot4
                        for c8 in range(8):
                            p.do("tensor", lambda e, S, ot=ot, c8=c8, tc=tc, w1b=w1b: e.matmul(
                                ps[ot // 4].ap()[:, 256 * (ot % 4):256 * (ot % 4 + 1)],
                                w1b[:, c8, 128 * ot:128 * (ot + 1)],
                                hF[:, c8, 256 * tc:256 * (tc + 1)],
                                start=(c8 == 0), stop=(c8 == 7)))
                    t_heh = p.mark("tensor", "td")
                    p.wait("scalar", t_heh)
                    for ot4 in range(4):
                        ot = 4 * ohalf + ot4
                        p.do("scalar", lambda e, S, ot=ot, ex=ex: e.activation(
                            heb[:, ot, :], ps[ot // 4].ap()[:, 256 * (ot % 4):256 * (ot % 4 + 1)],
                            AF.Gelu, bias=sb["b1c"].ap()[:, 8 * ex + ot:8 * ex + ot + 1]))
                    t_gelh[ohalf] = p.mark("scalar", "sd")
                t_heprev = t_gelh[1]
                p.waits("vector", [t_gelh[1], t_mload[par]])
                p.do("vector", lambda e, S, tc=tc, par=par: e.tensor_tensor(
                    hemb, heb,
                    mkev[par][:, 256 * tc:256 * (tc + 1)].unsqueeze(1).broadcast_to([128, 8, 256]),
                    ALU.mult))
                t_msk = p.mark("vector", "vd")
                for ts in range(2):
                    p.waits("tensor", [t_msk, t_wload[par], t_ev2.get((tc, ts))])
                    for ot in range(8):
                        for oh in range(2):
                            p.do("tensor", lambda e, S, ts=ts, oh=oh, ot=ot, w2b=w2b, ex=ex: e.matmul(
                                ps[2 + ts].ap()[:, 512 * oh:512 * (oh + 1)],
                                hemb[:, ot, 128 * ts:128 * (ts + 1)],
                                w2b[:, ot, 512 * oh:512 * (oh + 1)],
                                start=(ot == 0), stop=(ot == 7 and ex != E - 1)))
                    if ex == E - 1:
                        p.waits("tensor", [t_ohf])
                        for oh in range(2):
                            p.do("tensor", lambda e, S, ts=ts, oh=oh, tc=tc: e.matmul(
                                ps[2 + ts].ap()[:, 512 * oh:512 * (oh + 1)],
                                sb["OHF"].ap()[0:8, 256 * tc + 128 * ts:256 * tc + 128 * (ts + 1)],
                                sb["b2r"].ap()[0:8, 512 * oh:512 * (oh + 1)],
                                start=False, stop=True))
                    t_mm2 = p.mark("tensor", "td")
                    t_mm2done = t_mm2
                    p.waits("vector", [t_mm2, t_taph])
                    p.do("vector", lambda e, S, ts=ts, tc=tc: e.tensor_tensor(
                        hT[:, 2 * tc + ts, :], ps[2 + ts].ap()[:, 0:1024], hT[:, 2 * tc + ts, :], ALU.add))
                    t_ev2[(tc, ts)] = p.mark("vector", "vd")
        t_moe_ev = t_ev2[(7, 1)]

        tpz = tapout("z2", (128, 16384), BF16)
        t_tapz = None
        if tpz is not None:
            p.wait("sync", t_moe_ev)
            t_tapz = p.do("sync", lambda e, S, tpz=tpz: e.dma_start(out=tpz.ap(), in_=sb["RB"][:, :]), "syt", 16)
        p.waits("gpsimd", [t_moe_ev])
        t_g2 = p.do("gpsimd", lambda e, S: e.dma_start(
            out=sb["gbc"][:, :], in_=dti["g2"].ap().partition_broadcast(128).rearrange("p o e -> p (o e)")), "gg", 16)
        t_h2 = p.do("gpsimd", lambda e, S: e.dma_start(
            out=sb["hbc"][:, :], in_=dti["h2"].ap().partition_broadcast(128).rearrange("p o e -> p (o e)")), "gg", 16)
        t_ln2 = [None] * 16
        for i in range(16):
            tln = ln_tile(hT[:, i, :], zbf, [t_moe_ev, t_g2, t_h2, t_tapz], par=i % 2,
                          gain=sb["gbc"][:, :], bias=sb["hbc"][:, :], out2=hT[:, i, :])
            t_ln2[i] = tln
        pid_holder = {}
        def get_pid_off(e, S):
            if "off" not in pid_holder:
                pid = nc.partition_id(engines=[mybir.EngineType.Pool])
                pid_holder["off"] = e.snap(pid * PSL, donate=False, min_val=0, max_val=(B - 1) * PSL)
            return pid_holder["off"]
        t_st = [None, None]
        t_sg = [None, None]
        t_out = [None, None]
        t_sil = [None, None]
        gtile = [bfv("RF", 0, 1024), bfv("RF", 1024, 1024)]
        prodf = [sb["RF"].ap().bitcast(F32)[:, 1024:2048], sb["RF"].ap().bitcast(F32)[:, 2048:3072]]
        xcf = [sb["RF"].ap().bitcast(F32)[:, 3072:4096], sb["KFQ"].ap().bitcast(F32)[:, 0:1024]]
        stile = [bfv("RG", 0, 1024), bfv("RG", 4096 - 1024, 1024)]
        for i in range(16):
            pr = i % 2
            p.waits("gpsimd", [None if no_cc else t_cc, t_sg[pr], t_moe_ev, t_out[pr]])
            def sgather(e, S, i=i, pr=pr):
                off = get_pid_off(e, S)
                src = cc_out.ap().rearrange("(b t) c -> t b c", t=T)[16 * i:, :, :]
                return e.dma_start(out=stile[pr], in_=src[bass.ds(off, 16), :, :])
            t_sg[pr] = p.do("gpsimd", sgather, "gq0" if pr == 0 else "gq1", 16)
            p.waits("sync", [t_st[pr], t_out[pr]])
            t_x = p.do("sync", lambda e, S, i=i, pr=pr: e.dma_start(
                out=xcf[pr], in_=dti["xs"].ap()[128 * i:128 * (i + 1), :]),
                "sy0" if pr == 0 else "sy1", 16)
            p.waits("scalar", [t_ln2[i], t_out[pr]])
            p.do("scalar", lambda e, S, i=i, pr=pr: e.activation(gtile[pr], hT[:, i, :], AF.Silu))
            t_sil[pr] = p.mark("scalar", "sd")
            p.waits("vector", [t_sil[pr], t_sg[pr], t_x, t_st[pr]])
            p.do("vector", lambda e, S, pr=pr: e.tensor_tensor(prodf[pr], stile[pr], gtile[pr], ALU.mult))
            p.bar("vector", "vd")
            p.do("vector", lambda e, S, pr=pr: e.tensor_tensor(prodf[pr], prodf[pr], xcf[pr], ALU.add))
            t_out[pr] = p.mark("vector", "vd")
            p.wait("sync", t_out[pr])
            t_st[pr] = p.do("sync", lambda e, S, i=i, pr=pr: e.dma_start(
                out=dti["out"].ap()[128 * i:128 * (i + 1), :], in_=prodf[pr]), "syg", 16)
        p.waits("sync", [t_st[0], t_st[1]])
        p.waits("gpsimd", [t_st[0], t_st[1]])

    def ffn_stub(t_cc):
        p.wait("sync", t_cc)
        t_pr = None
        for i in range(16):
            t_l = p.do("sync", lambda e, S, i=i: e.dma_start(
                out=bfv("RF", 0, 1024),
                in_=(cc_in if no_cc else cc_out).ap()[128 * i:128 * (i + 1), :]), "syg", 16)
            p.waits("vector", [t_l, t_pr])
            p.do("vector", lambda e, S: e.tensor_copy(KNS, bfv("RF", 0, 1024)))
            t_c = p.mark("vector", "vd")
            p.wait("sync", t_c)
            t_pr = p.do("sync", lambda e, S, i=i: e.dma_start(
                out=dti["out"].ap()[128 * i:128 * (i + 1), :], in_=KNS), "syg", 16)
        p.wait("sync", t_pr)

    t_abc = setup()
    t_ret = titans(t_abc)
    t_cc, t_sev0 = t_ret if isinstance(t_ret, tuple) else (t_ret, t_ret)
    if stop_after is not None:
        p.wait("sync", t_cc)
        tz = p.do("sync", lambda e, S: e.dma_start(
            out=dti["out"].ap()[0:128, :], in_=KNS), "syg", 16)
        p.wait("sync", tz)
    elif not ffn:
        ffn_stub(t_cc)
    else:
        ffn(t_cc, t_sev0)
    emit(nc, p)
    return nc, dti, tap_set



def make_inmaps(inputs):
    x = np.asarray(inputs["L0_attn_x"])
    bin_ = np.asarray(inputs["L0_ffn_in_proj_b"])
    shared = dict(
        wq=np.asarray(inputs["L0_attn_Wq"]).T,
        wk=np.asarray(inputs["L0_attn_Wk"]).T,
        wv=np.asarray(inputs["L0_attn_Wv"]).T,
        wproj=np.asarray(inputs["L0_attn_Wproj"]).T,
        win=np.asarray(inputs["L0_ffn_in_proj_w"]).T,
        wout=np.asarray(inputs["L0_ffn_out_w"]).T,
        wr=np.asarray(inputs["L0_ffn_router_w"]).T,
        w1=np.asarray(inputs["L0_ffn_w1"]).transpose(0, 2, 1),
        w2=np.asarray(inputs["L0_ffn_w2"]).transpose(0, 2, 1),
        binrow=bin_.reshape(1, 3 * CD),
        ones1=np.ones((1, 512), np.float32),
        bout=np.asarray(inputs["L0_ffn_out_b"]).reshape(1, CD),
        brout=np.asarray(inputs["L0_ffn_router_b"]).reshape(1, E),
        b1c=np.asarray(inputs["L0_ffn_b1"]).reshape(E * 8, 128).T,
        b2r=np.asarray(inputs["L0_ffn_b2"]),
        g1=np.asarray(inputs["L0_ffn_ln1_g"]).reshape(1, CD),
        h1=np.asarray(inputs["L0_ffn_ln1_b"]).reshape(1, CD),
        g2=np.asarray(inputs["L0_ffn_ln2_g"]).reshape(1, CD),
        h2=np.asarray(inputs["L0_ffn_ln2_b"]).reshape(1, CD),
        araw=np.asarray(inputs["L0_attn_alpha_raw"]).reshape(1, H),
        braw=np.asarray(inputs["L0_attn_beta_raw"]).reshape(1, H),
        ident=np.eye(128, dtype=np.float32),
        stril=np.tril(np.ones((128, 128), np.float32), -1),
        striu=np.triu(np.ones((128, 128), np.float32), 1),
        blk8=np.kron(np.eye(16, dtype=np.float32), np.ones((8, 8), np.float32)),
        iota8=np.arange(E, dtype=np.float32).reshape(1, E),
    )
    shared = {k: np.ascontiguousarray(v, dtype=np.float32) for k, v in shared.items()}
    in_maps = []
    for i in range(8):
        m = dict(shared)
        m["xb"] = np.ascontiguousarray(x[i], dtype=np.float32)
        m["xs"] = np.ascontiguousarray(
            x[:, PSL * i:PSL * (i + 1), :].transpose(1, 0, 2).reshape(TOK, CD), dtype=np.float32)
        in_maps.append(m)
    return in_maps


def assemble_out(results):
    out = np.empty((B, T, CD), np.float32)
    for i in range(8):
        o = results[i]["out"].reshape(PSL, B, CD).transpose(1, 0, 2)
        out[:, PSL * i:PSL * (i + 1), :] = o
    return out


_CACHE = {}


def kernel(**inputs):
    if "nc" not in _CACHE:
        nc, dti, tapset = build_nc()
        _CACHE["nc"] = nc
    nc = _CACHE["nc"]
    in_maps = make_inmaps(inputs)
    res = run_bass_kernel_spmd(nc, in_maps, core_ids=list(range(8)))
    return assemble_out(res.results)


# version 06 (final)
# speedup vs baseline: 1.0500x; 1.0500x over previous
import numpy as np
from contextlib import ExitStack

import concourse.bass as bass
import concourse.mybir as mybir
from concourse.bass_utils import run_bass_kernel_spmd

F32 = mybir.dt.float32
BF16 = mybir.dt.bfloat16
AF = mybir.ActivationFunctionType
ALU = mybir.AluOpType
AX = mybir.AxisListType

B, T, CD, H, D, CS = 8, 2048, 1024, 16, 64, 128
NCH = T // CS
PSL = T // 8
TOK = B * PSL
E = 8
QH = 4
NQ = H // QH
EPS = 1e-5

ENGINES = ("tensor", "vector", "scalar", "gpsimd", "sync")
DMA_SEMS = ("gs", "gw", "gw0", "gw1", "gm0", "gm1", "gg", "gq0", "gq1", "sy0", "sy1", "syt", "syg")


class Prog:
    def __init__(self):
        self.st = {e: [] for e in ENGINES}
        self.semnames = []
        self.cnt = {}

    def sem(self, name):
        self.semnames.append(name)
        self.cnt[name] = 0

    def op(self, eng, fn):
        self.st[eng].append(fn)

    def wait(self, eng, tok):
        if tok is None:
            return
        sem, val = tok
        if sem in DMA_SEMS:
            val = self.cnt[sem]
        if val <= 0:
            return
        self.op(eng, lambda e, S, sem=sem, val=val: e.wait_ge(S[sem], val))

    def waits(self, eng, toks):
        best = {}
        for t in toks:
            if t is None:
                continue
            s, v = t
            if v > best.get(s, -1):
                best[s] = v
        for s, v in best.items():
            self.wait(eng, (s, v))

    def do(self, eng, fn, sem=None, k=1):
        if sem is None:
            self.op(eng, fn)
            return None
        self.cnt[sem] += k
        v = self.cnt[sem]

        def wrapped(e, S, fn=fn, sem=sem, k=k):
            fn(e, S).then_inc(S[sem], k)

        self.op(eng, wrapped)
        return (sem, v)

    def bar(self, eng, sem):
        t = self.mark(eng, sem)
        self.wait(eng, t)
        return t

    def mark(self, eng, sem, k=1):
        self.cnt[sem] += k
        v = self.cnt[sem]
        fn = self.st[eng].pop()

        def wrapped(e, S, fn=fn, sem=sem, k=k):
            fn(e, S).then_inc(S[sem], k)

        self.st[eng].append(wrapped)
        return (sem, v)


def emit(nc, p):
    with ExitStack() as ctx:
        S = {}
        for name in p.semnames:
            S[name] = ctx.enter_context(nc.semaphore(name))
        block = ctx.enter_context(nc.Block())
        for eng in ENGINES:
            stream = p.st[eng]

            def body(handle, stream=stream, S=S):
                for fn in stream:
                    fn(handle, S)

            getattr(block, eng)(body)


def build_nc(taps=(), ffn=True, no_cc=False, stop_after=None):
    taps = set(taps)
    nc = bass.Bass("TRN2", target_bir_lowering=False)
    p = Prog()
    for s in ("td", "vd", "sd", "gs", "gw", "gw0", "gw1", "gm0", "gm1", "gg", "gq0", "gq1", "sy0", "sy1", "syt", "syg", "cc"):
        p.sem(s)

    dti = {}

    def din(name, shape, dtype=F32):
        dti[name] = nc.dram_tensor(name, list(shape), dtype, kind="ExternalInput")
        return dti[name]

    def dout(name, shape, dtype=F32):
        dti[name] = nc.dram_tensor(name, list(shape), dtype, kind="ExternalOutput")
        return dti[name]

    din("xb", (T, CD))
    din("xs", (TOK, CD))
    for w in ("wq", "wk", "wv", "wproj", "wout"):
        din(w, (CD, CD))
    din("win", (CD, 3 * CD))
    din("wr", (CD, E))
    din("w1", (E, CD, CD))
    din("w2", (E, CD, CD))
    din("binrow", (1, 3 * CD))
    din("bout", (1, CD))
    din("ones1", (1, 512))
    din("brout", (1, E))
    din("b1c", (128, E * 8))
    din("b2r", (E, CD))
    din("g1", (1, CD)); din("h1", (1, CD)); din("g2", (1, CD)); din("h2", (1, CD))
    din("araw", (1, H)); din("braw", (1, H))
    din("ident", (128, 128))
    din("stril", (128, 128))
    din("striu", (128, 128))
    din("blk8", (128, 128))
    din("iota8", (1, E))

    dout("out", (TOK, CD))
    tap_set = {}

    def tapout(name, shape, dtype=BF16):
        if name in taps:
            tap_set[name] = dout("tap_" + name, shape, dtype)
            return tap_set[name]
        return None

    ab_dram = nc.dram_tensor("ab_dram", [1, 2 * H], F32)
    oh_dram = nc.dram_tensor("oh_dram", [E, TOK], BF16)
    cc_in = nc.dram_tensor("cc_in", [T, CD], BF16)
    cc_out = nc.dram_tensor("cc_out", [B * T, CD], BF16)

    sb = {}

    def salloc(name, shape, dtype):
        sb[name] = nc.alloc_sbuf_tensor("sb_" + name, list(shape), dtype)
        return sb[name]

    salloc("ident", (128, 128), BF16)
    salloc("stril", (128, 128), F32)
    salloc("striu", (128, 128), F32)
    salloc("blk8", (128, 128), BF16)
    salloc("iota", (128, E), F32)
    salloc("ab_raw", (1, 2 * H), F32)
    salloc("ab_sig", (1, 2 * H), F32)
    salloc("abc", (128, 2 * H), F32)
    salloc("ones", (1, 512), BF16)
    salloc("WRB", (128, 64), BF16)
    salloc("OHF", (8, 2048), BF16)
    salloc("b1c", (128, E * 8), F32)
    salloc("b2r", (E, CD), BF16)
    salloc("brows", (1, 4 * CD + E), BF16)
    salloc("gbc", (128, CD), BF16)
    salloc("hbc", (128, CD), BF16)

    salloc("RA", (128, 16384), BF16)
    salloc("RB", (128, 16384), BF16)
    salloc("RC", (128, 8192), BF16)
    salloc("KFQ", (128, 8192), BF16)
    salloc("QFQ", (128, 8192), BF16)
    salloc("RE", (128, 15360), BF16)
    salloc("RF", (128, 8192), BF16)
    salloc("RG", (128, 4096), BF16)
    salloc("MS", (128, 1024), F32)
    salloc("GAB", (64, 512), F32)
    salloc("STK", (128, 384), F32)
    salloc("GBF", (64, 256), BF16)

    ps = [nc.alloc_psum_tensor(f"PP{i}", [128, 1024], F32) for i in range(4)]

    def psA(i, n=512):
        return ps[i].ap()[:, 0:n]

    def psbf(i, off, n):
        return ps[i].ap().bitcast(BF16)[:, off:off + n]

    def bfv(region, off, n, inner=None):
        ap = sb[region][:, off:off + n]
        if inner is not None:
            ap = ap.rearrange("p (a b) -> p a b", b=inner)
        return ap

    MSL = sb["MS"].ap()[:, 0:512].rearrange("p (a b) -> p a b", b=128)
    MSU = sb["MS"].ap()[:, 512:1024].rearrange("p (a b) -> p a b", b=128)
    GA = sb["GAB"].ap()[0:64, 0:256]
    GB = sb["GAB"].ap()[0:64, 256:512]
    KNS = sb["RG"].ap().bitcast(F32)[:, 0:1024]
    STT = [sb["STK"].ap()[:, 0:16], sb["STK"].ap()[:, 16:32]]
    KST = sb["STK"].ap()[:, 32:40]
    QTS = bfv("RG", 2048, 512)

    def PWT(lvl):
        return bfv("RE", lvl * 512, 512, 128)

    def PWU(par):
        return bfv("RE", (7 + par) * 512, 512, 128)

    def RBUF(par):
        return bfv("RE", (9 + par) * 512, 512, 128)

    ATB = bfv("RE", 11 * 512, 512, 128)
    NBUF = bfv("RE", 12 * 512, 512, 128)
    UFB = sb["RE"].ap()[0:64, 13 * 512:13 * 512 + 512].rearrange("p (a b) -> p a b", b=128)
    WBUF = bfv("RE", 14 * 512, 256, 64)
    GBF = sb["RE"].ap()[0:64, 14 * 512 + 256:14 * 512 + 512].rearrange("p (a b) -> p a b", b=64)

    XSTG = sb["RF"].ap().bitcast(F32)[:, 2048:4096].rearrange("p (a b) -> p a b", b=1024)
    XNT = bfv("RF", 0, 2048, 1024)
    WKB = bfv("RF", 2048, 2048, 256)
    WQB = bfv("RF", 4096, 2048, 256)
    WVBc = bfv("RF", 6144, 2048, 256)

    def setup():
        def g(dst, src):
            return p.do("gpsimd", lambda e, S, d=dst, s=src: e.dma_start(out=d, in_=s), "gs", 16)

        def bc(src):
            return src.partition_broadcast(128).rearrange("p o e -> p (o e)")

        g(sb["ident"][:, :], dti["ident"].ap())
        g(sb["stril"][:, :], dti["stril"].ap())
        g(sb["striu"][:, :], dti["striu"].ap())
        g(sb["blk8"][:, :], dti["blk8"].ap())
        g(sb["iota"][:, :], bc(dti["iota8"].ap()))
        g(sb["ones"][0:1, :], dti["ones1"].ap())
        g(sb["b1c"][:, :], dti["b1c"].ap())
        g(sb["b2r"][:, :], dti["b2r"].ap())
        g(sb["brows"][0:1, 0:3 * CD], dti["binrow"].ap())
        g(sb["brows"][0:1, 3 * CD:4 * CD], dti["bout"].ap())
        g(sb["brows"][0:1, 4 * CD:4 * CD + E], dti["brout"].ap())
        g(sb["WRB"].ap().rearrange("p (a b) -> p a b", b=8),
          dti["wr"].ap().rearrange("(a p) o -> p a o", p=128))
        g(sb["gbc"][:, :], bc(dti["g1"].ap()))
        g(sb["hbc"][:, :], bc(dti["h1"].ap()))
        g(sb["ab_raw"][0:1, 0:H], dti["araw"].ap())
        t_last = g(sb["ab_raw"][0:1, H:2 * H], dti["braw"].ap())
        p.wait("scalar", t_last)
        t_exp = p.do("scalar", lambda e, S: e.activation(sb["ab_sig"][0:1, :], sb["ab_raw"][0:1, :], AF.Exp, scale=-1.0), "sd")
        p.wait("vector", t_exp)
        p.do("vector", lambda e, S: e.tensor_scalar_add(sb["ab_sig"][0:1, :], sb["ab_sig"][0:1, :], 1.0))
        p.bar("vector", "vd")
        p.do("vector", lambda e, S: e.reciprocal(sb["ab_sig"][0:1, :], sb["ab_sig"][0:1, :]))
        p.bar("vector", "vd")
        p.do("vector", lambda e, S: e.tensor_scalar_mul(sb["ab_sig"][0:1, :], sb["ab_sig"][0:1, :], 0.5))
        t_sig = p.mark("vector", "vd")
        p.wait("gpsimd", t_sig)
        t_ab1 = p.do("gpsimd", lambda e, S: e.dma_start(out=ab_dram.ap(), in_=sb["ab_sig"][0:1, :]), "gs", 16)
        p.wait("gpsimd", t_ab1)
        t_abc = p.do("gpsimd", lambda e, S: e.dma_start(out=sb["abc"][:, :], in_=bc(ab_dram.ap())), "gs", 16)
        return t_abc

    ln_state = {"prev_apply": None}

    def ln_tile(xin, out, wait_toks, par=0, gain=None, bias=None, out2=None):
        st = STT[par]
        p.waits("vector", wait_toks + [ln_state["prev_apply"]])
        p.do("vector", lambda e, S, st=st, xin=xin: e.bn_stats(st[:, 4:10], xin[:, 0:512]))
        p.bar("vector", "vd")
        p.do("vector", lambda e, S, st=st, xin=xin: e.bn_stats(st[:, 10:16], xin[:, 512:1024]))
        p.bar("vector", "vd")
        p.do("vector", lambda e, S, st=st: e.bn_aggr(st[:, 0:2], st[:, 4:16]))
        p.bar("vector", "vd")
        p.do("vector", lambda e, S, st=st: e.tensor_scalar_add(st[:, 2:3], st[:, 1:2], float(EPS)))
        t_ag = p.mark("vector", "vd")
        p.wait("scalar", t_ag)
        p.do("scalar", lambda e, S, st=st: e.activation(st[:, 2:3], st[:, 2:3], AF.Ln))
        p.bar("scalar", "sd")
        p.do("scalar", lambda e, S, st=st: e.activation(st[:, 2:3], st[:, 2:3], AF.Exp, scale=-0.5))
        t_r = p.mark("scalar", "sd")
        p.wait("vector", t_r)
        p.do("vector", lambda e, S, st=st: e.tensor_scalar(st[:, 3:4], st[:, 0:1], st[:, 2:3], -1.0, ALU.mult, ALU.mult))
        t_b = p.mark("vector", "vd")
        p.wait("scalar", t_b)
        p.do("scalar", lambda e, S, st=st, out=out, xin=xin: e.activation(out, xin, AF.Identity, bias=st[:, 3:4], scale=st[:, 2:3]))
        tok = p.mark("scalar", "sd")
        ln_state["prev_apply"] = tok
        if gain is not None:
            p.wait("vector", tok)
            p.do("vector", lambda e, S, out=out, gain=gain: e.tensor_mul(out, out, gain))
            p.bar("vector", "vd")
            p.do("vector", lambda e, S, out=out, bias=bias, out2=out2: e.tensor_add(out2 if out2 is not None else out, out, bias))
            tok = p.mark("vector", "vd")
        return tok

    def titans(t_abc):
        xnf = bfv("RA", 0, 16384, 2048)
        yT = bfv("RB", 0, 16384, 1024)
        kv = bfv("RC", 0, 8192, 512).rearrange("p a (h d) -> p a h d", d=128)
        kfq = sb["KFQ"].ap()[0:64, :].rearrange("p (a b) -> p a b", b=2048)
        qfq = sb["QFQ"].ap()[0:64, :].rearrange("p (a b) -> p a b", b=2048)

        prev_slot = [None, None]
        t_ev = None
        for i in range(16):
            par = i % 2
            p.waits("sync", [prev_slot[par]])
            t_load = p.do("sync", lambda e, S, i=i, par=par: e.dma_start(
                out=XSTG[:, par, :], in_=dti["xb"].ap()[128 * i:128 * (i + 1), :]),
                "sy0" if par == 0 else "sy1", 16)
            tln = ln_tile(XSTG[:, par, :], XNT[:, par, :], [t_load, prev_slot[par]], par=par)
            p.waits("tensor", [tln, t_ev])
            for c8 in range(8):
                p.do("tensor", lambda e, S, par=par, c8=c8: e.transpose(
                    psbf(3, 128 * c8, 128), XNT[:, par, 128 * c8:128 * (c8 + 1)], sb["ident"][:, :]))
            t_tr = p.mark("tensor", "td")
            p.wait("vector", t_tr)
            p.do("vector", lambda e, S, i=i: e.tensor_copy(
                xnf[:, :, 128 * i:128 * (i + 1)], psbf(3, 0, 1024).rearrange("p (a b) -> p a b", b=128)))
            t_ev = p.mark("vector", "vd")
            prev_slot[par] = t_ev
        t_xnf = t_ev
        tp = tapout("xnf", (128, 16384), BF16)
        if tp is not None:
            p.wait("sync", t_xnf)
            p.do("sync", lambda e, S, tp=tp: e.dma_start(out=tp.ap(), in_=sb["RA"][:, :]), "syt", 16)
        if stop_after == "xn":
            return t_xnf

        t_pass_done = None
        t_pass_sd = None
        t_yev = None
        for q in range(NQ):
            h0 = q * QH
            p.waits("vector", [t_abc, t_pass_done])
            for hh in range(QH):
                h = h0 + hh
                p.do("vector", lambda e, S, hh=hh, h=h: e.tensor_scalar(
                    MSL[:, hh, :], sb["stril"][:, :], sb["abc"][:, h:h + 1], None, ALU.mult))
                p.do("vector", lambda e, S, hh=hh, h=h: e.tensor_scalar(
                    MSU[:, hh, :], sb["striu"][:, :], sb["abc"][:, h:h + 1], None, ALU.mult))
            t_masks = p.mark("vector", "vd")

            p.waits("gpsimd", [t_xnf, t_pass_done])
            t_wk = p.do("gpsimd", lambda e, S, h0=h0: e.dma_start(
                out=WKB, in_=dti["wk"].ap()[:, 64 * h0:64 * h0 + 256].rearrange("(a p) o -> p a o", p=128)), "gw", 16)
            t_wv = p.do("gpsimd", lambda e, S, h0=h0: e.dma_start(
                out=WVBc, in_=dti["wv"].ap()[:, 64 * h0:64 * h0 + 256].rearrange("(a p) o -> p a o", p=128)), "gw", 16)
            t_wq = p.do("gpsimd", lambda e, S, h0=h0: e.dma_start(
                out=WQB, in_=dti["wq"].ap()[:, 64 * h0:64 * h0 + 256].rearrange("(a p) o -> p a o", p=128)), "gw", 16)

            kv_ready = []
            t_ktr = None
            t_qtr = None
            prev_q = None
            for i in range(16):
                p.waits("tensor", [t_wk, t_xnf] + ([kv_ready[i - 1][0]] if i else [t_pass_done, t_pass_sd]))
                for c8 in range(8):
                    p.do("tensor", lambda e, S, i=i, c8=c8: e.matmul(
                        psA(0, 256), xnf[:, c8, 128 * i:128 * (i + 1)], WKB[:, c8, :],
                        start=(c8 == 0), stop=(c8 == 7)))
                t_kt = p.mark("tensor", "td")
                p.waits("tensor", [t_wv] + ([kv_ready[i - 1][1]] if i else [t_pass_done, t_pass_sd]))
                for c8 in range(8):
                    p.do("tensor", lambda e, S, i=i, c8=c8: e.matmul(
                        psA(1, 256), xnf[:, c8, 128 * i:128 * (i + 1)], WVBc[:, c8, :],
                        start=(c8 == 0), stop=(c8 == 7)))
                t_vt = p.mark("tensor", "td")
                p.waits("tensor", [t_wq, prev_q] + ([] if i else [t_pass_done, t_pass_sd]))
                for c8 in range(8):
                    p.do("tensor", lambda e, S, i=i, c8=c8: e.matmul(
                        psA(2, 256), xnf[:, c8, 128 * i:128 * (i + 1)], WQB[:, c8, :],
                        start=(c8 == 0), stop=(c8 == 7)))
                t_qt = p.mark("tensor", "td")
                p.wait("vector", t_kt)
                p.do("vector", lambda e, S: e.tensor_copy(KNS[:, 0:256], psA(0, 256)))
                t_cp = p.mark("vector", "vd")
                p.wait("scalar", t_cp)
                p.do("scalar", lambda e, S: e.activation(KNS[:, 512:768], KNS[:, 0:256], AF.Square))
                t_sq = p.mark("scalar", "sd")
                p.wait("vector", t_sq)
                p.do("vector", lambda e, S: e.reduce_sum(
                    KST[:, 0:4], KNS[:, 512:768].rearrange("p (h d) -> p h d", d=64), axis=AX.X))
                p.bar("vector", "vd")
                p.do("vector", lambda e, S: e.tensor_scalar_max(KST[:, 0:4], KST[:, 0:4], 1e-24))
                t_ss = p.mark("vector", "vd")
                p.wait("scalar", t_ss)
                p.do("scalar", lambda e, S: e.activation(KST[:, 0:4], KST[:, 0:4], AF.Ln))
                p.bar("scalar", "sd")
                p.do("scalar", lambda e, S: e.activation(KST[:, 0:4], KST[:, 0:4], AF.Exp, scale=-0.5))
                t_inv = p.mark("scalar", "sd")
                p.wait("vector", t_inv)
                p.do("vector", lambda e, S, i=i: e.tensor_tensor(
                    kv[:, i, :, 0:64], KNS[:, 0:256].rearrange("p (h d) -> p h d", d=64),
                    KST[:, 0:4].unsqueeze(2).broadcast_to([128, 4, 64]), ALU.mult))
                t_k = p.mark("vector", "vd")
                p.waits("vector", [t_vt, t_abc])
                p.do("vector", lambda e, S, i=i, h0=h0: e.tensor_tensor(
                    kv[:, i, :, 64:128], psA(1, 256).rearrange("p (h d) -> p h d", d=64),
                    sb["abc"].ap()[:, H + h0:H + h0 + 4].unsqueeze(2).broadcast_to([128, 4, 64]), ALU.mult))
                t_v = p.mark("vector", "vd")
                kv_ready.append((t_k, t_v))
                p.waits("vector", [t_qt, t_qtr])
                p.do("vector", lambda e, S: e.tensor_copy(QTS[:, 0:256], psA(2, 256)))
                t_qs = p.mark("vector", "vd")
                prev_q = t_qs
                p.waits("tensor", [t_qs, t_k, t_ktr])
                for hh in range(QH):
                    p.do("tensor", lambda e, S, hh=hh: e.transpose(
                        ps[3].ap().bitcast(BF16)[0:64, 128 * hh:128 * (hh + 1)],
                        QTS[:, 64 * hh:64 * (hh + 1)], sb["ident"][:, :]))
                for hh in range(QH):
                    p.do("tensor", lambda e, S, i=i, hh=hh: e.transpose(
                        ps[3].ap().bitcast(BF16)[0:64, 512 + 128 * hh:512 + 128 * (hh + 1)],
                        kv[:, i, hh, 0:64], sb["ident"][:, :]))
                t_tr = p.mark("tensor", "td")
                p.wait("vector", t_tr)
                p.do("vector", lambda e, S, i=i: e.tensor_copy(
                    qfq[:, :, 128 * i:128 * (i + 1)],
                    ps[3].ap().bitcast(BF16)[0:64, 0:512].rearrange("p (a b) -> p a b", b=128)))
                p.bar("vector", "vd")
                p.do("vector", lambda e, S, i=i: e.tensor_copy(
                    kfq[:, :, 128 * i:128 * (i + 1)],
                    ps[3].ap().bitcast(BF16)[0:64, 512:1024].rearrange("p (a b) -> p a b", b=128)))
                t_qtr = p.mark("vector", "vd")
                t_ktr = t_qtr
            t_proj = t_qtr
            if stop_after == "proj" and q == 0:
                return t_proj

            SETB = 7680

            def sPWT(s, lvl):
                return bfv("RE", SETB * s + lvl * 512, 512, 128)

            def sPWU(s, pr2):
                return bfv("RE", SETB * s + (7 + pr2) * 512, 512, 128)

            def sRBUF(s, pr2):
                return bfv("RE", SETB * s + (9 + pr2) * 512, 512, 128)

            def sATB(s):
                return bfv("RE", SETB * s + 11 * 512, 512, 128)

            def sNBUF(s):
                return bfv("RE", SETB * s + 12 * 512, 512, 128)

            def sUFB(s):
                return sb["RE"].ap()[0:64, SETB * s + 13 * 512:SETB * s + 13 * 512 + 512].rearrange("p (a b) -> p a b", b=128)

            def sWBUF(s):
                return bfv("RE", SETB * s + 13 * 512 + 512, 256, 64)

            def psH(i, s, off=0, n=512):
                return ps[i].ap()[:, 512 * s + off:512 * s + off + n]

            GBFs = sb["GBF"].ap().rearrange("p (a b) -> p a b", b=64)

            free_tok = {}
            for i4 in range(4):
                for s2 in range(2):
                    free_tok[(i4, s2)] = t_proj
            t_gbf = None
            t_gf = None
            t_uf0 = None
            for cp in range(NCH // 2):
                cs = (2 * cp, 2 * cp + 1)
                t_s = [None, None]; t_kq = [None, None]; t_n = [None, None]
                t_at = [None, None]; t_rfin = [None, None]; t_uf = [None, None]
                rfin = [None, None]
                for s in range(2):
                    c = cs[s]
                    p.waits("tensor", [t_proj, free_tok[(0, s)]])
                    for hh in range(QH):
                        p.do("tensor", lambda e, S, hh=hh, c=c, s=s: e.matmul(
                            psH(0, s, 128 * hh, 128),
                            kfq[0:64, hh, 128 * c:128 * (c + 1)],
                            kfq[0:64, hh, 128 * c:128 * (c + 1)], start=True, stop=True))
                    t_s[s] = p.mark("tensor", "td")
                    p.waits("tensor", [free_tok[(1, s)]])
                    for hh in range(QH):
                        p.do("tensor", lambda e, S, hh=hh, c=c, s=s: e.matmul(
                            psH(1, s, 128 * hh, 128),
                            kfq[0:64, hh, 128 * c:128 * (c + 1)],
                            qfq[0:64, hh, 128 * c:128 * (c + 1)], start=True, stop=True))
                    t_kq[s] = p.mark("tensor", "td")
                for s in range(2):
                    p.waits("vector", [t_s[s], t_masks])
                    p.do("vector", lambda e, S, s=s: e.tensor_tensor(
                        sNBUF(s).rearrange("p a b -> p (a b)"), psH(0, s), MSL.rearrange("p a b -> p (a b)"), ALU.mult))
                    p.bar("vector", "vd")
                    p.do("vector", lambda e, S, s=s: e.tensor_tensor(
                        sPWT(s, 0).rearrange("p a b -> p (a b)"), psH(0, s), MSU.rearrange("p a b -> p (a b)"), ALU.mult))
                    t_n[s] = p.mark("vector", "vd")
                    free_tok[(0, s)] = t_n[s]
                    p.wait("vector", t_kq[s])
                    p.do("vector", lambda e, S, s=s: e.tensor_tensor(
                        sATB(s), psH(1, s).rearrange("p (a b) -> p a b", b=128),
                        sb["striu"].ap().unsqueeze(1).broadcast_to([128, 4, 128]), ALU.mult))
                    t_at[s] = p.mark("vector", "vd")
                    free_tok[(1, s)] = t_at[s]

                cur_u = [sNBUF(0), sNBUF(1)]
                cur_t = [sPWT(0, 0), sPWT(1, 0)]
                t_cu = [t_n[0], t_n[1]]
                t_ctl = [t_n[0], t_n[1]]
                for lvl in range(1, 7):
                    for s in range(2):
                        t_mu_s = None
                        if lvl < 6:
                            p.waits("tensor", [t_cu[s], t_ctl[s], free_tok[(2, s)]])
                            for hh in range(QH):
                                p.do("tensor", lambda e, S, hh=hh, s=s, ct=cur_t[s], cu=cur_u[s]: e.matmul(
                                    psH(2, s, 128 * hh, 128), ct[:, hh, :], cu[:, hh, :],
                                    start=True, stop=True))
                            t_mu_s = p.mark("tensor", "td")
                        p.waits("tensor", [t_cu[s], t_ctl[s], free_tok[(3, s)]])
                        for hh in range(QH):
                            p.do("tensor", lambda e, S, hh=hh, s=s, ct=cur_t[s], cu=cur_u[s]: e.matmul(
                                psH(3, s, 128 * hh, 128), cu[:, hh, :], ct[:, hh, :],
                                start=True, stop=True))
                        t_mt_s = p.mark("tensor", "td")
                        if lvl < 6:
                            nxt_u = sPWU(s, lvl % 2)
                            p.wait("scalar", t_mu_s)
                            p.do("scalar", lambda e, S, s=s, nxt_u=nxt_u: e.copy(
                                nxt_u.rearrange("p a b -> p (a b)"), psH(2, s)))
                            t_cu[s] = p.mark("scalar", "sd")
                            free_tok[(2, s)] = t_cu[s]
                            cur_u[s] = nxt_u
                        p.wait("scalar", t_mt_s)
                        p.do("scalar", lambda e, S, s=s, lvl=lvl: e.copy(
                            sPWT(s, lvl).rearrange("p a b -> p (a b)"), psH(3, s)))
                        t_ctl[s] = p.mark("scalar", "sd")
                        free_tok[(3, s)] = t_ctl[s]
                        cur_t[s] = sPWT(s, lvl)

                r_prev = [kv[:, cs[0], :, :], kv[:, cs[1], :, :]]
                r_tok = [kv_ready[cs[0]][1], kv_ready[cs[1]][1]]
                for j, lvl in enumerate([6, 5, 4, 3, 2, 1, 0]):
                    pr = j % 2
                    for s in range(2):
                        tgt = sRBUF(s, j % 2)
                        p.waits("tensor", [r_tok[s], t_ctl[s], free_tok[(pr, s)]])
                        for hh in range(QH):
                            p.do("tensor", lambda e, S, hh=hh, s=s, lvl=lvl, rp=r_prev[s], pr=pr: e.matmul(
                                psH(pr, s, 128 * hh, 128), sPWT(s, lvl)[:, hh, :], rp[:, hh, :],
                                start=True, stop=True))
                        t_mm = p.mark("tensor", "td")
                        p.wait("vector", t_mm)
                        if lvl > 0:
                            p.do("vector", lambda e, S, s=s, tgt=tgt, rp=r_prev[s], pr=pr: e.tensor_tensor(
                                tgt.rearrange("p a b -> p (a b)"), psH(pr, s),
                                rp.rearrange("p a b -> p (a b)"), ALU.add))
                        else:
                            p.do("vector", lambda e, S, s=s, tgt=tgt, rp=r_prev[s], pr=pr: e.tensor_tensor(
                                tgt.rearrange("p a b -> p (a b)"),
                                rp.rearrange("p a b -> p (a b)"), psH(pr, s), ALU.subtract))
                        r_tok[s] = p.mark("vector", "vd")
                        free_tok[(pr, s)] = r_tok[s]
                        r_prev[s] = tgt
                for s in range(2):
                    rfin[s], t_rfin[s] = r_prev[s], r_tok[s]
                    p.waits("tensor", [t_rfin[s], free_tok[(2, s)]])
                    for hh in range(QH):
                        p.do("tensor", lambda e, S, hh=hh, s=s, rf=rfin[s]: e.transpose(
                            ps[2].ap().bitcast(BF16)[0:64, 1024 * s + 128 * hh:1024 * s + 128 * (hh + 1)],
                            rf[:, hh, 0:64], sb["ident"][:, :]))
                    t_ut = p.mark("tensor", "td")
                    p.waits("scalar", [t_ut, t_abc])
                    for hh in range(QH):
                        h = h0 + hh
                        p.do("scalar", lambda e, S, hh=hh, h=h, s=s: e.activation(
                            sUFB(s)[:, hh, :], ps[2].ap().bitcast(BF16)[0:64, 1024 * s + 128 * hh:1024 * s + 128 * (hh + 1)],
                            AF.Identity, scale=sb["abc"].ap()[0:64, h:h + 1]))
                    t_uf[s] = p.mark("scalar", "sd")
                    free_tok[(2, s)] = t_uf[s]
                t_uf0 = t_uf[1]

                for s in range(2):
                    c = cs[s]
                    if c == 0:
                        w_view, t_w = rfin[s][:, :, 64:128], t_rfin[s]
                    else:
                        p.waits("tensor", [t_uf[s], t_gbf, free_tok[(3, s)]])
                        for hh in range(QH):
                            p.do("tensor", lambda e, S, hh=hh, s=s: e.matmul(
                                psH(3, s, 64 * hh, 64),
                                sUFB(s)[0:64, hh, :], GBFs[0:64, hh, :], start=True, stop=True))
                        t_wmm = p.mark("tensor", "td")
                        p.waits("vector", [t_wmm, t_rfin[s]])
                        p.do("vector", lambda e, S, s=s, rf=rfin[s]: e.tensor_tensor(
                            sWBUF(s), rf[:, :, 64:128],
                            psH(3, s, 0, 256).rearrange("p (a b) -> p a b", b=64), ALU.subtract))
                        t_w = p.mark("vector", "vd")
                        free_tok[(3, s)] = t_w
                        w_view = sWBUF(s)

                    p.waits("tensor", [t_w, t_at[s], free_tok[(0, s)]] + ([t_gbf] if c > 0 else []))
                    for hh in range(QH):
                        if c > 0:
                            p.do("tensor", lambda e, S, hh=hh, c=c, s=s: e.matmul(
                                psH(0, s, 64 * hh, 64),
                                qfq[0:64, hh, 128 * c:128 * (c + 1)], GBFs[0:64, hh, :],
                                start=True, stop=False))
                        p.do("tensor", lambda e, S, hh=hh, c=c, s=s, w_view=w_view: e.matmul(
                            psH(0, s, 64 * hh, 64),
                            sATB(s)[:, hh, :], w_view[:, hh, :], start=(c == 0), stop=True))
                    t_y = p.mark("tensor", "td")
                    p.wait("vector", t_y)
                    p.do("vector", lambda e, S, c=c, q=q, s=s: e.tensor_copy(
                        yT[:, c, 256 * q:256 * (q + 1)], psH(0, s, 0, 256)))
                    t_yev = p.mark("vector", "vd")
                    free_tok[(0, s)] = t_yev

                    gsrc = GA if c % 2 == 0 else GB
                    gdst = GB if c % 2 == 0 else GA
                    p.waits("tensor", [t_w, free_tok[(1, s)]])
                    for hh in range(QH):
                        p.do("tensor", lambda e, S, hh=hh, c=c, s=s, w_view=w_view: e.matmul(
                            ps[1].ap()[0:64, 512 * s + 64 * hh:512 * s + 64 * (hh + 1)],
                            kv[:, c, hh, 0:64], w_view[:, hh, :], start=True, stop=True))
                    t_dg = p.mark("tensor", "td")
                    p.wait("vector", t_dg)
                    if c == 0:
                        p.do("vector", lambda e, S, gdst=gdst, s=s: e.tensor_copy(
                            gdst, ps[1].ap()[0:64, 512 * s:512 * s + 256]))
                    else:
                        p.do("vector", lambda e, S, gsrc=gsrc, gdst=gdst, s=s: e.tensor_tensor(
                            gdst, ps[1].ap()[0:64, 512 * s:512 * s + 256], gsrc, ALU.add))
                    t_gf = p.mark("vector", "vd")
                    free_tok[(1, s)] = t_gf
                    if c < NCH - 1:
                        p.wait("vector", t_gf)
                        p.do("vector", lambda e, S, gdst=gdst: e.tensor_copy(
                            sb["GBF"][:, :], gdst))
                        t_gbf = p.mark("vector", "vd")
            t_pass_done = t_gf
            t_pass_sd = t_uf0
            if stop_after == "chunks" and q == 0:
                return t_pass_done

        yF = bfv("RA", 0, 16384, 2048)
        t_yf = None
        for i in range(16):
            p.waits("tensor", [t_pass_done, t_pass_sd, t_yf])
            for c8 in range(8):
                p.do("tensor", lambda e, S, i=i, c8=c8: e.transpose(
                    psbf(3, 128 * c8, 128), yT[:, i, 128 * c8:128 * (c8 + 1)], sb["ident"][:, :]))
            t_tr = p.mark("tensor", "td")
            p.wait("vector", t_tr)
            p.do("vector", lambda e, S, i=i: e.tensor_copy(
                yF[:, :, 128 * i:128 * (i + 1)], psbf(3, 0, 1024).rearrange("p (a b) -> p a b", b=128)))
            t_yf = p.mark("vector", "vd")

        p.waits("gpsimd", [t_pass_done])
        t_wp = p.do("gpsimd", lambda e, S: e.dma_start(
            out=bfv("RF", 0, 4096, 512),
            in_=dti["wproj"].ap()[:, 0:512].rearrange("(a p) o -> p a o", p=128)), "gw", 16)
        t_wp2 = p.do("gpsimd", lambda e, S: e.dma_start(
            out=bfv("RF", 4096, 4096, 512),
            in_=dti["wproj"].ap()[:, 512:1024].rearrange("(a p) o -> p a o", p=128)), "gw", 16)
        sT = bfv("RB", 0, 16384, 1024)
        t_sev = None
        free_sp = [None, None]
        for i in range(16):
            for oh in range(2):
                wb = bfv("RF", 4096 * oh, 4096, 512)
                twb = t_wp if oh == 0 else t_wp2
                spsl = ps[0].ap()[:, 512 * oh:512 * (oh + 1)]
                p.waits("tensor", [twb, t_yf, free_sp[oh]])
                for c8 in range(8):
                    p.do("tensor", lambda e, S, i=i, c8=c8, wb=wb, spsl=spsl: e.matmul(
                        spsl, yF[:, c8, 128 * i:128 * (i + 1)], wb[:, c8, :],
                        start=(c8 == 0), stop=(c8 == 7)))
                t_mm = p.mark("tensor", "td")
                p.wait("vector", t_mm)
                p.do("vector", lambda e, S, i=i, oh=oh, spsl=spsl: e.tensor_copy(
                    sT[:, i, 512 * oh:512 * (oh + 1)], spsl))
                t_sev = p.mark("vector", "vd")
                free_sp[oh] = t_sev
        p.wait("gpsimd", t_sev)
        t_ccin = p.do("gpsimd", lambda e, S: e.dma_start(
            out=cc_in.ap().rearrange("(a p) c -> p a c", p=128), in_=sT), "gw", 16)
        if no_cc:
            t_cc = t_ccin
        else:
            p.wait("gpsimd", t_ccin)
            t_cc = p.do("gpsimd", lambda e, S: e.collective_compute(
                "AllGather", ALU.bypass, replica_groups=[list(range(8))],
                ins=[cc_in.ap()], outs=[cc_out.ap()]), "cc", 1)
        tp = tapout("sT", (128, 16384), BF16)
        if tp is not None:
            p.wait("sync", t_sev)
            p.do("sync", lambda e, S, tp=tp: e.dma_start(out=tp.ap(), in_=sb["RB"][:, :]), "syt", 16)
        return t_cc, t_sev


    def ffn(t_cc, t_sev0):
        xnf2 = bfv("RA", 0, 16384, 2048)
        ctxT = bfv("RB", 0, 16384, 1024)
        vS = bfv("RC", 0, 4096, 256)
        kfq = sb["KFQ"].ap()[0:64, :].rearrange("p (a b) -> p a b", b=2048)
        qfq = sb["QFQ"].ap()[0:64, :].rearrange("p (a b) -> p a b", b=2048)
        XS2 = sb["RF"].ap().bitcast(F32)[:, 2048:4096].rearrange("p (a b) -> p a b", b=1024)
        XN2 = bfv("RF", 0, 2048, 1024)
        WKB2 = bfv("RF", 2048, 2048, 256)
        WQB2 = bfv("RF", 4096, 2048, 256)
        WVB2 = bfv("RF", 6144, 2048, 256)
        QT2 = bfv("RG", 2048, 1024)

        prev_slot = [None, None]
        t_ev = None
        for i in range(16):
            par = i % 2
            p.waits("sync", [prev_slot[par], t_sev0])
            t_load = p.do("sync", lambda e, S, i=i, par=par: e.dma_start(
                out=XS2[:, par, :], in_=dti["xs"].ap()[128 * i:128 * (i + 1), :]),
                "sy0" if par == 0 else "sy1", 16)
            tln = ln_tile(XS2[:, par, :], XN2[:, par, :], [t_load, prev_slot[par]], par=par)
            p.waits("tensor", [tln, t_ev])
            for c8 in range(8):
                p.do("tensor", lambda e, S, par=par, c8=c8: e.transpose(
                    psbf(3, 128 * c8, 128), XN2[:, par, 128 * c8:128 * (c8 + 1)], sb["ident"][:, :]))
            t_tr = p.mark("tensor", "td")
            p.wait("vector", t_tr)
            p.do("vector", lambda e, S, i=i: e.tensor_copy(
                xnf2[:, :, 128 * i:128 * (i + 1)], psbf(3, 0, 1024).rearrange("p (a b) -> p a b", b=128)))
            t_ev = p.mark("vector", "vd")
            prev_slot[par] = t_ev
        t_xnf2 = t_ev

        rb_gate = ("gw", p.cnt["gw"])
        t_grp = None
        t_ctxe = None
        for g4 in range(4):
            p.waits("gpsimd", [t_xnf2, t_grp])
            t_wq = p.do("gpsimd", lambda e, S, g4=g4: e.dma_start(
                out=WQB2, in_=dti["win"].ap()[:, 256 * g4:256 * g4 + 256].rearrange("(a p) o -> p a o", p=128)), "gw", 16)
            t_wk = p.do("gpsimd", lambda e, S, g4=g4: e.dma_start(
                out=WKB2, in_=dti["win"].ap()[:, CD + 256 * g4:CD + 256 * g4 + 256].rearrange("(a p) o -> p a o", p=128)), "gw", 16)
            t_wv = p.do("gpsimd", lambda e, S, g4=g4: e.dma_start(
                out=WVB2, in_=dti["win"].ap()[:, 2 * CD + 256 * g4:2 * CD + 256 * g4 + 256].rearrange("(a p) o -> p a o", p=128)), "gw", 16)
            free_q = [t_grp, t_grp]
            free_k = [t_grp, t_grp]
            free_v = [t_grp, t_grp]
            free_qt = [t_grp, t_grp]
            free_tr = [t_grp, t_grp]
            st_qs = [None] * 16
            st_ks = [None] * 16
            for ii in range(17):
                if ii < 16:
                    i = ii
                    par = i % 2
                    qsl = ps[0].ap()[:, 512 * par:512 * par + 256]
                    ksl = ps[1].ap()[:, 512 * par:512 * par + 256]
                    vsl = ps[2].ap()[:, 512 * par:512 * par + 256]
                    p.waits("tensor", [t_wq, t_xnf2, free_q[par]])
                    for c8 in range(8):
                        p.do("tensor", lambda e, S, i=i, c8=c8, qsl=qsl: e.matmul(
                            qsl, xnf2[:, c8, 128 * i:128 * (i + 1)], WQB2[:, c8, :],
                            start=(c8 == 0), stop=False))
                    p.do("tensor", lambda e, S, g4=g4, qsl=qsl: e.matmul(
                        qsl, sb["ones"][0:1, 0:128], sb["brows"].ap()[0:1, 256 * g4:256 * g4 + 256],
                        start=False, stop=True))
                    t_qmm = p.mark("tensor", "td")
                    p.waits("tensor", [t_wk, free_k[par]])
                    for c8 in range(8):
                        p.do("tensor", lambda e, S, i=i, c8=c8, ksl=ksl: e.matmul(
                            ksl, xnf2[:, c8, 128 * i:128 * (i + 1)], WKB2[:, c8, :],
                            start=(c8 == 0), stop=False))
                    p.do("tensor", lambda e, S, g4=g4, ksl=ksl: e.matmul(
                        ksl, sb["ones"][0:1, 0:128], sb["brows"].ap()[0:1, CD + 256 * g4:CD + 256 * g4 + 256],
                        start=False, stop=True))
                    t_kmm = p.mark("tensor", "td")
                    p.waits("tensor", [t_wv, free_v[par]])
                    for c8 in range(8):
                        p.do("tensor", lambda e, S, i=i, c8=c8, vsl=vsl: e.matmul(
                            vsl, xnf2[:, c8, 128 * i:128 * (i + 1)], WVB2[:, c8, :],
                            start=(c8 == 0), stop=False))
                    p.do("tensor", lambda e, S, g4=g4, vsl=vsl: e.matmul(
                        vsl, sb["ones"][0:1, 0:128], sb["brows"].ap()[0:1, 2 * CD + 256 * g4:2 * CD + 256 * g4 + 256],
                        start=False, stop=True))
                    t_vmm = p.mark("tensor", "td")
                    p.waits("vector", [t_vmm])
                    p.do("vector", lambda e, S, i=i, vsl=vsl: e.tensor_copy(vS[:, i, :], vsl))
                    t_vev = p.mark("vector", "vd")
                    free_v[par] = t_vev
                    p.waits("vector", [t_qmm, free_qt[par]])
                    p.do("vector", lambda e, S, par=par, qsl=qsl: e.tensor_copy(
                        QT2[:, 512 * par:512 * par + 256], qsl))
                    t_qs = p.mark("vector", "vd")
                    free_q[par] = t_qs
                    st_qs[i] = t_qs
                    p.waits("vector", [t_kmm])
                    p.do("vector", lambda e, S, par=par, ksl=ksl: e.tensor_copy(
                        QT2[:, 512 * par + 256:512 * par + 512], ksl))
                    t_ks = p.mark("vector", "vd")
                    free_k[par] = t_ks
                    st_ks[i] = t_ks
                if ii >= 1:
                    i = ii - 1
                    par = i % 2
                    p.waits("tensor", [st_qs[i], st_ks[i], free_tr[par]])
                    for hh in range(QH):
                        p.do("tensor", lambda e, S, hh=hh, par=par: e.transpose(
                            ps[3].ap().bitcast(BF16)[0:64, 1024 * par + 128 * hh:1024 * par + 128 * (hh + 1)],
                            QT2[:, 512 * par + 64 * hh:512 * par + 64 * (hh + 1)], sb["ident"][:, :]))
                    for hh in range(QH):
                        p.do("tensor", lambda e, S, hh=hh, par=par: e.transpose(
                            ps[3].ap().bitcast(BF16)[0:64, 1024 * par + 512 + 128 * hh:1024 * par + 512 + 128 * (hh + 1)],
                            QT2[:, 512 * par + 256 + 64 * hh:512 * par + 256 + 64 * (hh + 1)], sb["ident"][:, :]))
                    t_tr = p.mark("tensor", "td")
                    free_qt[par] = t_tr
                    p.waits("vector", [t_tr])
                    p.do("vector", lambda e, S, i=i, par=par: e.tensor_copy(
                        qfq[:, :, 128 * i:128 * (i + 1)],
                        ps[3].ap().bitcast(BF16)[0:64, 1024 * par:1024 * par + 512].rearrange("p (a b) -> p a b", b=128)))
                    p.bar("vector", "vd")
                    p.do("vector", lambda e, S, i=i, par=par: e.tensor_copy(
                        kfq[:, :, 128 * i:128 * (i + 1)],
                        ps[3].ap().bitcast(BF16)[0:64, 1024 * par + 512:1024 * par + 1024].rearrange("p (a b) -> p a b", b=128)))
                    t_tre = p.mark("vector", "vd")
                    free_tr[par] = t_tre
            t_inproj = t_tre

            free_sc = [t_inproj, t_inproj]
            free1 = [t_inproj, t_inproj]
            free3 = [t_inproj, t_inproj]
            st_ex = [None] * 16
            for jj in range(17):
                if jj < 16:
                    hh, pg = jj // 4, jj % 4
                    par = jj % 2
                    sbase = 40 + 16 * par
                    pss = ps[par].ap()
                    ebase = 1536 * par
                    p.waits("tensor", [t_inproj, free_sc[par]])
                    for pt4 in range(4):
                        pt = 4 * pg + pt4
                        p.do("tensor", lambda e, S, hh=hh, pt=pt, pt4=pt4, pss=pss: e.matmul(
                            pss[:, 128 * pt4:128 * (pt4 + 1)],
                            qfq[0:64, hh, 128 * pt:128 * (pt + 1)],
                            kfq[0:64, hh, 128 * pt:128 * (pt + 1)], start=True, stop=True))
                    t_sc = p.mark("tensor", "td")
                    p.waits("vector", [t_sc])
                    p.do("vector", lambda e, S, pss=pss, sbase=sbase: e.reduce_max(
                        sb["STK"].ap()[:, sbase:sbase + 4],
                        pss[:, 0:512].rearrange("p (a b) -> p a b", b=128), axis=AX.X))
                    p.bar("vector", "vd")
                    p.do("vector", lambda e, S, sbase=sbase: e.tensor_scalar_mul(
                        sb["STK"].ap()[:, sbase + 4:sbase + 8], sb["STK"].ap()[:, sbase:sbase + 4], -0.125))
                    t_mx = p.mark("vector", "vd")
                    p.waits("scalar", [t_mx])
                    for pt4 in range(4):
                        p.do("scalar", lambda e, S, pt4=pt4, pss=pss, ebase=ebase, sbase=sbase: e.activation(
                            bfv("RE", ebase + 128 * pt4, 128),
                            pss[:, 128 * pt4:128 * (pt4 + 1)], AF.Exp,
                            scale=0.125, bias=sb["STK"].ap()[:, sbase + 4 + pt4:sbase + 5 + pt4]))
                    t_ex = p.mark("scalar", "sd")
                    st_ex[jj] = t_ex
                    free_sc[par] = t_ex
                if jj >= 1:
                    j = jj - 1
                    hh, pg = j // 4, j % 4
                    h = 4 * g4 + hh
                    par = j % 2
                    sbase = 40 + 16 * par
                    ebase = 1536 * par
                    p.waits("vector", [st_ex[j]])
                    p.do("vector", lambda e, S, ebase=ebase: e.tensor_tensor(
                        bfv("RE", ebase + 512, 512, 128), bfv("RE", ebase, 512, 128),
                        sb["blk8"].ap().unsqueeze(1).broadcast_to([128, 4, 128]), ALU.mult))
                    p.bar("vector", "vd")
                    p.do("vector", lambda e, S, ebase=ebase, sbase=sbase: e.reduce_sum(
                        sb["STK"].ap()[:, sbase + 8:sbase + 12], bfv("RE", ebase + 512, 512, 128), axis=AX.X))
                    p.bar("vector", "vd")
                    p.do("vector", lambda e, S, sbase=sbase: e.reciprocal(
                        sb["STK"].ap()[:, sbase + 12:sbase + 16], sb["STK"].ap()[:, sbase + 8:sbase + 12]))
                    p.bar("vector", "vd")
                    p.do("vector", lambda e, S, ebase=ebase, sbase=sbase: e.tensor_tensor(
                        bfv("RE", ebase, 512, 128), bfv("RE", ebase + 512, 512, 128),
                        sb["STK"].ap()[:, sbase + 12:sbase + 16].unsqueeze(2).broadcast_to([128, 4, 128]), ALU.mult))
                    t_pn = p.mark("vector", "vd")
                    p.waits("tensor", [t_pn, free3[par]])
                    for pt4 in range(4):
                        p.do("tensor", lambda e, S, pt4=pt4, par=par, ebase=ebase: e.transpose(
                            ps[3].ap().bitcast(BF16)[:, 1024 * par + 128 * pt4:1024 * par + 128 * (pt4 + 1)],
                            bfv("RE", ebase + 128 * pt4, 128), sb["ident"][:, :]))
                    t_ptr = p.mark("tensor", "td")
                    p.waits("vector", [t_ptr])
                    p.do("vector", lambda e, S, par=par, ebase=ebase: e.tensor_copy(
                        bfv("RE", ebase + 1024, 512, 128),
                        ps[3].ap().bitcast(BF16)[:, 1024 * par:1024 * par + 512].rearrange("p (a b) -> p a b", b=128)))
                    t_pte = p.mark("vector", "vd")
                    free3[par] = t_pte
                    p.waits("tensor", [t_pte, free1[par]])
                    for pt4 in range(4):
                        pt = 4 * pg + pt4
                        p.do("tensor", lambda e, S, hh=hh, pt=pt, pt4=pt4, par=par, ebase=ebase: e.matmul(
                            ps[2].ap()[:, 512 * par + 64 * pt4:512 * par + 64 * (pt4 + 1)],
                            bfv("RE", ebase + 1024 + 128 * pt4, 128),
                            vS[:, pt, 64 * hh:64 * (hh + 1)], start=True, stop=True))
                    t_cx = p.mark("tensor", "td")
                    p.waits("vector", [t_cx, rb_gate])
                    p.do("vector", lambda e, S, h=h, pg=pg, par=par: e.tensor_copy(
                        ctxT[:, 4 * pg:4 * (pg + 1), 64 * h:64 * (h + 1)],
                        ps[2].ap()[:, 512 * par:512 * par + 256].rearrange("p (a b) -> p a b", b=64)))
                    t_ctxe = p.mark("vector", "vd")
                    free1[par] = t_ctxe
            t_grp = t_ctxe

        tpc = tapout("ctx", (128, 16384), BF16)
        t_tapc = None
        if tpc is not None:
            p.wait("sync", t_grp)
            t_tapc = p.do("sync", lambda e, S, tpc=tpc: e.dma_start(out=tpc.ap(), in_=sb["RB"][:, :]), "syt", 16)
        ctxF = bfv("RA", 0, 16384, 2048)
        t_cf = None
        for i in range(16):
            p.waits("tensor", [t_grp, t_cf])
            for c8 in range(8):
                p.do("tensor", lambda e, S, i=i, c8=c8: e.transpose(
                    psbf(3, 128 * c8, 128), ctxT[:, i, 128 * c8:128 * (c8 + 1)], sb["ident"][:, :]))
            t_tr = p.mark("tensor", "td")
            p.wait("vector", t_tr)
            p.do("vector", lambda e, S, i=i: e.tensor_copy(
                ctxF[:, :, 128 * i:128 * (i + 1)], psbf(3, 0, 1024).rearrange("p (a b) -> p a b", b=128)))
            t_cf = p.mark("vector", "vd")

        p.waits("gpsimd", [t_cf])
        t_wo1 = p.do("gpsimd", lambda e, S: e.dma_start(
            out=bfv("RF", 0, 4096, 512),
            in_=dti["wout"].ap()[:, 0:512].rearrange("(a p) o -> p a o", p=128)), "gw", 16)
        t_wo2 = p.do("gpsimd", lambda e, S: e.dma_start(
            out=bfv("RE", 0, 4096, 512),
            in_=dti["wout"].ap()[:, 512:1024].rearrange("(a p) o -> p a o", p=128)), "gw", 16)
        hT = ctxT
        t_hev = t_tapc
        t_xld = None
        xslotA = sb["RF"].ap().bitcast(F32)[:, 3072:4096]
        xslotB = KNS
        zbf = bfv("RG", 2048, 1024)
        for i in range(16):
            xs_slot = xslotA if i % 2 == 0 else xslotB
            p.waits("sync", [t_hev, t_cf])
            t_x = p.do("sync", lambda e, S, i=i, xs_slot=xs_slot: e.dma_start(
                out=xs_slot, in_=dti["xs"].ap()[128 * i:128 * (i + 1), :]),
                "sy0" if i % 2 == 0 else "sy1", 16)
            for oh in range(2):
                wb = bfv("RF", 0, 4096, 512) if oh == 0 else bfv("RE", 0, 4096, 512)
                twb = t_wo1 if oh == 0 else t_wo2
                p.waits("tensor", [twb, t_cf] + ([t_hev] if i or oh else []))
                for c8 in range(8):
                    p.do("tensor", lambda e, S, i=i, c8=c8, wb=wb, oh=oh: e.matmul(
                        ps[2].ap()[:, 512 * oh:512 * (oh + 1)], ctxF[:, c8, 128 * i:128 * (i + 1)], wb[:, c8, :],
                        start=(c8 == 0), stop=False))
                p.do("tensor", lambda e, S, oh=oh: e.matmul(
                    ps[2].ap()[:, 512 * oh:512 * (oh + 1)], sb["ones"][0:1, 0:128],
                    sb["brows"].ap()[0:1, 3 * CD + 512 * oh:3 * CD + 512 * (oh + 1)],
                    start=False, stop=True))
            t_om = p.mark("tensor", "td")
            p.waits("vector", [t_om, t_x])
            p.do("vector", lambda e, S, xs_slot=xs_slot: e.tensor_tensor(
                xs_slot, ps[2].ap()[:, 0:1024], xs_slot, ALU.add))
            t_hp = p.mark("vector", "vd")
            tln = ln_tile(xs_slot, zbf, [t_hp], par=i % 2,
                          gain=sb["gbc"][:, :], bias=sb["hbc"][:, :], out2=hT[:, i, :])
            t_hev = tln
        t_h = t_hev

        tph = tapout("hT", (128, 16384), BF16)
        t_taph = None
        if tph is not None:
            p.wait("sync", t_h)
            t_taph = p.do("sync", lambda e, S, tph=tph: e.dma_start(out=tph.ap(), in_=sb["RB"][:, :]), "syt", 16)
        hF = bfv("RA", 0, 16384, 2048)
        t_hf = None
        for i in range(16):
            p.waits("tensor", [t_h, t_hf])
            for c8 in range(8):
                p.do("tensor", lambda e, S, i=i, c8=c8: e.transpose(
                    psbf(3, 128 * c8, 128), hT[:, i, 128 * c8:128 * (c8 + 1)], sb["ident"][:, :]))
            t_tr = p.mark("tensor", "td")
            p.wait("vector", t_tr)
            p.do("vector", lambda e, S, i=i: e.tensor_copy(
                hF[:, :, 128 * i:128 * (i + 1)], psbf(3, 0, 1024).rearrange("p (a b) -> p a b", b=128)))
            t_hf = p.mark("vector", "vd")

        p.waits("tensor", [t_hf])
        for i in range(16):
            for c8 in range(8):
                p.do("tensor", lambda e, S, i=i, c8=c8: e.matmul(
                    ps[2].ap()[:, 8 * i:8 * (i + 1)], hF[:, c8, 128 * i:128 * (i + 1)],
                    sb["WRB"].ap().rearrange("p (a b) -> p a b", b=8)[:, c8, :],
                    start=(c8 == 0), stop=False))
            p.do("tensor", lambda e, S, i=i: e.matmul(
                ps[2].ap()[:, 8 * i:8 * (i + 1)], sb["ones"][0:1, 0:128],
                sb["brows"].ap()[0:1, 4 * CD:4 * CD + E], start=False, stop=True))
        t_rt = p.mark("tensor", "td")
        p.wait("vector", t_rt)
        LG = sb["STK"].ap()[:, 96:224]
        p.do("vector", lambda e, S: e.tensor_copy(LG, ps[2].ap()[:, 0:128]))
        p.bar("vector", "vd")
        p.do("vector", lambda e, S: e.reduce_max(
            sb["STK"].ap()[:, 40:56], LG.rearrange("p (a b) -> p a b", b=8), axis=AX.X))
        t_m1 = p.mark("vector", "vd")
        p.wait("vector", t_m1)
        CAND = sb["STK"].ap()[:, 224:352]
        for i in range(16):
            p.do("vector", lambda e, S, i=i: e.tensor_scalar(
                CAND[:, 8 * i:8 * (i + 1)], LG[:, 8 * i:8 * (i + 1)],
                sb["STK"].ap()[:, 40 + i:41 + i], None, ALU.is_ge))
        p.bar("vector", "vd")
        p.do("vector", lambda e, S: e.scalar_tensor_tensor(
            CAND, CAND, -1e30, LG, ALU.mult, ALU.add))
        p.bar("vector", "vd")
        p.do("vector", lambda e, S: e.reduce_max(
            sb["STK"].ap()[:, 56:72], CAND.rearrange("p (a b) -> p a b", b=8), axis=AX.X))
        t_m2 = p.mark("vector", "vd")
        p.wait("vector", t_m2)
        for i in range(16):
            p.do("vector", lambda e, S, i=i: e.tensor_scalar(
                CAND[:, 8 * i:8 * (i + 1)], LG[:, 8 * i:8 * (i + 1)],
                sb["STK"].ap()[:, 56 + i:57 + i], None, ALU.is_ge))
        p.bar("vector", "vd")
        p.do("vector", lambda e, S: e.tensor_tensor(
            CAND.rearrange("p (a b) -> p a b", b=8), CAND.rearrange("p (a b) -> p a b", b=8),
            sb["iota"].ap().unsqueeze(1).broadcast_to([128, 16, 8]), ALU.mult))
        p.bar("vector", "vd")
        p.do("vector", lambda e, S: e.reduce_max(
            sb["STK"].ap()[:, 72:88], CAND.rearrange("p (a b) -> p a b", b=8), axis=AX.X))
        t_asn = p.mark("vector", "vd")
        p.wait("vector", t_asn)
        OHT = bfv("RG", 3072, 128)
        for i in range(16):
            p.do("vector", lambda e, S, i=i: e.tensor_scalar(
                OHT[:, 8 * i:8 * (i + 1)], sb["iota"][:, :],
                sb["STK"].ap()[:, 72 + i:73 + i], None, ALU.is_equal))
        t_oht = p.mark("vector", "vd")
        t_ohf = None
        for ig in range(4):
            p.waits("tensor", [t_oht, t_ohf])
            for i4 in range(4):
                i = 4 * ig + i4
                p.do("tensor", lambda e, S, i=i, i4=i4: e.transpose(
                    ps[3].ap().bitcast(BF16)[0:8, 128 * i4:128 * (i4 + 1)],
                    OHT[:, 8 * i:8 * (i + 1)], sb["ident"][:, :]))
            t_tr = p.mark("tensor", "td")
            p.wait("vector", t_tr)
            p.do("vector", lambda e, S, ig=ig: e.tensor_copy(
                sb["OHF"].ap()[0:8, 512 * ig:512 * (ig + 1)],
                ps[3].ap().bitcast(BF16)[0:8, 0:512]))
            t_ohf = p.mark("vector", "vd")
        p.wait("gpsimd", t_ohf)
        t_ohd = p.do("gpsimd", lambda e, S: e.dma_start(out=oh_dram.ap(), in_=sb["OHF"][:, :]), "gs", 16)
        tpo = tapout("ohf", (E, TOK), BF16)
        if tpo is not None:
            p.wait("sync", t_ohf)
            p.do("sync", lambda e, S, tpo=tpo: e.dma_start(out=tpo.ap(), in_=sb["OHF"][:, :]), "syt", 16)

        wbufs = [(bfv("RC", 0, 8192, 1024), bfv("RE", 0, 8192, 1024)),
                 (sb["KFQ"].ap().rearrange("p (a b) -> p a b", b=1024),
                  sb["QFQ"].ap().rearrange("p (a b) -> p a b", b=1024))]
        heb = bfv("RF", 0, 2048, 256)
        hemb = bfv("RF", 2048, 2048, 256)
        mke = bfv("RG", 0, 2048)
        t_wload = [None, None]
        t_mload = [None, None]
        p.waits("gpsimd", [t_hf, t_ohd])
        t_w1 = p.do("gpsimd", lambda e, S: e.dma_start(
            out=wbufs[0][0], in_=dti["w1"].ap()[0].rearrange("(a p) o -> p a o", p=128)), "gw0", 16)
        t_w2 = p.do("gpsimd", lambda e, S: e.dma_start(
            out=wbufs[0][1], in_=dti["w2"].ap()[0].rearrange("(a p) o -> p a o", p=128)), "gw0", 16)
        t_wload[0] = t_w2
        t_mk = p.do("gpsimd", lambda e, S: e.dma_start(
            out=mke, in_=oh_dram.ap()[0:1, :].partition_broadcast(128).rearrange("p o e -> p (o e)")), "gm0", 16)
        t_mload[0] = t_mk
        mkev = [mke, bfv("RG", 2048, 2048)]
        t_heprev = None
        t_msk = None
        t_ev2 = {}
        t_mm2done = None
        for ex in range(E):
            par = ex % 2
            w1b, w2b = wbufs[par]
            if ex + 1 < E:
                npar = 1 - par
                p.waits("gpsimd", [t_wload[npar], t_mm2done, t_hf])
                t_nw1 = p.do("gpsimd", lambda e, S, ex=ex, npar=npar: e.dma_start(
                    out=wbufs[npar][0], in_=dti["w1"].ap()[ex + 1].rearrange("(a p) o -> p a o", p=128)), "gw0" if npar == 0 else "gw1", 16)
                t_nw2 = p.do("gpsimd", lambda e, S, ex=ex, npar=npar: e.dma_start(
                    out=wbufs[npar][1], in_=dti["w2"].ap()[ex + 1].rearrange("(a p) o -> p a o", p=128)), "gw0" if npar == 0 else "gw1", 16)
                t_wload[npar] = t_nw2
                p.waits("gpsimd", [t_mload[npar], t_msk])
                t_nmk = p.do("gpsimd", lambda e, S, ex=ex, npar=npar: e.dma_start(
                    out=mkev[npar], in_=oh_dram.ap()[ex + 1:ex + 2, :].partition_broadcast(128).rearrange("p o e -> p (o e)")), "gm0" if npar == 0 else "gm1", 16)
                t_mload[npar] = t_nmk
            t_gelh = {0: None, 1: None}
            for tc in range(8):
                for ohalf in range(2):
                    p.waits("tensor", [t_wload[par], t_hf, t_gelh[ohalf]])
                    for ot4 in range(4):
                        ot = 4 * ohalf + ot4
                        for c8 in range(8):
                            p.do("tensor", lambda e, S, ot=ot, c8=c8, tc=tc, w1b=w1b: e.matmul(
                                ps[ot // 4].ap()[:, 256 * (ot % 4):256 * (ot % 4 + 1)],
                                w1b[:, c8, 128 * ot:128 * (ot + 1)],
                                hF[:, c8, 256 * tc:256 * (tc + 1)],
                                start=(c8 == 0), stop=(c8 == 7)))
                    t_heh = p.mark("tensor", "td")
                    p.wait("scalar", t_heh)
                    for ot4 in range(4):
                        ot = 4 * ohalf + ot4
                        p.do("scalar", lambda e, S, ot=ot, ex=ex: e.activation(
                            heb[:, ot, :], ps[ot // 4].ap()[:, 256 * (ot % 4):256 * (ot % 4 + 1)],
                            AF.Gelu, bias=sb["b1c"].ap()[:, 8 * ex + ot:8 * ex + ot + 1]))
                    t_gelh[ohalf] = p.mark("scalar", "sd")
                t_heprev = t_gelh[1]
                p.waits("vector", [t_gelh[1], t_mload[par]])
                p.do("vector", lambda e, S, tc=tc, par=par: e.tensor_tensor(
                    hemb, heb,
                    mkev[par][:, 256 * tc:256 * (tc + 1)].unsqueeze(1).broadcast_to([128, 8, 256]),
                    ALU.mult))
                t_msk = p.mark("vector", "vd")
                for ts in range(2):
                    p.waits("tensor", [t_msk, t_wload[par], t_ev2.get((tc, ts))])
                    for ot in range(8):
                        for oh in range(2):
                            p.do("tensor", lambda e, S, ts=ts, oh=oh, ot=ot, w2b=w2b, ex=ex: e.matmul(
                                ps[2 + ts].ap()[:, 512 * oh:512 * (oh + 1)],
                                hemb[:, ot, 128 * ts:128 * (ts + 1)],
                                w2b[:, ot, 512 * oh:512 * (oh + 1)],
                                start=(ot == 0), stop=(ot == 7 and ex != E - 1)))
                    if ex == E - 1:
                        p.waits("tensor", [t_ohf])
                        for oh in range(2):
                            p.do("tensor", lambda e, S, ts=ts, oh=oh, tc=tc: e.matmul(
                                ps[2 + ts].ap()[:, 512 * oh:512 * (oh + 1)],
                                sb["OHF"].ap()[0:8, 256 * tc + 128 * ts:256 * tc + 128 * (ts + 1)],
                                sb["b2r"].ap()[0:8, 512 * oh:512 * (oh + 1)],
                                start=False, stop=True))
                    t_mm2 = p.mark("tensor", "td")
                    t_mm2done = t_mm2
                    p.waits("vector", [t_mm2, t_taph])
                    p.do("vector", lambda e, S, ts=ts, tc=tc: e.tensor_tensor(
                        hT[:, 2 * tc + ts, :], ps[2 + ts].ap()[:, 0:1024], hT[:, 2 * tc + ts, :], ALU.add))
                    t_ev2[(tc, ts)] = p.mark("vector", "vd")
        t_moe_ev = t_ev2[(7, 1)]

        tpz = tapout("z2", (128, 16384), BF16)
        t_tapz = None
        if tpz is not None:
            p.wait("sync", t_moe_ev)
            t_tapz = p.do("sync", lambda e, S, tpz=tpz: e.dma_start(out=tpz.ap(), in_=sb["RB"][:, :]), "syt", 16)
        p.waits("gpsimd", [t_moe_ev])
        t_g2 = p.do("gpsimd", lambda e, S: e.dma_start(
            out=sb["gbc"][:, :], in_=dti["g2"].ap().partition_broadcast(128).rearrange("p o e -> p (o e)")), "gg", 16)
        t_h2 = p.do("gpsimd", lambda e, S: e.dma_start(
            out=sb["hbc"][:, :], in_=dti["h2"].ap().partition_broadcast(128).rearrange("p o e -> p (o e)")), "gg", 16)
        t_ln2 = [None] * 16
        for i in range(16):
            tln = ln_tile(hT[:, i, :], zbf, [t_moe_ev, t_g2, t_h2, t_tapz], par=i % 2,
                          gain=sb["gbc"][:, :], bias=sb["hbc"][:, :], out2=hT[:, i, :])
            t_ln2[i] = tln
        pid_holder = {}
        def get_pid_off(e, S):
            if "off" not in pid_holder:
                pid = nc.partition_id(engines=[mybir.EngineType.Pool])
                pid_holder["off"] = e.snap(pid * PSL, donate=False, min_val=0, max_val=(B - 1) * PSL)
            return pid_holder["off"]
        t_st = [None, None]
        t_sg = [None, None]
        t_out = [None, None]
        t_sil = [None, None]
        gtile = [bfv("RF", 0, 1024), bfv("RF", 1024, 1024)]
        prodf = [sb["RF"].ap().bitcast(F32)[:, 1024:2048], sb["RF"].ap().bitcast(F32)[:, 2048:3072]]
        xcf = [sb["RF"].ap().bitcast(F32)[:, 3072:4096], sb["KFQ"].ap().bitcast(F32)[:, 0:1024]]
        stile = [bfv("RG", 0, 1024), bfv("RG", 4096 - 1024, 1024)]
        for i in range(16):
            pr = i % 2
            p.waits("gpsimd", [None if no_cc else t_cc, t_sg[pr], t_moe_ev, t_out[pr]])
            def sgather(e, S, i=i, pr=pr):
                off = get_pid_off(e, S)
                src = cc_out.ap().rearrange("(b t) c -> t b c", t=T)[16 * i:, :, :]
                return e.dma_start(out=stile[pr], in_=src[bass.ds(off, 16), :, :])
            t_sg[pr] = p.do("gpsimd", sgather, "gq0" if pr == 0 else "gq1", 16)
            p.waits("sync", [t_st[pr], t_out[pr]])
            t_x = p.do("sync", lambda e, S, i=i, pr=pr: e.dma_start(
                out=xcf[pr], in_=dti["xs"].ap()[128 * i:128 * (i + 1), :]),
                "sy0" if pr == 0 else "sy1", 16)
            p.waits("scalar", [t_ln2[i], t_out[pr]])
            p.do("scalar", lambda e, S, i=i, pr=pr: e.activation(gtile[pr], hT[:, i, :], AF.Silu))
            t_sil[pr] = p.mark("scalar", "sd")
            p.waits("vector", [t_sil[pr], t_sg[pr], t_x, t_st[pr]])
            p.do("vector", lambda e, S, pr=pr: e.tensor_tensor(prodf[pr], stile[pr], gtile[pr], ALU.mult))
            p.bar("vector", "vd")
            p.do("vector", lambda e, S, pr=pr: e.tensor_tensor(prodf[pr], prodf[pr], xcf[pr], ALU.add))
            t_out[pr] = p.mark("vector", "vd")
            p.wait("sync", t_out[pr])
            t_st[pr] = p.do("sync", lambda e, S, i=i, pr=pr: e.dma_start(
                out=dti["out"].ap()[128 * i:128 * (i + 1), :], in_=prodf[pr]), "syg", 16)
        p.waits("sync", [t_st[0], t_st[1]])
        p.waits("gpsimd", [t_st[0], t_st[1]])

    def ffn_stub(t_cc):
        p.wait("sync", t_cc)
        t_pr = None
        for i in range(16):
            t_l = p.do("sync", lambda e, S, i=i: e.dma_start(
                out=bfv("RF", 0, 1024),
                in_=(cc_in if no_cc else cc_out).ap()[128 * i:128 * (i + 1), :]), "syg", 16)
            p.waits("vector", [t_l, t_pr])
            p.do("vector", lambda e, S: e.tensor_copy(KNS, bfv("RF", 0, 1024)))
            t_c = p.mark("vector", "vd")
            p.wait("sync", t_c)
            t_pr = p.do("sync", lambda e, S, i=i: e.dma_start(
                out=dti["out"].ap()[128 * i:128 * (i + 1), :], in_=KNS), "syg", 16)
        p.wait("sync", t_pr)

    t_abc = setup()
    t_ret = titans(t_abc)
    t_cc, t_sev0 = t_ret if isinstance(t_ret, tuple) else (t_ret, t_ret)
    if stop_after is not None:
        p.wait("sync", t_cc)
        tz = p.do("sync", lambda e, S: e.dma_start(
            out=dti["out"].ap()[0:128, :], in_=KNS), "syg", 16)
        p.wait("sync", tz)
    elif not ffn:
        ffn_stub(t_cc)
    else:
        ffn(t_cc, t_sev0)
    emit(nc, p)
    return nc, dti, tap_set



def make_inmaps(inputs):
    x = np.asarray(inputs["L0_attn_x"])
    bin_ = np.asarray(inputs["L0_ffn_in_proj_b"])
    shared = dict(
        wq=np.asarray(inputs["L0_attn_Wq"]).T,
        wk=np.asarray(inputs["L0_attn_Wk"]).T,
        wv=np.asarray(inputs["L0_attn_Wv"]).T,
        wproj=np.asarray(inputs["L0_attn_Wproj"]).T,
        win=np.asarray(inputs["L0_ffn_in_proj_w"]).T,
        wout=np.asarray(inputs["L0_ffn_out_w"]).T,
        wr=np.asarray(inputs["L0_ffn_router_w"]).T,
        w1=np.asarray(inputs["L0_ffn_w1"]).transpose(0, 2, 1),
        w2=np.asarray(inputs["L0_ffn_w2"]).transpose(0, 2, 1),
        binrow=bin_.reshape(1, 3 * CD),
        ones1=np.ones((1, 512), np.float32),
        bout=np.asarray(inputs["L0_ffn_out_b"]).reshape(1, CD),
        brout=np.asarray(inputs["L0_ffn_router_b"]).reshape(1, E),
        b1c=np.asarray(inputs["L0_ffn_b1"]).reshape(E * 8, 128).T,
        b2r=np.asarray(inputs["L0_ffn_b2"]),
        g1=np.asarray(inputs["L0_ffn_ln1_g"]).reshape(1, CD),
        h1=np.asarray(inputs["L0_ffn_ln1_b"]).reshape(1, CD),
        g2=np.asarray(inputs["L0_ffn_ln2_g"]).reshape(1, CD),
        h2=np.asarray(inputs["L0_ffn_ln2_b"]).reshape(1, CD),
        araw=np.asarray(inputs["L0_attn_alpha_raw"]).reshape(1, H),
        braw=np.asarray(inputs["L0_attn_beta_raw"]).reshape(1, H),
        ident=np.eye(128, dtype=np.float32),
        stril=np.tril(np.ones((128, 128), np.float32), -1),
        striu=np.triu(np.ones((128, 128), np.float32), 1),
        blk8=np.kron(np.eye(16, dtype=np.float32), np.ones((8, 8), np.float32)),
        iota8=np.arange(E, dtype=np.float32).reshape(1, E),
    )
    shared = {k: np.ascontiguousarray(v, dtype=np.float32) for k, v in shared.items()}
    in_maps = []
    for i in range(8):
        m = dict(shared)
        m["xb"] = np.ascontiguousarray(x[i], dtype=np.float32)
        m["xs"] = np.ascontiguousarray(
            x[:, PSL * i:PSL * (i + 1), :].transpose(1, 0, 2).reshape(TOK, CD), dtype=np.float32)
        in_maps.append(m)
    return in_maps


def assemble_out(results):
    out = np.empty((B, T, CD), np.float32)
    for i in range(8):
        o = results[i]["out"].reshape(PSL, B, CD).transpose(1, 0, 2)
        out[:, PSL * i:PSL * (i + 1), :] = o
    return out


_CACHE = {}


def kernel(**inputs):
    if "nc" not in _CACHE:
        nc, dti, tapset = build_nc()
        _CACHE["nc"] = nc
    nc = _CACHE["nc"]
    in_maps = make_inmaps(inputs)
    res = run_bass_kernel_spmd(nc, in_maps, core_ids=list(range(8)))
    return assemble_out(res.results)


# version 07
# speedup vs baseline: 1.1340x; 1.0800x over previous
import numpy as np
from contextlib import ExitStack

import concourse.bass as bass
import concourse.mybir as mybir
from concourse.bass_utils import run_bass_kernel_spmd

F32 = mybir.dt.float32
BF16 = mybir.dt.bfloat16
AF = mybir.ActivationFunctionType
ALU = mybir.AluOpType
AX = mybir.AxisListType

B, T, CD, H, D, CS = 8, 2048, 1024, 16, 64, 128
NCH = T // CS
PSL = T // 8
TOK = B * PSL
E = 8
QH = 4
NQ = H // QH
EPS = 1e-5

ENGINES = ("tensor", "vector", "scalar", "gpsimd", "sync")
DMA_SEMS = ("gs", "gw", "gw0", "gw1", "gm0", "gm1", "gg", "gq0", "gq1", "sy0", "sy1", "syt", "syg")


class Prog:
    def __init__(self):
        self.st = {e: [] for e in ENGINES}
        self.semnames = []
        self.cnt = {}

    def sem(self, name):
        self.semnames.append(name)
        self.cnt[name] = 0

    def op(self, eng, fn):
        self.st[eng].append(fn)

    def wait(self, eng, tok):
        if tok is None:
            return
        sem, val = tok
        if sem in DMA_SEMS:
            val = self.cnt[sem]
        if val <= 0:
            return
        self.op(eng, lambda e, S, sem=sem, val=val: e.wait_ge(S[sem], val))

    def waits(self, eng, toks):
        best = {}
        for t in toks:
            if t is None:
                continue
            s, v = t
            if v > best.get(s, -1):
                best[s] = v
        for s, v in best.items():
            self.wait(eng, (s, v))

    def do(self, eng, fn, sem=None, k=1):
        if sem is None:
            self.op(eng, fn)
            return None
        self.cnt[sem] += k
        v = self.cnt[sem]

        def wrapped(e, S, fn=fn, sem=sem, k=k):
            fn(e, S).then_inc(S[sem], k)

        self.op(eng, wrapped)
        return (sem, v)

    def bar(self, eng, sem):
        t = self.mark(eng, sem)
        self.wait(eng, t)
        return t

    def mark(self, eng, sem, k=1):
        self.cnt[sem] += k
        v = self.cnt[sem]
        fn = self.st[eng].pop()

        def wrapped(e, S, fn=fn, sem=sem, k=k):
            fn(e, S).then_inc(S[sem], k)

        self.st[eng].append(wrapped)
        return (sem, v)


def emit(nc, p):
    with ExitStack() as ctx:
        S = {}
        for name in p.semnames:
            S[name] = ctx.enter_context(nc.semaphore(name))
        block = ctx.enter_context(nc.Block())
        for eng in ENGINES:
            stream = p.st[eng]

            def body(handle, stream=stream, S=S):
                for fn in stream:
                    fn(handle, S)

            getattr(block, eng)(body)


def build_nc(taps=(), ffn=True, no_cc=False, stop_after=None):
    taps = set(taps)
    nc = bass.Bass("TRN2", target_bir_lowering=False)
    p = Prog()
    for s in ("td", "vd", "sd", "gs", "gw", "gw0", "gw1", "gm0", "gm1", "gg", "gq0", "gq1", "sy0", "sy1", "syt", "syg", "cc"):
        p.sem(s)

    dti = {}

    def din(name, shape, dtype=F32):
        dti[name] = nc.dram_tensor(name, list(shape), dtype, kind="ExternalInput")
        return dti[name]

    def dout(name, shape, dtype=F32):
        dti[name] = nc.dram_tensor(name, list(shape), dtype, kind="ExternalOutput")
        return dti[name]

    din("xb", (T, CD))
    din("xs", (TOK, CD))
    for w in ("wq", "wk", "wv", "wproj", "wout"):
        din(w, (CD, CD))
    din("win", (CD, 3 * CD))
    din("wr", (CD, E))
    din("w1", (E, CD, CD))
    din("w2", (E, CD, CD))
    din("binrow", (1, 3 * CD))
    din("bout", (1, CD))
    din("ones1", (1, 512))
    din("brout", (1, E))
    din("b1c", (128, E * 8))
    din("b2r", (E, CD))
    din("g1", (1, CD)); din("h1", (1, CD)); din("g2", (1, CD)); din("h2", (1, CD))
    din("araw", (1, H)); din("braw", (1, H))
    din("ident", (128, 128))
    din("stril", (128, 128))
    din("striu", (128, 128))
    din("blk8", (128, 128))
    din("iota8", (1, E))

    dout("out", (TOK, CD))
    tap_set = {}

    def tapout(name, shape, dtype=BF16):
        if name in taps:
            tap_set[name] = dout("tap_" + name, shape, dtype)
            return tap_set[name]
        return None

    ab_dram = nc.dram_tensor("ab_dram", [1, 2 * H], F32)
    oh_dram = nc.dram_tensor("oh_dram", [E, TOK], BF16)
    cc_in = nc.dram_tensor("cc_in", [T, CD], BF16)
    cc_out = nc.dram_tensor("cc_out", [B * T, CD], BF16)

    sb = {}

    def salloc(name, shape, dtype):
        sb[name] = nc.alloc_sbuf_tensor("sb_" + name, list(shape), dtype)
        return sb[name]

    salloc("ident", (128, 128), BF16)
    salloc("stril", (128, 128), F32)
    salloc("striu", (128, 128), F32)
    salloc("blk8", (128, 128), BF16)
    salloc("iota", (128, E), F32)
    salloc("ab_raw", (1, 2 * H), F32)
    salloc("ab_sig", (1, 2 * H), F32)
    salloc("abc", (128, 2 * H), F32)
    salloc("ones", (1, 512), BF16)
    salloc("WRB", (128, 64), BF16)
    salloc("OHF", (8, 2048), BF16)
    salloc("b1c", (128, E * 8), F32)
    salloc("b2r", (E, CD), BF16)
    salloc("brows", (1, 4 * CD + E), BF16)
    salloc("gbc", (128, CD), BF16)
    salloc("hbc", (128, CD), BF16)

    salloc("RA", (128, 16384), BF16)
    salloc("RB", (128, 16384), BF16)
    salloc("RC", (128, 8192), BF16)
    salloc("KFQ", (128, 8192), BF16)
    salloc("QFQ", (128, 8192), BF16)
    salloc("RE", (128, 15360), BF16)
    salloc("RF", (128, 8192), BF16)
    salloc("RG", (128, 4096), BF16)
    salloc("MS", (128, 1024), F32)
    salloc("GAB", (64, 512), F32)
    salloc("STK", (128, 384), F32)
    salloc("GBF", (64, 256), BF16)

    ps = [nc.alloc_psum_tensor(f"PP{i}", [128, 1024], F32) for i in range(4)]

    def psA(i, n=512):
        return ps[i].ap()[:, 0:n]

    def psbf(i, off, n):
        return ps[i].ap().bitcast(BF16)[:, off:off + n]

    def bfv(region, off, n, inner=None):
        ap = sb[region][:, off:off + n]
        if inner is not None:
            ap = ap.rearrange("p (a b) -> p a b", b=inner)
        return ap

    MSL = sb["MS"].ap()[:, 0:512].rearrange("p (a b) -> p a b", b=128)
    MSU = sb["MS"].ap()[:, 512:1024].rearrange("p (a b) -> p a b", b=128)
    GA = sb["GAB"].ap()[0:64, 0:256]
    GB = sb["GAB"].ap()[0:64, 256:512]
    KNS = sb["RG"].ap().bitcast(F32)[:, 0:1024]
    STT = [sb["STK"].ap()[:, 0:16], sb["STK"].ap()[:, 16:32]]
    KST = sb["STK"].ap()[:, 32:40]
    QTS = bfv("RG", 2048, 512)

    def PWT(lvl):
        return bfv("RE", lvl * 512, 512, 128)

    def PWU(par):
        return bfv("RE", (7 + par) * 512, 512, 128)

    def RBUF(par):
        return bfv("RE", (9 + par) * 512, 512, 128)

    ATB = bfv("RE", 11 * 512, 512, 128)
    NBUF = bfv("RE", 12 * 512, 512, 128)
    UFB = sb["RE"].ap()[0:64, 13 * 512:13 * 512 + 512].rearrange("p (a b) -> p a b", b=128)
    WBUF = bfv("RE", 14 * 512, 256, 64)
    GBF = sb["RE"].ap()[0:64, 14 * 512 + 256:14 * 512 + 512].rearrange("p (a b) -> p a b", b=64)

    XSTG = sb["RF"].ap().bitcast(F32)[:, 2048:4096].rearrange("p (a b) -> p a b", b=1024)
    XNT = bfv("RF", 0, 2048, 1024)
    WKB = bfv("RF", 2048, 2048, 256)
    WQB = bfv("RF", 4096, 2048, 256)
    WVBc = bfv("RF", 6144, 2048, 256)

    def setup():
        def g(dst, src):
            return p.do("gpsimd", lambda e, S, d=dst, s=src: e.dma_start(out=d, in_=s), "gs", 16)

        def bc(src):
            return src.partition_broadcast(128).rearrange("p o e -> p (o e)")

        g(sb["ident"][:, :], dti["ident"].ap())
        g(sb["stril"][:, :], dti["stril"].ap())
        g(sb["striu"][:, :], dti["striu"].ap())
        g(sb["blk8"][:, :], dti["blk8"].ap())
        g(sb["iota"][:, :], bc(dti["iota8"].ap()))
        g(sb["ones"][0:1, :], dti["ones1"].ap())
        g(sb["b1c"][:, :], dti["b1c"].ap())
        g(sb["b2r"][:, :], dti["b2r"].ap())
        g(sb["brows"][0:1, 0:3 * CD], dti["binrow"].ap())
        g(sb["brows"][0:1, 3 * CD:4 * CD], dti["bout"].ap())
        g(sb["brows"][0:1, 4 * CD:4 * CD + E], dti["brout"].ap())
        g(sb["WRB"].ap().rearrange("p (a b) -> p a b", b=8),
          dti["wr"].ap().rearrange("(a p) o -> p a o", p=128))
        g(sb["gbc"][:, :], bc(dti["g1"].ap()))
        g(sb["hbc"][:, :], bc(dti["h1"].ap()))
        g(sb["ab_raw"][0:1, 0:H], dti["araw"].ap())
        t_last = g(sb["ab_raw"][0:1, H:2 * H], dti["braw"].ap())
        p.wait("scalar", t_last)
        t_exp = p.do("scalar", lambda e, S: e.activation(sb["ab_sig"][0:1, :], sb["ab_raw"][0:1, :], AF.Exp, scale=-1.0), "sd")
        p.wait("vector", t_exp)
        p.do("vector", lambda e, S: e.tensor_scalar_add(sb["ab_sig"][0:1, :], sb["ab_sig"][0:1, :], 1.0))
        p.bar("vector", "vd")
        p.do("vector", lambda e, S: e.reciprocal(sb["ab_sig"][0:1, :], sb["ab_sig"][0:1, :]))
        p.bar("vector", "vd")
        p.do("vector", lambda e, S: e.tensor_scalar_mul(sb["ab_sig"][0:1, :], sb["ab_sig"][0:1, :], 0.5))
        t_sig = p.mark("vector", "vd")
        p.wait("gpsimd", t_sig)
        t_ab1 = p.do("gpsimd", lambda e, S: e.dma_start(out=ab_dram.ap(), in_=sb["ab_sig"][0:1, :]), "gs", 16)
        p.wait("gpsimd", t_ab1)
        t_abc = p.do("gpsimd", lambda e, S: e.dma_start(out=sb["abc"][:, :], in_=bc(ab_dram.ap())), "gs", 16)
        return t_abc

    ln_state = {"prev_apply": [None, None]}

    def ln_tile(xin, out, wait_toks, par=0, gain=None, bias=None, out2=None):
        st = STT[par]
        p.waits("vector", wait_toks + [ln_state["prev_apply"][par]])
        p.do("vector", lambda e, S, st=st, xin=xin: e.bn_stats(st[:, 4:10], xin[:, 0:512]))
        p.bar("vector", "vd")
        p.do("vector", lambda e, S, st=st, xin=xin: e.bn_stats(st[:, 10:16], xin[:, 512:1024]))
        p.bar("vector", "vd")
        p.do("vector", lambda e, S, st=st: e.bn_aggr(st[:, 0:2], st[:, 4:16]))
        p.bar("vector", "vd")
        p.do("vector", lambda e, S, st=st: e.tensor_scalar_add(st[:, 2:3], st[:, 1:2], float(EPS)))
        t_ag = p.mark("vector", "vd")
        p.wait("scalar", t_ag)
        p.do("scalar", lambda e, S, st=st: e.activation(st[:, 2:3], st[:, 2:3], AF.Ln))
        p.bar("scalar", "sd")
        p.do("scalar", lambda e, S, st=st: e.activation(st[:, 2:3], st[:, 2:3], AF.Exp, scale=-0.5))
        t_r = p.mark("scalar", "sd")
        p.wait("vector", t_r)
        p.do("vector", lambda e, S, st=st: e.tensor_scalar(st[:, 3:4], st[:, 0:1], st[:, 2:3], -1.0, ALU.mult, ALU.mult))
        t_b = p.mark("vector", "vd")
        p.wait("scalar", t_b)
        p.do("scalar", lambda e, S, st=st, out=out, xin=xin: e.activation(out, xin, AF.Identity, bias=st[:, 3:4], scale=st[:, 2:3]))
        tok = p.mark("scalar", "sd")
        ln_state["prev_apply"][par] = tok
        if gain is not None:
            p.wait("vector", tok)
            p.do("vector", lambda e, S, out=out, gain=gain: e.tensor_mul(out, out, gain))
            p.bar("vector", "vd")
            p.do("vector", lambda e, S, out=out, bias=bias, out2=out2: e.tensor_add(out2 if out2 is not None else out, out, bias))
            tok = p.mark("vector", "vd")
        return tok

    def titans(t_abc):
        xnf = bfv("RA", 0, 16384, 2048)
        yT = bfv("RB", 0, 16384, 1024)
        kv = bfv("RC", 0, 8192, 512).rearrange("p a (h d) -> p a h d", d=128)
        kfq = sb["KFQ"].ap()[0:64, :].rearrange("p (a b) -> p a b", b=2048)
        qfq = sb["QFQ"].ap()[0:64, :].rearrange("p (a b) -> p a b", b=2048)

        prev_slot = [None, None]
        t_ev = None
        for i in range(16):
            par = i % 2
            p.waits("sync", [prev_slot[par]])
            t_load = p.do("sync", lambda e, S, i=i, par=par: e.dma_start(
                out=XSTG[:, par, :], in_=dti["xb"].ap()[128 * i:128 * (i + 1), :]),
                "sy0" if par == 0 else "sy1", 16)
            tln = ln_tile(XSTG[:, par, :], XNT[:, par, :], [t_load, prev_slot[par]], par=par)
            p.waits("tensor", [tln, t_ev])
            for c8 in range(8):
                p.do("tensor", lambda e, S, par=par, c8=c8: e.transpose(
                    psbf(3, 128 * c8, 128), XNT[:, par, 128 * c8:128 * (c8 + 1)], sb["ident"][:, :]))
            t_tr = p.mark("tensor", "td")
            p.wait("vector", t_tr)
            p.do("vector", lambda e, S, i=i: e.tensor_copy(
                xnf[:, :, 128 * i:128 * (i + 1)], psbf(3, 0, 1024).rearrange("p (a b) -> p a b", b=128)))
            t_ev = p.mark("vector", "vd")
            prev_slot[par] = t_ev
        t_xnf = t_ev
        tp = tapout("xnf", (128, 16384), BF16)
        if tp is not None:
            p.wait("sync", t_xnf)
            p.do("sync", lambda e, S, tp=tp: e.dma_start(out=tp.ap(), in_=sb["RA"][:, :]), "syt", 16)
        if stop_after == "xn":
            return t_xnf

        t_pass_done = None
        t_pass_sd = None
        t_yev = None
        for q in range(NQ):
            h0 = q * QH
            p.waits("vector", [t_abc, t_pass_done])
            for hh in range(QH):
                h = h0 + hh
                p.do("vector", lambda e, S, hh=hh, h=h: e.tensor_scalar(
                    MSL[:, hh, :], sb["stril"][:, :], sb["abc"][:, h:h + 1], None, ALU.mult))
                p.do("vector", lambda e, S, hh=hh, h=h: e.tensor_scalar(
                    MSU[:, hh, :], sb["striu"][:, :], sb["abc"][:, h:h + 1], None, ALU.mult))
            t_masks = p.mark("vector", "vd")

            p.waits("gpsimd", [t_xnf, t_pass_done])
            t_wk = p.do("gpsimd", lambda e, S, h0=h0: e.dma_start(
                out=WKB, in_=dti["wk"].ap()[:, 64 * h0:64 * h0 + 256].rearrange("(a p) o -> p a o", p=128)), "gw", 16)
            t_wv = p.do("gpsimd", lambda e, S, h0=h0: e.dma_start(
                out=WVBc, in_=dti["wv"].ap()[:, 64 * h0:64 * h0 + 256].rearrange("(a p) o -> p a o", p=128)), "gw", 16)
            t_wq = p.do("gpsimd", lambda e, S, h0=h0: e.dma_start(
                out=WQB, in_=dti["wq"].ap()[:, 64 * h0:64 * h0 + 256].rearrange("(a p) o -> p a o", p=128)), "gw", 16)

            kv_ready = []
            t_ktr = None
            t_qtr = None
            prev_q = None
            for i in range(16):
                p.waits("tensor", [t_wk, t_xnf] + ([kv_ready[i - 1][0]] if i else [t_pass_done, t_pass_sd]))
                for c8 in range(8):
                    p.do("tensor", lambda e, S, i=i, c8=c8: e.matmul(
                        psA(0, 256), xnf[:, c8, 128 * i:128 * (i + 1)], WKB[:, c8, :],
                        start=(c8 == 0), stop=(c8 == 7)))
                t_kt = p.mark("tensor", "td")
                p.waits("tensor", [t_wv] + ([kv_ready[i - 1][1]] if i else [t_pass_done, t_pass_sd]))
                for c8 in range(8):
                    p.do("tensor", lambda e, S, i=i, c8=c8: e.matmul(
                        psA(1, 256), xnf[:, c8, 128 * i:128 * (i + 1)], WVBc[:, c8, :],
                        start=(c8 == 0), stop=(c8 == 7)))
                t_vt = p.mark("tensor", "td")
                p.waits("tensor", [t_wq, prev_q] + ([] if i else [t_pass_done, t_pass_sd]))
                for c8 in range(8):
                    p.do("tensor", lambda e, S, i=i, c8=c8: e.matmul(
                        psA(2, 256), xnf[:, c8, 128 * i:128 * (i + 1)], WQB[:, c8, :],
                        start=(c8 == 0), stop=(c8 == 7)))
                t_qt = p.mark("tensor", "td")
                p.wait("vector", t_kt)
                p.do("vector", lambda e, S: e.tensor_copy(KNS[:, 0:256], psA(0, 256)))
                t_cp = p.mark("vector", "vd")
                p.wait("scalar", t_cp)
                p.do("scalar", lambda e, S: e.activation(KNS[:, 512:768], KNS[:, 0:256], AF.Square))
                t_sq = p.mark("scalar", "sd")
                p.wait("vector", t_sq)
                p.do("vector", lambda e, S: e.reduce_sum(
                    KST[:, 0:4], KNS[:, 512:768].rearrange("p (h d) -> p h d", d=64), axis=AX.X))
                p.bar("vector", "vd")
                p.do("vector", lambda e, S: e.tensor_scalar_max(KST[:, 0:4], KST[:, 0:4], 1e-24))
                t_ss = p.mark("vector", "vd")
                p.wait("scalar", t_ss)
                p.do("scalar", lambda e, S: e.activation(KST[:, 0:4], KST[:, 0:4], AF.Ln))
                p.bar("scalar", "sd")
                p.do("scalar", lambda e, S: e.activation(KST[:, 0:4], KST[:, 0:4], AF.Exp, scale=-0.5))
                t_inv = p.mark("scalar", "sd")
                p.wait("vector", t_inv)
                p.do("vector", lambda e, S, i=i: e.tensor_tensor(
                    kv[:, i, :, 0:64], KNS[:, 0:256].rearrange("p (h d) -> p h d", d=64),
                    KST[:, 0:4].unsqueeze(2).broadcast_to([128, 4, 64]), ALU.mult))
                t_k = p.mark("vector", "vd")
                p.waits("vector", [t_vt, t_abc])
                p.do("vector", lambda e, S, i=i, h0=h0: e.tensor_tensor(
                    kv[:, i, :, 64:128], psA(1, 256).rearrange("p (h d) -> p h d", d=64),
                    sb["abc"].ap()[:, H + h0:H + h0 + 4].unsqueeze(2).broadcast_to([128, 4, 64]), ALU.mult))
                t_v = p.mark("vector", "vd")
                kv_ready.append((t_k, t_v))
                p.waits("vector", [t_qt, t_qtr])
                p.do("vector", lambda e, S: e.tensor_copy(QTS[:, 0:256], psA(2, 256)))
                t_qs = p.mark("vector", "vd")
                prev_q = t_qs
                p.waits("tensor", [t_qs, t_k, t_ktr])
                for hh in range(QH):
                    p.do("tensor", lambda e, S, hh=hh: e.transpose(
                        ps[3].ap().bitcast(BF16)[0:64, 128 * hh:128 * (hh + 1)],
                        QTS[:, 64 * hh:64 * (hh + 1)], sb["ident"][:, :]))
                for hh in range(QH):
                    p.do("tensor", lambda e, S, i=i, hh=hh: e.transpose(
                        ps[3].ap().bitcast(BF16)[0:64, 512 + 128 * hh:512 + 128 * (hh + 1)],
                        kv[:, i, hh, 0:64], sb["ident"][:, :]))
                t_tr = p.mark("tensor", "td")
                p.wait("vector", t_tr)
                p.do("vector", lambda e, S, i=i: e.tensor_copy(
                    qfq[:, :, 128 * i:128 * (i + 1)],
                    ps[3].ap().bitcast(BF16)[0:64, 0:512].rearrange("p (a b) -> p a b", b=128)))
                p.bar("vector", "vd")
                p.do("vector", lambda e, S, i=i: e.tensor_copy(
                    kfq[:, :, 128 * i:128 * (i + 1)],
                    ps[3].ap().bitcast(BF16)[0:64, 512:1024].rearrange("p (a b) -> p a b", b=128)))
                t_qtr = p.mark("vector", "vd")
                t_ktr = t_qtr
            t_proj = t_qtr
            if stop_after == "proj" and q == 0:
                return t_proj

            SETB = 7680

            def sPWT(s, lvl):
                return bfv("RE", SETB * s + lvl * 512, 512, 128)

            def sPWU(s, pr2):
                return bfv("RE", SETB * s + (7 + pr2) * 512, 512, 128)

            def sRBUF(s, pr2):
                return bfv("RE", SETB * s + (9 + pr2) * 512, 512, 128)

            def sATB(s):
                return bfv("RE", SETB * s + 11 * 512, 512, 128)

            def sNBUF(s):
                return bfv("RE", SETB * s + 12 * 512, 512, 128)

            def sUFB(s):
                return sb["RE"].ap()[0:64, SETB * s + 13 * 512:SETB * s + 13 * 512 + 512].rearrange("p (a b) -> p a b", b=128)

            def sWBUF(s):
                return bfv("RE", SETB * s + 13 * 512 + 512, 256, 64)

            def psH(i, s, off=0, n=512):
                return ps[i].ap()[:, 512 * s + off:512 * s + off + n]

            GBFs = sb["GBF"].ap().rearrange("p (a b) -> p a b", b=64)

            free_tok = {}
            for i4 in range(4):
                for s2 in range(2):
                    free_tok[(i4, s2)] = t_proj
            t_gbf = None
            t_gf = None
            t_uf0 = None
            for cp in range(NCH // 2):
                cs = (2 * cp, 2 * cp + 1)
                t_s = [None, None]; t_kq = [None, None]; t_n = [None, None]
                t_at = [None, None]; t_rfin = [None, None]; t_uf = [None, None]
                rfin = [None, None]
                for s in range(2):
                    c = cs[s]
                    p.waits("tensor", [t_proj, free_tok[(0, s)]])
                    for hh in range(QH):
                        p.do("tensor", lambda e, S, hh=hh, c=c, s=s: e.matmul(
                            psH(0, s, 128 * hh, 128),
                            kfq[0:64, hh, 128 * c:128 * (c + 1)],
                            kfq[0:64, hh, 128 * c:128 * (c + 1)], start=True, stop=True))
                    t_s[s] = p.mark("tensor", "td")
                    p.waits("tensor", [free_tok[(1, s)]])
                    for hh in range(QH):
                        p.do("tensor", lambda e, S, hh=hh, c=c, s=s: e.matmul(
                            psH(1, s, 128 * hh, 128),
                            kfq[0:64, hh, 128 * c:128 * (c + 1)],
                            qfq[0:64, hh, 128 * c:128 * (c + 1)], start=True, stop=True))
                    t_kq[s] = p.mark("tensor", "td")
                for s in range(2):
                    p.waits("vector", [t_s[s], t_masks])
                    p.do("vector", lambda e, S, s=s: e.tensor_tensor(
                        sNBUF(s).rearrange("p a b -> p (a b)"), psH(0, s), MSL.rearrange("p a b -> p (a b)"), ALU.mult))
                    p.bar("vector", "vd")
                    p.do("vector", lambda e, S, s=s: e.tensor_tensor(
                        sPWT(s, 0).rearrange("p a b -> p (a b)"), psH(0, s), MSU.rearrange("p a b -> p (a b)"), ALU.mult))
                    t_n[s] = p.mark("vector", "vd")
                    free_tok[(0, s)] = t_n[s]
                    p.wait("vector", t_kq[s])
                    p.do("vector", lambda e, S, s=s: e.tensor_tensor(
                        sATB(s), psH(1, s).rearrange("p (a b) -> p a b", b=128),
                        sb["striu"].ap().unsqueeze(1).broadcast_to([128, 4, 128]), ALU.mult))
                    t_at[s] = p.mark("vector", "vd")
                    free_tok[(1, s)] = t_at[s]

                cur_u = [sNBUF(0), sNBUF(1)]
                cur_t = [sPWT(0, 0), sPWT(1, 0)]
                t_cu = [t_n[0], t_n[1]]
                t_ctl = [t_n[0], t_n[1]]
                for lvl in range(1, 7):
                    for s in range(2):
                        t_mu_s = None
                        if lvl < 6:
                            p.waits("tensor", [t_cu[s], t_ctl[s], free_tok[(2, s)]])
                            for hh in range(QH):
                                p.do("tensor", lambda e, S, hh=hh, s=s, ct=cur_t[s], cu=cur_u[s]: e.matmul(
                                    psH(2, s, 128 * hh, 128), ct[:, hh, :], cu[:, hh, :],
                                    start=True, stop=True))
                            t_mu_s = p.mark("tensor", "td")
                        p.waits("tensor", [t_cu[s], t_ctl[s], free_tok[(3, s)]])
                        for hh in range(QH):
                            p.do("tensor", lambda e, S, hh=hh, s=s, ct=cur_t[s], cu=cur_u[s]: e.matmul(
                                psH(3, s, 128 * hh, 128), cu[:, hh, :], ct[:, hh, :],
                                start=True, stop=True))
                        t_mt_s = p.mark("tensor", "td")
                        if lvl < 6:
                            nxt_u = sPWU(s, lvl % 2)
                            p.wait("scalar", t_mu_s)
                            p.do("scalar", lambda e, S, s=s, nxt_u=nxt_u: e.copy(
                                nxt_u.rearrange("p a b -> p (a b)"), psH(2, s)))
                            t_cu[s] = p.mark("scalar", "sd")
                            free_tok[(2, s)] = t_cu[s]
                            cur_u[s] = nxt_u
                        p.wait("scalar", t_mt_s)
                        p.do("scalar", lambda e, S, s=s, lvl=lvl: e.copy(
                            sPWT(s, lvl).rearrange("p a b -> p (a b)"), psH(3, s)))
                        t_ctl[s] = p.mark("scalar", "sd")
                        free_tok[(3, s)] = t_ctl[s]
                        cur_t[s] = sPWT(s, lvl)

                r_prev = [kv[:, cs[0], :, :], kv[:, cs[1], :, :]]
                r_tok = [kv_ready[cs[0]][1], kv_ready[cs[1]][1]]
                for j, lvl in enumerate([6, 5, 4, 3, 2, 1, 0]):
                    pr = j % 2
                    for s in range(2):
                        tgt = sRBUF(s, j % 2)
                        p.waits("tensor", [r_tok[s], t_ctl[s], free_tok[(pr, s)]])
                        for hh in range(QH):
                            p.do("tensor", lambda e, S, hh=hh, s=s, lvl=lvl, rp=r_prev[s], pr=pr: e.matmul(
                                psH(pr, s, 128 * hh, 128), sPWT(s, lvl)[:, hh, :], rp[:, hh, :],
                                start=True, stop=True))
                        t_mm = p.mark("tensor", "td")
                        p.wait("vector", t_mm)
                        if lvl > 0:
                            p.do("vector", lambda e, S, s=s, tgt=tgt, rp=r_prev[s], pr=pr: e.tensor_tensor(
                                tgt.rearrange("p a b -> p (a b)"), psH(pr, s),
                                rp.rearrange("p a b -> p (a b)"), ALU.add))
                        else:
                            p.do("vector", lambda e, S, s=s, tgt=tgt, rp=r_prev[s], pr=pr: e.tensor_tensor(
                                tgt.rearrange("p a b -> p (a b)"),
                                rp.rearrange("p a b -> p (a b)"), psH(pr, s), ALU.subtract))
                        r_tok[s] = p.mark("vector", "vd")
                        free_tok[(pr, s)] = r_tok[s]
                        r_prev[s] = tgt
                for s in range(2):
                    rfin[s], t_rfin[s] = r_prev[s], r_tok[s]
                    p.waits("tensor", [t_rfin[s], free_tok[(2, s)]])
                    for hh in range(QH):
                        p.do("tensor", lambda e, S, hh=hh, s=s, rf=rfin[s]: e.transpose(
                            ps[2].ap().bitcast(BF16)[0:64, 1024 * s + 128 * hh:1024 * s + 128 * (hh + 1)],
                            rf[:, hh, 0:64], sb["ident"][:, :]))
                    t_ut = p.mark("tensor", "td")
                    p.waits("scalar", [t_ut, t_abc])
                    for hh in range(QH):
                        h = h0 + hh
                        p.do("scalar", lambda e, S, hh=hh, h=h, s=s: e.activation(
                            sUFB(s)[:, hh, :], ps[2].ap().bitcast(BF16)[0:64, 1024 * s + 128 * hh:1024 * s + 128 * (hh + 1)],
                            AF.Identity, scale=sb["abc"].ap()[0:64, h:h + 1]))
                    t_uf[s] = p.mark("scalar", "sd")
                    free_tok[(2, s)] = t_uf[s]
                t_uf0 = t_uf[1]

                for s in range(2):
                    c = cs[s]
                    if c == 0:
                        w_view, t_w = rfin[s][:, :, 64:128], t_rfin[s]
                    else:
                        p.waits("tensor", [t_uf[s], t_gbf, free_tok[(3, s)]])
                        for hh in range(QH):
                            p.do("tensor", lambda e, S, hh=hh, s=s: e.matmul(
                                psH(3, s, 64 * hh, 64),
                                sUFB(s)[0:64, hh, :], GBFs[0:64, hh, :], start=True, stop=True))
                        t_wmm = p.mark("tensor", "td")
                        p.waits("vector", [t_wmm, t_rfin[s]])
                        p.do("vector", lambda e, S, s=s, rf=rfin[s]: e.tensor_tensor(
                            sWBUF(s), rf[:, :, 64:128],
                            psH(3, s, 0, 256).rearrange("p (a b) -> p a b", b=64), ALU.subtract))
                        t_w = p.mark("vector", "vd")
                        free_tok[(3, s)] = t_w
                        w_view = sWBUF(s)

                    p.waits("tensor", [t_w, t_at[s], free_tok[(0, s)]] + ([t_gbf] if c > 0 else []))
                    for hh in range(QH):
                        if c > 0:
                            p.do("tensor", lambda e, S, hh=hh, c=c, s=s: e.matmul(
                                psH(0, s, 64 * hh, 64),
                                qfq[0:64, hh, 128 * c:128 * (c + 1)], GBFs[0:64, hh, :],
                                start=True, stop=False))
                        p.do("tensor", lambda e, S, hh=hh, c=c, s=s, w_view=w_view: e.matmul(
                            psH(0, s, 64 * hh, 64),
                            sATB(s)[:, hh, :], w_view[:, hh, :], start=(c == 0), stop=True))
                    t_y = p.mark("tensor", "td")
                    p.wait("vector", t_y)
                    p.do("vector", lambda e, S, c=c, q=q, s=s: e.tensor_copy(
                        yT[:, c, 256 * q:256 * (q + 1)], psH(0, s, 0, 256)))
                    t_yev = p.mark("vector", "vd")
                    free_tok[(0, s)] = t_yev

                    gsrc = GA if c % 2 == 0 else GB
                    gdst = GB if c % 2 == 0 else GA
                    p.waits("tensor", [t_w, free_tok[(1, s)]])
                    for hh in range(QH):
                        p.do("tensor", lambda e, S, hh=hh, c=c, s=s, w_view=w_view: e.matmul(
                            ps[1].ap()[0:64, 512 * s + 64 * hh:512 * s + 64 * (hh + 1)],
                            kv[:, c, hh, 0:64], w_view[:, hh, :], start=True, stop=True))
                    t_dg = p.mark("tensor", "td")
                    p.wait("vector", t_dg)
                    if c == 0:
                        p.do("vector", lambda e, S, gdst=gdst, s=s: e.tensor_copy(
                            gdst, ps[1].ap()[0:64, 512 * s:512 * s + 256]))
                    else:
                        p.do("vector", lambda e, S, gsrc=gsrc, gdst=gdst, s=s: e.tensor_tensor(
                            gdst, ps[1].ap()[0:64, 512 * s:512 * s + 256], gsrc, ALU.add))
                    t_gf = p.mark("vector", "vd")
                    free_tok[(1, s)] = t_gf
                    if c < NCH - 1:
                        p.wait("vector", t_gf)
                        p.do("vector", lambda e, S, gdst=gdst: e.tensor_copy(
                            sb["GBF"][:, :], gdst))
                        t_gbf = p.mark("vector", "vd")
            t_pass_done = t_gf
            t_pass_sd = t_uf0
            if stop_after == "chunks" and q == 0:
                return t_pass_done

        yF = bfv("RA", 0, 16384, 2048)
        t_yf = None
        for i in range(16):
            p.waits("tensor", [t_pass_done, t_pass_sd, t_yf])
            for c8 in range(8):
                p.do("tensor", lambda e, S, i=i, c8=c8: e.transpose(
                    psbf(3, 128 * c8, 128), yT[:, i, 128 * c8:128 * (c8 + 1)], sb["ident"][:, :]))
            t_tr = p.mark("tensor", "td")
            p.wait("vector", t_tr)
            p.do("vector", lambda e, S, i=i: e.tensor_copy(
                yF[:, :, 128 * i:128 * (i + 1)], psbf(3, 0, 1024).rearrange("p (a b) -> p a b", b=128)))
            t_yf = p.mark("vector", "vd")

        p.waits("gpsimd", [t_pass_done])
        t_wp = p.do("gpsimd", lambda e, S: e.dma_start(
            out=bfv("RF", 0, 4096, 512),
            in_=dti["wproj"].ap()[:, 0:512].rearrange("(a p) o -> p a o", p=128)), "gw", 16)
        t_wp2 = p.do("gpsimd", lambda e, S: e.dma_start(
            out=bfv("RF", 4096, 4096, 512),
            in_=dti["wproj"].ap()[:, 512:1024].rearrange("(a p) o -> p a o", p=128)), "gw", 16)
        sT = bfv("RB", 0, 16384, 1024)
        t_sev = None
        free_sp = [None, None]
        for i in range(16):
            for oh in range(2):
                wb = bfv("RF", 4096 * oh, 4096, 512)
                twb = t_wp if oh == 0 else t_wp2
                spsl = ps[0].ap()[:, 512 * oh:512 * (oh + 1)]
                p.waits("tensor", [twb, t_yf, free_sp[oh]])
                for c8 in range(8):
                    p.do("tensor", lambda e, S, i=i, c8=c8, wb=wb, spsl=spsl: e.matmul(
                        spsl, yF[:, c8, 128 * i:128 * (i + 1)], wb[:, c8, :],
                        start=(c8 == 0), stop=(c8 == 7)))
                t_mm = p.mark("tensor", "td")
                p.wait("vector", t_mm)
                p.do("vector", lambda e, S, i=i, oh=oh, spsl=spsl: e.tensor_copy(
                    sT[:, i, 512 * oh:512 * (oh + 1)], spsl))
                t_sev = p.mark("vector", "vd")
                free_sp[oh] = t_sev
        p.wait("gpsimd", t_sev)
        t_ccin = p.do("gpsimd", lambda e, S: e.dma_start(
            out=cc_in.ap().rearrange("(a p) c -> p a c", p=128), in_=sT), "gw", 16)
        if no_cc:
            t_cc = t_ccin
        else:
            p.wait("gpsimd", t_ccin)
            t_cc = p.do("gpsimd", lambda e, S: e.collective_compute(
                "AllGather", ALU.bypass, replica_groups=[list(range(8))],
                ins=[cc_in.ap()], outs=[cc_out.ap()]), "cc", 1)
        tp = tapout("sT", (128, 16384), BF16)
        if tp is not None:
            p.wait("sync", t_sev)
            p.do("sync", lambda e, S, tp=tp: e.dma_start(out=tp.ap(), in_=sb["RB"][:, :]), "syt", 16)
        return t_cc, t_sev


    def ffn(t_cc, t_sev0):
        xnf2 = bfv("RA", 0, 16384, 2048)
        ctxT = bfv("RB", 0, 16384, 1024)
        vS = bfv("RC", 0, 4096, 256)
        kfq = sb["KFQ"].ap()[0:64, :].rearrange("p (a b) -> p a b", b=2048)
        qfq = sb["QFQ"].ap()[0:64, :].rearrange("p (a b) -> p a b", b=2048)
        XS2 = sb["RF"].ap().bitcast(F32)[:, 2048:4096].rearrange("p (a b) -> p a b", b=1024)
        XN2 = bfv("RF", 0, 2048, 1024)
        WKB2 = bfv("RF", 2048, 2048, 256)
        WQB2 = bfv("RF", 4096, 2048, 256)
        WVB2 = bfv("RF", 6144, 2048, 256)
        QT2 = bfv("RG", 2048, 1024)

        prev_slot = [None, None]
        t_ev = None
        for i in range(16):
            par = i % 2
            p.waits("sync", [prev_slot[par], t_sev0])
            t_load = p.do("sync", lambda e, S, i=i, par=par: e.dma_start(
                out=XS2[:, par, :], in_=dti["xs"].ap()[128 * i:128 * (i + 1), :]),
                "sy0" if par == 0 else "sy1", 16)
            tln = ln_tile(XS2[:, par, :], XN2[:, par, :], [t_load, prev_slot[par]], par=par)
            p.waits("tensor", [tln, t_ev])
            for c8 in range(8):
                p.do("tensor", lambda e, S, par=par, c8=c8: e.transpose(
                    psbf(3, 128 * c8, 128), XN2[:, par, 128 * c8:128 * (c8 + 1)], sb["ident"][:, :]))
            t_tr = p.mark("tensor", "td")
            p.wait("vector", t_tr)
            p.do("vector", lambda e, S, i=i: e.tensor_copy(
                xnf2[:, :, 128 * i:128 * (i + 1)], psbf(3, 0, 1024).rearrange("p (a b) -> p a b", b=128)))
            t_ev = p.mark("vector", "vd")
            prev_slot[par] = t_ev
        t_xnf2 = t_ev

        rb_gate = ("gw", p.cnt["gw"])
        t_grp = None
        t_ctxe = None
        for g4 in range(4):
            p.waits("gpsimd", [t_xnf2, t_grp])
            t_wq = p.do("gpsimd", lambda e, S, g4=g4: e.dma_start(
                out=WQB2, in_=dti["win"].ap()[:, 256 * g4:256 * g4 + 256].rearrange("(a p) o -> p a o", p=128)), "gw", 16)
            t_wk = p.do("gpsimd", lambda e, S, g4=g4: e.dma_start(
                out=WKB2, in_=dti["win"].ap()[:, CD + 256 * g4:CD + 256 * g4 + 256].rearrange("(a p) o -> p a o", p=128)), "gw", 16)
            t_wv = p.do("gpsimd", lambda e, S, g4=g4: e.dma_start(
                out=WVB2, in_=dti["win"].ap()[:, 2 * CD + 256 * g4:2 * CD + 256 * g4 + 256].rearrange("(a p) o -> p a o", p=128)), "gw", 16)
            free_q = [t_grp, t_grp]
            free_k = [t_grp, t_grp]
            free_v = [t_grp, t_grp]
            free_qt = [t_grp, t_grp]
            free_tr = [t_grp, t_grp]
            st_qs = [None] * 16
            st_ks = [None] * 16
            for ii in range(17):
                if ii < 16:
                    i = ii
                    par = i % 2
                    qsl = ps[0].ap()[:, 512 * par:512 * par + 256]
                    ksl = ps[1].ap()[:, 512 * par:512 * par + 256]
                    vsl = ps[2].ap()[:, 512 * par:512 * par + 256]
                    p.waits("tensor", [t_wq, t_xnf2, free_q[par]])
                    for c8 in range(8):
                        p.do("tensor", lambda e, S, i=i, c8=c8, qsl=qsl: e.matmul(
                            qsl, xnf2[:, c8, 128 * i:128 * (i + 1)], WQB2[:, c8, :],
                            start=(c8 == 0), stop=False))
                    p.do("tensor", lambda e, S, g4=g4, qsl=qsl: e.matmul(
                        qsl, sb["ones"][0:1, 0:128], sb["brows"].ap()[0:1, 256 * g4:256 * g4 + 256],
                        start=False, stop=True))
                    t_qmm = p.mark("tensor", "td")
                    p.waits("tensor", [t_wk, free_k[par]])
                    for c8 in range(8):
                        p.do("tensor", lambda e, S, i=i, c8=c8, ksl=ksl: e.matmul(
                            ksl, xnf2[:, c8, 128 * i:128 * (i + 1)], WKB2[:, c8, :],
                            start=(c8 == 0), stop=False))
                    p.do("tensor", lambda e, S, g4=g4, ksl=ksl: e.matmul(
                        ksl, sb["ones"][0:1, 0:128], sb["brows"].ap()[0:1, CD + 256 * g4:CD + 256 * g4 + 256],
                        start=False, stop=True))
                    t_kmm = p.mark("tensor", "td")
                    p.waits("tensor", [t_wv, free_v[par]])
                    for c8 in range(8):
                        p.do("tensor", lambda e, S, i=i, c8=c8, vsl=vsl: e.matmul(
                            vsl, xnf2[:, c8, 128 * i:128 * (i + 1)], WVB2[:, c8, :],
                            start=(c8 == 0), stop=False))
                    p.do("tensor", lambda e, S, g4=g4, vsl=vsl: e.matmul(
                        vsl, sb["ones"][0:1, 0:128], sb["brows"].ap()[0:1, 2 * CD + 256 * g4:2 * CD + 256 * g4 + 256],
                        start=False, stop=True))
                    t_vmm = p.mark("tensor", "td")
                    p.waits("vector", [t_vmm])
                    p.do("vector", lambda e, S, i=i, vsl=vsl: e.tensor_copy(vS[:, i, :], vsl))
                    t_vev = p.mark("vector", "vd")
                    free_v[par] = t_vev
                    p.waits("vector", [t_qmm, free_qt[par]])
                    p.do("vector", lambda e, S, par=par, qsl=qsl: e.tensor_copy(
                        QT2[:, 512 * par:512 * par + 256], qsl))
                    t_qs = p.mark("vector", "vd")
                    free_q[par] = t_qs
                    st_qs[i] = t_qs
                    p.waits("vector", [t_kmm])
                    p.do("vector", lambda e, S, par=par, ksl=ksl: e.tensor_copy(
                        QT2[:, 512 * par + 256:512 * par + 512], ksl))
                    t_ks = p.mark("vector", "vd")
                    free_k[par] = t_ks
                    st_ks[i] = t_ks
                if ii >= 1:
                    i = ii - 1
                    par = i % 2
                    p.waits("tensor", [st_qs[i], st_ks[i], free_tr[par]])
                    for hh in range(QH):
                        p.do("tensor", lambda e, S, hh=hh, par=par: e.transpose(
                            ps[3].ap().bitcast(BF16)[0:64, 1024 * par + 128 * hh:1024 * par + 128 * (hh + 1)],
                            QT2[:, 512 * par + 64 * hh:512 * par + 64 * (hh + 1)], sb["ident"][:, :]))
                    for hh in range(QH):
                        p.do("tensor", lambda e, S, hh=hh, par=par: e.transpose(
                            ps[3].ap().bitcast(BF16)[0:64, 1024 * par + 512 + 128 * hh:1024 * par + 512 + 128 * (hh + 1)],
                            QT2[:, 512 * par + 256 + 64 * hh:512 * par + 256 + 64 * (hh + 1)], sb["ident"][:, :]))
                    t_tr = p.mark("tensor", "td")
                    free_qt[par] = t_tr
                    p.waits("vector", [t_tr])
                    p.do("vector", lambda e, S, i=i, par=par: e.tensor_copy(
                        qfq[:, :, 128 * i:128 * (i + 1)],
                        ps[3].ap().bitcast(BF16)[0:64, 1024 * par:1024 * par + 512].rearrange("p (a b) -> p a b", b=128)))
                    p.bar("vector", "vd")
                    p.do("vector", lambda e, S, i=i, par=par: e.tensor_copy(
                        kfq[:, :, 128 * i:128 * (i + 1)],
                        ps[3].ap().bitcast(BF16)[0:64, 1024 * par + 512:1024 * par + 1024].rearrange("p (a b) -> p a b", b=128)))
                    t_tre = p.mark("vector", "vd")
                    free_tr[par] = t_tre
            t_inproj = t_tre

            free_sc = [t_inproj, t_inproj]
            free1 = [t_inproj, t_inproj]
            free3 = [t_inproj, t_inproj]
            st_ex = [None] * 16
            for jj in range(17):
                if jj < 16:
                    hh, pg = jj // 4, jj % 4
                    par = jj % 2
                    sbase = 40 + 16 * par
                    pss = ps[par].ap()
                    ebase = 1536 * par
                    p.waits("tensor", [t_inproj, free_sc[par]])
                    for pt4 in range(4):
                        pt = 4 * pg + pt4
                        p.do("tensor", lambda e, S, hh=hh, pt=pt, pt4=pt4, pss=pss: e.matmul(
                            pss[:, 128 * pt4:128 * (pt4 + 1)],
                            qfq[0:64, hh, 128 * pt:128 * (pt + 1)],
                            kfq[0:64, hh, 128 * pt:128 * (pt + 1)], start=True, stop=True))
                    t_sc = p.mark("tensor", "td")
                    p.waits("vector", [t_sc])
                    p.do("vector", lambda e, S, pss=pss, sbase=sbase: e.reduce_max(
                        sb["STK"].ap()[:, sbase:sbase + 4],
                        pss[:, 0:512].rearrange("p (a b) -> p a b", b=128), axis=AX.X))
                    p.bar("vector", "vd")
                    p.do("vector", lambda e, S, sbase=sbase: e.tensor_scalar_mul(
                        sb["STK"].ap()[:, sbase + 4:sbase + 8], sb["STK"].ap()[:, sbase:sbase + 4], -0.125))
                    t_mx = p.mark("vector", "vd")
                    p.waits("scalar", [t_mx])
                    for pt4 in range(4):
                        p.do("scalar", lambda e, S, pt4=pt4, pss=pss, ebase=ebase, sbase=sbase: e.activation(
                            bfv("RE", ebase + 128 * pt4, 128),
                            pss[:, 128 * pt4:128 * (pt4 + 1)], AF.Exp,
                            scale=0.125, bias=sb["STK"].ap()[:, sbase + 4 + pt4:sbase + 5 + pt4]))
                    t_ex = p.mark("scalar", "sd")
                    st_ex[jj] = t_ex
                    free_sc[par] = t_ex
                if jj >= 1:
                    j = jj - 1
                    hh, pg = j // 4, j % 4
                    h = 4 * g4 + hh
                    par = j % 2
                    sbase = 40 + 16 * par
                    ebase = 1536 * par
                    p.waits("vector", [st_ex[j]])
                    p.do("vector", lambda e, S, ebase=ebase: e.tensor_tensor(
                        bfv("RE", ebase + 512, 512, 128), bfv("RE", ebase, 512, 128),
                        sb["blk8"].ap().unsqueeze(1).broadcast_to([128, 4, 128]), ALU.mult))
                    p.bar("vector", "vd")
                    p.do("vector", lambda e, S, ebase=ebase, sbase=sbase: e.reduce_sum(
                        sb["STK"].ap()[:, sbase + 8:sbase + 12], bfv("RE", ebase + 512, 512, 128), axis=AX.X))
                    p.bar("vector", "vd")
                    p.do("vector", lambda e, S, sbase=sbase: e.reciprocal(
                        sb["STK"].ap()[:, sbase + 12:sbase + 16], sb["STK"].ap()[:, sbase + 8:sbase + 12]))
                    p.bar("vector", "vd")
                    p.do("vector", lambda e, S, ebase=ebase, sbase=sbase: e.tensor_tensor(
                        bfv("RE", ebase, 512, 128), bfv("RE", ebase + 512, 512, 128),
                        sb["STK"].ap()[:, sbase + 12:sbase + 16].unsqueeze(2).broadcast_to([128, 4, 128]), ALU.mult))
                    t_pn = p.mark("vector", "vd")
                    p.waits("tensor", [t_pn, free3[par]])
                    for pt4 in range(4):
                        p.do("tensor", lambda e, S, pt4=pt4, par=par, ebase=ebase: e.transpose(
                            ps[3].ap().bitcast(BF16)[:, 1024 * par + 128 * pt4:1024 * par + 128 * (pt4 + 1)],
                            bfv("RE", ebase + 128 * pt4, 128), sb["ident"][:, :]))
                    t_ptr = p.mark("tensor", "td")
                    p.waits("vector", [t_ptr])
                    p.do("vector", lambda e, S, par=par, ebase=ebase: e.tensor_copy(
                        bfv("RE", ebase + 1024, 512, 128),
                        ps[3].ap().bitcast(BF16)[:, 1024 * par:1024 * par + 512].rearrange("p (a b) -> p a b", b=128)))
                    t_pte = p.mark("vector", "vd")
                    free3[par] = t_pte
                    p.waits("tensor", [t_pte, free1[par]])
                    for pt4 in range(4):
                        pt = 4 * pg + pt4
                        p.do("tensor", lambda e, S, hh=hh, pt=pt, pt4=pt4, par=par, ebase=ebase: e.matmul(
                            ps[2].ap()[:, 512 * par + 64 * pt4:512 * par + 64 * (pt4 + 1)],
                            bfv("RE", ebase + 1024 + 128 * pt4, 128),
                            vS[:, pt, 64 * hh:64 * (hh + 1)], start=True, stop=True))
                    t_cx = p.mark("tensor", "td")
                    p.waits("vector", [t_cx, rb_gate])
                    p.do("vector", lambda e, S, h=h, pg=pg, par=par: e.tensor_copy(
                        ctxT[:, 4 * pg:4 * (pg + 1), 64 * h:64 * (h + 1)],
                        ps[2].ap()[:, 512 * par:512 * par + 256].rearrange("p (a b) -> p a b", b=64)))
                    t_ctxe = p.mark("vector", "vd")
                    free1[par] = t_ctxe
            t_grp = t_ctxe

        tpc = tapout("ctx", (128, 16384), BF16)
        t_tapc = None
        if tpc is not None:
            p.wait("sync", t_grp)
            t_tapc = p.do("sync", lambda e, S, tpc=tpc: e.dma_start(out=tpc.ap(), in_=sb["RB"][:, :]), "syt", 16)
        ctxF = bfv("RA", 0, 16384, 2048)
        t_cf = None
        for i in range(16):
            p.waits("tensor", [t_grp, t_cf])
            for c8 in range(8):
                p.do("tensor", lambda e, S, i=i, c8=c8: e.transpose(
                    psbf(3, 128 * c8, 128), ctxT[:, i, 128 * c8:128 * (c8 + 1)], sb["ident"][:, :]))
            t_tr = p.mark("tensor", "td")
            p.wait("vector", t_tr)
            p.do("vector", lambda e, S, i=i: e.tensor_copy(
                ctxF[:, :, 128 * i:128 * (i + 1)], psbf(3, 0, 1024).rearrange("p (a b) -> p a b", b=128)))
            t_cf = p.mark("vector", "vd")

        p.waits("gpsimd", [t_cf])
        t_wo1 = p.do("gpsimd", lambda e, S: e.dma_start(
            out=bfv("RF", 0, 4096, 512),
            in_=dti["wout"].ap()[:, 0:512].rearrange("(a p) o -> p a o", p=128)), "gw", 16)
        t_wo2 = p.do("gpsimd", lambda e, S: e.dma_start(
            out=bfv("RE", 0, 4096, 512),
            in_=dti["wout"].ap()[:, 512:1024].rearrange("(a p) o -> p a o", p=128)), "gw", 16)
        hT = ctxT
        t_hev = t_tapc
        t_xld = None
        xslotA = sb["RF"].ap().bitcast(F32)[:, 3072:4096]
        xslotB = KNS
        zbf = bfv("RG", 2048, 1024)
        free_op = [None, None]
        prev_lnx = [None, None]
        for i in range(16):
            par = i % 2
            xs_slot = xslotA if par == 0 else xslotB
            pps = ps[2] if par == 0 else ps[3]
            p.waits("sync", [prev_lnx[par], t_cf])
            t_x = p.do("sync", lambda e, S, i=i, xs_slot=xs_slot: e.dma_start(
                out=xs_slot, in_=dti["xs"].ap()[128 * i:128 * (i + 1), :]),
                "sy0" if par == 0 else "sy1", 16)
            for oh in range(2):
                wb = bfv("RF", 0, 4096, 512) if oh == 0 else bfv("RE", 0, 4096, 512)
                twb = t_wo1 if oh == 0 else t_wo2
                p.waits("tensor", [twb, t_cf, free_op[par]])
                for c8 in range(8):
                    p.do("tensor", lambda e, S, i=i, c8=c8, wb=wb, oh=oh, pps=pps: e.matmul(
                        pps.ap()[:, 512 * oh:512 * (oh + 1)], ctxF[:, c8, 128 * i:128 * (i + 1)], wb[:, c8, :],
                        start=(c8 == 0), stop=False))
                p.do("tensor", lambda e, S, oh=oh, pps=pps: e.matmul(
                    pps.ap()[:, 512 * oh:512 * (oh + 1)], sb["ones"][0:1, 0:128],
                    sb["brows"].ap()[0:1, 3 * CD + 512 * oh:3 * CD + 512 * (oh + 1)],
                    start=False, stop=True))
            t_om = p.mark("tensor", "td")
            p.waits("vector", [t_om, t_x])
            p.do("vector", lambda e, S, xs_slot=xs_slot, pps=pps: e.tensor_tensor(
                xs_slot, pps.ap()[:, 0:1024], xs_slot, ALU.add))
            t_hp = p.mark("vector", "vd")
            free_op[par] = t_hp
            tln = ln_tile(xs_slot, zbf, [t_hp], par=par,
                          gain=sb["gbc"][:, :], bias=sb["hbc"][:, :], out2=hT[:, i, :])
            t_hev = tln
            prev_lnx[par] = tln
        t_h = t_hev

        tph = tapout("hT", (128, 16384), BF16)
        t_taph = None
        if tph is not None:
            p.wait("sync", t_h)
            t_taph = p.do("sync", lambda e, S, tph=tph: e.dma_start(out=tph.ap(), in_=sb["RB"][:, :]), "syt", 16)
        hF = bfv("RA", 0, 16384, 2048)
        t_hf = None
        for i in range(16):
            p.waits("tensor", [t_h, t_hf])
            for c8 in range(8):
                p.do("tensor", lambda e, S, i=i, c8=c8: e.transpose(
                    psbf(3, 128 * c8, 128), hT[:, i, 128 * c8:128 * (c8 + 1)], sb["ident"][:, :]))
            t_tr = p.mark("tensor", "td")
            p.wait("vector", t_tr)
            p.do("vector", lambda e, S, i=i: e.tensor_copy(
                hF[:, :, 128 * i:128 * (i + 1)], psbf(3, 0, 1024).rearrange("p (a b) -> p a b", b=128)))
            t_hf = p.mark("vector", "vd")

        p.waits("tensor", [t_hf])
        for i in range(16):
            for c8 in range(8):
                p.do("tensor", lambda e, S, i=i, c8=c8: e.matmul(
                    ps[2].ap()[:, 8 * i:8 * (i + 1)], hF[:, c8, 128 * i:128 * (i + 1)],
                    sb["WRB"].ap().rearrange("p (a b) -> p a b", b=8)[:, c8, :],
                    start=(c8 == 0), stop=False))
            p.do("tensor", lambda e, S, i=i: e.matmul(
                ps[2].ap()[:, 8 * i:8 * (i + 1)], sb["ones"][0:1, 0:128],
                sb["brows"].ap()[0:1, 4 * CD:4 * CD + E], start=False, stop=True))
        t_rt = p.mark("tensor", "td")
        p.wait("vector", t_rt)
        LG = sb["STK"].ap()[:, 96:224]
        p.do("vector", lambda e, S: e.tensor_copy(LG, ps[2].ap()[:, 0:128]))
        p.bar("vector", "vd")
        p.do("vector", lambda e, S: e.reduce_max(
            sb["STK"].ap()[:, 40:56], LG.rearrange("p (a b) -> p a b", b=8), axis=AX.X))
        t_m1 = p.mark("vector", "vd")
        p.wait("vector", t_m1)
        CAND = sb["STK"].ap()[:, 224:352]
        for i in range(16):
            p.do("vector", lambda e, S, i=i: e.tensor_scalar(
                CAND[:, 8 * i:8 * (i + 1)], LG[:, 8 * i:8 * (i + 1)],
                sb["STK"].ap()[:, 40 + i:41 + i], None, ALU.is_ge))
        p.bar("vector", "vd")
        p.do("vector", lambda e, S: e.scalar_tensor_tensor(
            CAND, CAND, -1e30, LG, ALU.mult, ALU.add))
        p.bar("vector", "vd")
        p.do("vector", lambda e, S: e.reduce_max(
            sb["STK"].ap()[:, 56:72], CAND.rearrange("p (a b) -> p a b", b=8), axis=AX.X))
        t_m2 = p.mark("vector", "vd")
        p.wait("vector", t_m2)
        for i in range(16):
            p.do("vector", lambda e, S, i=i: e.tensor_scalar(
                CAND[:, 8 * i:8 * (i + 1)], LG[:, 8 * i:8 * (i + 1)],
                sb["STK"].ap()[:, 56 + i:57 + i], None, ALU.is_ge))
        p.bar("vector", "vd")
        p.do("vector", lambda e, S: e.tensor_tensor(
            CAND.rearrange("p (a b) -> p a b", b=8), CAND.rearrange("p (a b) -> p a b", b=8),
            sb["iota"].ap().unsqueeze(1).broadcast_to([128, 16, 8]), ALU.mult))
        p.bar("vector", "vd")
        p.do("vector", lambda e, S: e.reduce_max(
            sb["STK"].ap()[:, 72:88], CAND.rearrange("p (a b) -> p a b", b=8), axis=AX.X))
        t_asn = p.mark("vector", "vd")
        p.wait("vector", t_asn)
        OHT = bfv("RG", 3072, 128)
        for i in range(16):
            p.do("vector", lambda e, S, i=i: e.tensor_scalar(
                OHT[:, 8 * i:8 * (i + 1)], sb["iota"][:, :],
                sb["STK"].ap()[:, 72 + i:73 + i], None, ALU.is_equal))
        t_oht = p.mark("vector", "vd")
        t_ohf = None
        for ig in range(4):
            p.waits("tensor", [t_oht, t_ohf])
            for i4 in range(4):
                i = 4 * ig + i4
                p.do("tensor", lambda e, S, i=i, i4=i4: e.transpose(
                    ps[3].ap().bitcast(BF16)[0:8, 128 * i4:128 * (i4 + 1)],
                    OHT[:, 8 * i:8 * (i + 1)], sb["ident"][:, :]))
            t_tr = p.mark("tensor", "td")
            p.wait("vector", t_tr)
            p.do("vector", lambda e, S, ig=ig: e.tensor_copy(
                sb["OHF"].ap()[0:8, 512 * ig:512 * (ig + 1)],
                ps[3].ap().bitcast(BF16)[0:8, 0:512]))
            t_ohf = p.mark("vector", "vd")
        p.wait("gpsimd", t_ohf)
        t_ohd = p.do("gpsimd", lambda e, S: e.dma_start(out=oh_dram.ap(), in_=sb["OHF"][:, :]), "gs", 16)
        tpo = tapout("ohf", (E, TOK), BF16)
        if tpo is not None:
            p.wait("sync", t_ohf)
            p.do("sync", lambda e, S, tpo=tpo: e.dma_start(out=tpo.ap(), in_=sb["OHF"][:, :]), "syt", 16)

        wbufs = [(bfv("RC", 0, 8192, 1024), bfv("RE", 0, 8192, 1024)),
                 (sb["KFQ"].ap().rearrange("p (a b) -> p a b", b=1024),
                  sb["QFQ"].ap().rearrange("p (a b) -> p a b", b=1024))]
        heb = bfv("RF", 0, 2048, 256)
        hemb = bfv("RF", 2048, 2048, 256)
        mke = bfv("RG", 0, 2048)
        t_wload = [None, None]
        t_mload = [None, None]
        p.waits("gpsimd", [t_hf, t_ohd])
        t_w1 = p.do("gpsimd", lambda e, S: e.dma_start(
            out=wbufs[0][0], in_=dti["w1"].ap()[0].rearrange("(a p) o -> p a o", p=128)), "gw0", 16)
        t_w2 = p.do("gpsimd", lambda e, S: e.dma_start(
            out=wbufs[0][1], in_=dti["w2"].ap()[0].rearrange("(a p) o -> p a o", p=128)), "gw0", 16)
        t_wload[0] = t_w2
        t_mk = p.do("gpsimd", lambda e, S: e.dma_start(
            out=mke, in_=oh_dram.ap()[0:1, :].partition_broadcast(128).rearrange("p o e -> p (o e)")), "gm0", 16)
        t_mload[0] = t_mk
        mkev = [mke, bfv("RG", 2048, 2048)]
        t_heprev = None
        t_msk = None
        t_ev2 = {}
        t_mm2done = None
        for ex in range(E):
            par = ex % 2
            w1b, w2b = wbufs[par]
            if ex + 1 < E:
                npar = 1 - par
                p.waits("gpsimd", [t_wload[npar], t_mm2done, t_hf])
                t_nw1 = p.do("gpsimd", lambda e, S, ex=ex, npar=npar: e.dma_start(
                    out=wbufs[npar][0], in_=dti["w1"].ap()[ex + 1].rearrange("(a p) o -> p a o", p=128)), "gw0" if npar == 0 else "gw1", 16)
                t_nw2 = p.do("gpsimd", lambda e, S, ex=ex, npar=npar: e.dma_start(
                    out=wbufs[npar][1], in_=dti["w2"].ap()[ex + 1].rearrange("(a p) o -> p a o", p=128)), "gw0" if npar == 0 else "gw1", 16)
                t_wload[npar] = t_nw2
                p.waits("gpsimd", [t_mload[npar], t_msk])
                t_nmk = p.do("gpsimd", lambda e, S, ex=ex, npar=npar: e.dma_start(
                    out=mkev[npar], in_=oh_dram.ap()[ex + 1:ex + 2, :].partition_broadcast(128).rearrange("p o e -> p (o e)")), "gm0" if npar == 0 else "gm1", 16)
                t_mload[npar] = t_nmk
            t_gelh = {0: None, 1: None}
            for tc in range(8):
                for ohalf in range(2):
                    p.waits("tensor", [t_wload[par], t_hf, t_gelh[ohalf]])
                    for ot4 in range(4):
                        ot = 4 * ohalf + ot4
                        for c8 in range(8):
                            p.do("tensor", lambda e, S, ot=ot, c8=c8, tc=tc, w1b=w1b: e.matmul(
                                ps[ot // 4].ap()[:, 256 * (ot % 4):256 * (ot % 4 + 1)],
                                w1b[:, c8, 128 * ot:128 * (ot + 1)],
                                hF[:, c8, 256 * tc:256 * (tc + 1)],
                                start=(c8 == 0), stop=(c8 == 7)))
                    t_heh = p.mark("tensor", "td")
                    p.wait("scalar", t_heh)
                    for ot4 in range(4):
                        ot = 4 * ohalf + ot4
                        p.do("scalar", lambda e, S, ot=ot, ex=ex: e.activation(
                            heb[:, ot, :], ps[ot // 4].ap()[:, 256 * (ot % 4):256 * (ot % 4 + 1)],
                            AF.Gelu, bias=sb["b1c"].ap()[:, 8 * ex + ot:8 * ex + ot + 1]))
                    t_gelh[ohalf] = p.mark("scalar", "sd")
                t_heprev = t_gelh[1]
                p.waits("vector", [t_gelh[1], t_mload[par]])
                p.do("vector", lambda e, S, tc=tc, par=par: e.tensor_tensor(
                    hemb, heb,
                    mkev[par][:, 256 * tc:256 * (tc + 1)].unsqueeze(1).broadcast_to([128, 8, 256]),
                    ALU.mult))
                t_msk = p.mark("vector", "vd")
                for ts in range(2):
                    p.waits("tensor", [t_msk, t_wload[par], t_ev2.get((tc, ts))])
                    for ot in range(8):
                        for oh in range(2):
                            p.do("tensor", lambda e, S, ts=ts, oh=oh, ot=ot, w2b=w2b, ex=ex: e.matmul(
                                ps[2 + ts].ap()[:, 512 * oh:512 * (oh + 1)],
                                hemb[:, ot, 128 * ts:128 * (ts + 1)],
                                w2b[:, ot, 512 * oh:512 * (oh + 1)],
                                start=(ot == 0), stop=(ot == 7 and ex != E - 1)))
                    if ex == E - 1:
                        p.waits("tensor", [t_ohf])
                        for oh in range(2):
                            p.do("tensor", lambda e, S, ts=ts, oh=oh, tc=tc: e.matmul(
                                ps[2 + ts].ap()[:, 512 * oh:512 * (oh + 1)],
                                sb["OHF"].ap()[0:8, 256 * tc + 128 * ts:256 * tc + 128 * (ts + 1)],
                                sb["b2r"].ap()[0:8, 512 * oh:512 * (oh + 1)],
                                start=False, stop=True))
                    t_mm2 = p.mark("tensor", "td")
                    t_mm2done = t_mm2
                    p.waits("vector", [t_mm2, t_taph])
                    p.do("vector", lambda e, S, ts=ts, tc=tc: e.tensor_tensor(
                        hT[:, 2 * tc + ts, :], ps[2 + ts].ap()[:, 0:1024], hT[:, 2 * tc + ts, :], ALU.add))
                    t_ev2[(tc, ts)] = p.mark("vector", "vd")
        t_moe_ev = t_ev2[(7, 1)]

        tpz = tapout("z2", (128, 16384), BF16)
        t_tapz = None
        if tpz is not None:
            p.wait("sync", t_moe_ev)
            t_tapz = p.do("sync", lambda e, S, tpz=tpz: e.dma_start(out=tpz.ap(), in_=sb["RB"][:, :]), "syt", 16)
        p.waits("gpsimd", [t_moe_ev])
        t_g2 = p.do("gpsimd", lambda e, S: e.dma_start(
            out=sb["gbc"][:, :], in_=dti["g2"].ap().partition_broadcast(128).rearrange("p o e -> p (o e)")), "gg", 16)
        t_h2 = p.do("gpsimd", lambda e, S: e.dma_start(
            out=sb["hbc"][:, :], in_=dti["h2"].ap().partition_broadcast(128).rearrange("p o e -> p (o e)")), "gg", 16)
        t_ln2 = [None] * 16
        for i in range(16):
            tln = ln_tile(hT[:, i, :], zbf, [t_moe_ev, t_g2, t_h2, t_tapz], par=i % 2,
                          gain=sb["gbc"][:, :], bias=sb["hbc"][:, :], out2=hT[:, i, :])
            t_ln2[i] = tln
        pid_holder = {}
        def get_pid_off(e, S):
            if "off" not in pid_holder:
                pid = nc.partition_id(engines=[mybir.EngineType.Pool])
                pid_holder["off"] = e.snap(pid * PSL, donate=False, min_val=0, max_val=(B - 1) * PSL)
            return pid_holder["off"]
        t_st = [None, None]
        t_sg = [None, None]
        t_out = [None, None]
        t_sil = [None, None]
        gtile = [bfv("RF", 0, 1024), bfv("RF", 1024, 1024)]
        prodf = [sb["RF"].ap().bitcast(F32)[:, 1024:2048], sb["RF"].ap().bitcast(F32)[:, 2048:3072]]
        xcf = [sb["RF"].ap().bitcast(F32)[:, 3072:4096], sb["KFQ"].ap().bitcast(F32)[:, 0:1024]]
        stile = [bfv("RG", 0, 1024), bfv("RG", 4096 - 1024, 1024)]
        for i in range(16):
            pr = i % 2
            p.waits("gpsimd", [None if no_cc else t_cc, t_sg[pr], t_moe_ev, t_out[pr]])
            def sgather(e, S, i=i, pr=pr):
                off = get_pid_off(e, S)
                src = cc_out.ap().rearrange("(b t) c -> t b c", t=T)[16 * i:, :, :]
                return e.dma_start(out=stile[pr], in_=src[bass.ds(off, 16), :, :])
            t_sg[pr] = p.do("gpsimd", sgather, "gq0" if pr == 0 else "gq1", 16)
            p.waits("sync", [t_st[pr], t_out[pr], t_h])
            t_x = p.do("sync", lambda e, S, i=i, pr=pr: e.dma_start(
                out=xcf[pr], in_=dti["xs"].ap()[128 * i:128 * (i + 1), :]),
                "sy0" if pr == 0 else "sy1", 16)
            p.waits("scalar", [t_ln2[i], t_out[pr]])
            p.do("scalar", lambda e, S, i=i, pr=pr: e.activation(gtile[pr], hT[:, i, :], AF.Silu))
            t_sil[pr] = p.mark("scalar", "sd")
            p.waits("vector", [t_sil[pr], t_sg[pr], t_x, t_st[pr]])
            p.do("vector", lambda e, S, pr=pr: e.tensor_tensor(prodf[pr], stile[pr], gtile[pr], ALU.mult))
            p.bar("vector", "vd")
            p.do("vector", lambda e, S, pr=pr: e.tensor_tensor(prodf[pr], prodf[pr], xcf[pr], ALU.add))
            t_out[pr] = p.mark("vector", "vd")
            p.wait("sync", t_out[pr])
            t_st[pr] = p.do("sync", lambda e, S, i=i, pr=pr: e.dma_start(
                out=dti["out"].ap()[128 * i:128 * (i + 1), :], in_=prodf[pr]), "syg", 16)
        p.waits("sync", [t_st[0], t_st[1]])
        p.waits("gpsimd", [t_st[0], t_st[1]])

    def ffn_stub(t_cc):
        p.wait("sync", t_cc)
        t_pr = None
        for i in range(16):
            t_l = p.do("sync", lambda e, S, i=i: e.dma_start(
                out=bfv("RF", 0, 1024),
                in_=(cc_in if no_cc else cc_out).ap()[128 * i:128 * (i + 1), :]), "syg", 16)
            p.waits("vector", [t_l, t_pr])
            p.do("vector", lambda e, S: e.tensor_copy(KNS, bfv("RF", 0, 1024)))
            t_c = p.mark("vector", "vd")
            p.wait("sync", t_c)
            t_pr = p.do("sync", lambda e, S, i=i: e.dma_start(
                out=dti["out"].ap()[128 * i:128 * (i + 1), :], in_=KNS), "syg", 16)
        p.wait("sync", t_pr)

    t_abc = setup()
    t_ret = titans(t_abc)
    t_cc, t_sev0 = t_ret if isinstance(t_ret, tuple) else (t_ret, t_ret)
    if stop_after is not None:
        p.wait("sync", t_cc)
        tz = p.do("sync", lambda e, S: e.dma_start(
            out=dti["out"].ap()[0:128, :], in_=KNS), "syg", 16)
        p.wait("sync", tz)
    elif not ffn:
        ffn_stub(t_cc)
    else:
        ffn(t_cc, t_sev0)
    emit(nc, p)
    return nc, dti, tap_set



def make_inmaps(inputs):
    x = np.asarray(inputs["L0_attn_x"])
    bin_ = np.asarray(inputs["L0_ffn_in_proj_b"])
    shared = dict(
        wq=np.asarray(inputs["L0_attn_Wq"]).T,
        wk=np.asarray(inputs["L0_attn_Wk"]).T,
        wv=np.asarray(inputs["L0_attn_Wv"]).T,
        wproj=np.asarray(inputs["L0_attn_Wproj"]).T,
        win=np.asarray(inputs["L0_ffn_in_proj_w"]).T,
        wout=np.asarray(inputs["L0_ffn_out_w"]).T,
        wr=np.asarray(inputs["L0_ffn_router_w"]).T,
        w1=np.asarray(inputs["L0_ffn_w1"]).transpose(0, 2, 1),
        w2=np.asarray(inputs["L0_ffn_w2"]).transpose(0, 2, 1),
        binrow=bin_.reshape(1, 3 * CD),
        ones1=np.ones((1, 512), np.float32),
        bout=np.asarray(inputs["L0_ffn_out_b"]).reshape(1, CD),
        brout=np.asarray(inputs["L0_ffn_router_b"]).reshape(1, E),
        b1c=np.asarray(inputs["L0_ffn_b1"]).reshape(E * 8, 128).T,
        b2r=np.asarray(inputs["L0_ffn_b2"]),
        g1=np.asarray(inputs["L0_ffn_ln1_g"]).reshape(1, CD),
        h1=np.asarray(inputs["L0_ffn_ln1_b"]).reshape(1, CD),
        g2=np.asarray(inputs["L0_ffn_ln2_g"]).reshape(1, CD),
        h2=np.asarray(inputs["L0_ffn_ln2_b"]).reshape(1, CD),
        araw=np.asarray(inputs["L0_attn_alpha_raw"]).reshape(1, H),
        braw=np.asarray(inputs["L0_attn_beta_raw"]).reshape(1, H),
        ident=np.eye(128, dtype=np.float32),
        stril=np.tril(np.ones((128, 128), np.float32), -1),
        striu=np.triu(np.ones((128, 128), np.float32), 1),
        blk8=np.kron(np.eye(16, dtype=np.float32), np.ones((8, 8), np.float32)),
        iota8=np.arange(E, dtype=np.float32).reshape(1, E),
    )
    shared = {k: np.ascontiguousarray(v, dtype=np.float32) for k, v in shared.items()}
    in_maps = []
    for i in range(8):
        m = dict(shared)
        m["xb"] = np.ascontiguousarray(x[i], dtype=np.float32)
        m["xs"] = np.ascontiguousarray(
            x[:, PSL * i:PSL * (i + 1), :].transpose(1, 0, 2).reshape(TOK, CD), dtype=np.float32)
        in_maps.append(m)
    return in_maps


def assemble_out(results):
    out = np.empty((B, T, CD), np.float32)
    for i in range(8):
        o = results[i]["out"].reshape(PSL, B, CD).transpose(1, 0, 2)
        out[:, PSL * i:PSL * (i + 1), :] = o
    return out


_CACHE = {}


def kernel(**inputs):
    if "nc" not in _CACHE:
        nc, dti, tapset = build_nc()
        _CACHE["nc"] = nc
    nc = _CACHE["nc"]
    in_maps = make_inmaps(inputs)
    res = run_bass_kernel_spmd(nc, in_maps, core_ids=list(range(8)))
    return assemble_out(res.results)
